# version 01
# baseline (speedup 1.0000x reference)
import contextlib
import numpy as np
import concourse.bass as bass
import concourse.mybir as mybir
from concourse.bass_utils import run_bass_kernel_spmd

F32 = mybir.dt.float32
BF16 = mybir.dt.bfloat16
I32 = mybir.dt.int32
AF = mybir.ActivationFunctionType
ALU = mybir.AluOpType
AX = mybir.AxisListType

NCORES = 8
DEBUG_L0 = False
DEBUG_L2A = False
L0_STOP = 99
BB, SS, DD = 2, 8192, 1024
TC = 2048
EPS_LN = 1e-5


def IM(x):
    return np.float32(x)

ENGINES = ("tensor", "vector", "scalar", "gpsimd", "sync")


class Ph:

    def __init__(self, L):
        self.L = L
        self.ops = {e: [] for e in ENGINES}

    def t(self, fn):
        self.ops["tensor"].append(fn)

    def v(self, fn):
        self.ops["vector"].append(fn)

    def s(self, fn):
        self.ops["scalar"].append(fn)

    def g(self, fn):
        self.ops["gpsimd"].append(fn)

    def y(self, fn):
        self.ops["sync"].append(fn)

    def eng(self, name):
        return {"tensor": self.t, "vector": self.v, "scalar": self.s,
                "gpsimd": self.g, "sync": self.y,
                "t": self.t, "v": self.v, "s": self.s, "g": self.g,
                "y": self.y}[name]

    def run(self):
        nc = self.L.nc
        with nc.Block() as blk:
            for eng in ENGINES:
                fns = self.ops[eng]
                if not fns:
                    continue

                def body(q, fns=fns):
                    for f in fns:
                        f(q)

                getattr(blk, eng)(body)


class Launch:

    def __init__(self, name):
        self.name = name
        self.nc = bass.Bass("TRN2", target_bir_lowering=False)
        self.stack = contextlib.ExitStack()
        self._sems = {}
        self._cnt = {}
        self._uid = 0

    def uid(self):
        self._uid += 1
        return self._uid

    def sem(self, name):
        if name not in self._sems:
            self._sems[name] = self.stack.enter_context(self.nc.semaphore(name))
            self._cnt[name] = 0
        return self._sems[name]

    def bump(self, name, n=1):
        self.sem(name)
        self._cnt[name] += n
        return self._cnt[name]

    def cnt(self, name):
        self.sem(name)
        return self._cnt[name]

    def sb(self, name, shape, dt, stack=None):
        return (stack or self.stack).enter_context(
            self.nc.sbuf_tensor(f"{name}_{self.uid()}", list(shape), dt))

    def ps(self, name, shape, dt=F32, stack=None):
        return (stack or self.stack).enter_context(
            self.nc.psum_tensor(f"{name}_{self.uid()}", list(shape), dt))

    def dram_in(self, name, shape, dt=F32):
        return self.nc.dram_tensor(name, list(shape), dt, kind="ExternalInput").ap()

    def dram_out(self, name, shape, dt=F32):
        return self.nc.dram_tensor(name, list(shape), dt, kind="ExternalOutput").ap()

    def dram_tmp(self, name, shape, dt=F32):
        return self.nc.dram_tensor(name, list(shape), dt, kind="Internal").ap()

    def emit_consts(self):
        self.ones_col_f = self.sb("ones_col_f", (128, 1), F32)
        self.ones_col_b = self.sb("ones_col_b", (128, 1), BF16)
        self.ones_row_f = self.sb("ones_row_f", (1, 512), F32)
        self.ones_row_b = self.sb("ones_row_b", (1, 512), BF16)
        self.ones_r128_f = self.sb("ones_r128_f", (1, 128), F32)
        self.ones_r128_b = self.sb("ones_r128_b", (1, 128), BF16)
        self.ident_b = self.sb("ident_b", (128, 128), BF16)
        self.ident_f = self.sb("ident_f", (128, 128), F32)
        ident = self.dram_in("IDENT", (128, 128), F32)
        ph = Ph(self)
        sem = self.sem("Ld")
        self.bump("Ld", 32)

        def g(q):
            q.memset(self.ones_col_f[:, :], 1.0)
            q.memset(self.ones_col_b[:, :], 1.0)
            q.memset(self.ones_row_f[:, :], 1.0)
            q.memset(self.ones_row_b[:, :], 1.0)
            q.memset(self.ones_r128_f[:, :], 1.0)
            q.memset(self.ones_r128_b[:, :], 1.0)
            q.dma_start(out=self.ident_b[:, :], in_=ident).then_inc(sem, 16)
            q.dma_start(out=self.ident_f[:, :], in_=ident).then_inc(sem, 16)
        ph.g(g)
        tgt = self.cnt("Ld")

        def w(q):
            q.wait_ge(sem, tgt)
        ph.v(w)
        ph.run()

    def vsync(self, q):
        t = self.bump("Vx")
        q.engine_nop().then_inc(self.sem("Vx"), 1)
        q.wait_ge(self.sem("Vx"), t)

    def load(self, ph, dst_ap, src_ap, queue="gpsimd"):
        tgt = self.bump("Ld", 16)
        sem = self.sem("Ld")

        def f(q):
            q.dma_start(out=dst_ap, in_=src_ap).then_inc(sem, 16)
        getattr(ph, {"gpsimd": "g", "sync": "y", "scalar": "s"}[queue])(f)
        return tgt

    def store(self, ph, dst_ap, src_ap, queue="sync"):
        tgt = self.bump("St", 16)
        sem = self.sem("St")

        def f(q):
            q.dma_start(out=dst_ap, in_=src_ap).then_inc(sem, 16)
        getattr(ph, {"gpsimd": "g", "sync": "y", "scalar": "s"}[queue])(f)
        return tgt

    def wait(self, ph, eng, sname, tgt):
        sem = self.sem(sname)

        def f(q):
            q.wait_ge(sem, tgt)
        ph.eng(eng)(f)

    def barrier_wait_all_dma(self):
        ph = Ph(self)
        ld, st = self.cnt("Ld"), self.cnt("St")
        lsem, ssem = self.sem("Ld"), self.sem("St")

        def f(q):
            q.wait_ge(lsem, ld)
            q.wait_ge(ssem, st)
        ph.y(f)
        ph.run()

    def finish(self):
        self.barrier_wait_all_dma()
        self.stack.close()


    def ln_window(self, x_tiles, Tn, out_bf, sq_scratch, out_f32=None,
                  rms_mode=False, gamma_cols=None, beta_cols=None, nf=8,
                  in_bf16=False):
        D = nf * 128
        chunks = []
        c0 = 0
        while c0 < Tn:
            chunks.append((c0, min(512, Tn - c0)))
            c0 += 512
        NCH = len(chunks)
        loc = contextlib.ExitStack()
        s1r = self.sb("ln_s1", (1, Tn), F32, loc) if not rms_mode else None
        s2r = self.sb("ln_s2", (1, Tn), F32, loc)
        rstd = self.sb("ln_rstd", (1, Tn), F32, loc)
        mr = self.sb("ln_mr", (1, Tn), F32, loc) if not rms_mode else None
        tmp = self.sb("ln_tmp", (128, 512), F32, loc)
        ps2 = self.ps("ln_ps2", (1, 512), stack=loc)
        ps1 = self.ps("ln_ps1", (1, 512), stack=loc) if not rms_mode else None
        psA = self.ps("ln_psA", (128, 512), stack=loc)
        psB = self.ps("ln_psB", (128, 512), stack=loc) if not rms_mode else None

        for c, (cc0, cw) in enumerate(chunks):
            ph = Ph(self)

            def fs(q, cc0=cc0, cw=cw):
                last = None
                for f in range(nf):
                    last = q.activation(sq_scratch[:, f * 512:f * 512 + cw],
                                        x_tiles[f][:, cc0:cc0 + cw],
                                        AF.Square)
                last.then_inc(self.sem("Ac"), 1)
            ac = self.bump("Ac")
            ph.s(fs)

            def ft(q, cc0=cc0, cw=cw, ac=ac):
                q.wait_ge(self.sem("Ac"), ac)
                last = None
                onescol = self.ones_col_b if in_bf16 else self.ones_col_f
                for f in range(nf):
                    if not rms_mode:
                        q.matmul(ps1[:, 0:cw], onescol[:, :],
                                 x_tiles[f][:, cc0:cc0 + cw],
                                 start=(f == 0), stop=(f == nf - 1))
                    last = q.matmul(ps2[:, 0:cw], self.ones_col_f[:, :],
                                    sq_scratch[:, f * 512:f * 512 + cw],
                                    start=(f == 0), stop=(f == nf - 1))
                last.then_inc(self.sem("Pe"), 1)
            pe = self.bump("Pe")
            ph.t(ft)

            def fv(q, cc0=cc0, cw=cw, pe=pe):
                q.wait_ge(self.sem("Pe"), pe)
                if not rms_mode:
                    q.tensor_copy(s1r[:, cc0:cc0 + cw], ps1[:, 0:cw])
                q.tensor_copy(s2r[:, cc0:cc0 + cw], ps2[:, 0:cw])
            ph.v(fv)
            ph.run()

        ph = Ph(self)
        if not rms_mode:
            def fs(q):
                q.activation(mr[:, :], s1r[:, :], AF.Copy, scale=IM(1.0 / D))
                q.activation(s1r[:, :], s1r[:, :], AF.Square, scale=IM(1.0 / D))
                q.activation(s2r[:, :], s2r[:, :], AF.Copy,
                             scale=IM(1.0 / D)).then_inc(self.sem("Ac"), 1)
            ac = self.bump("Ac")
            ph.s(fs)

            def fv(q, ac=ac):
                q.wait_ge(self.sem("Ac"), ac)
                q.tensor_tensor(s2r[:, :], s2r[:, :], s1r[:, :],
                                ALU.subtract).then_inc(self.sem("Ve"), 1)
            ve = self.bump("Ve")
            ph.v(fv)

            def fs2(q, ve=ve):
                q.wait_ge(self.sem("Ve"), ve)
                q.activation(s2r[:, :], s2r[:, :], AF.Ln, bias=IM(EPS_LN))
                q.activation(rstd[:, :], s2r[:, :], AF.Exp,
                             scale=IM(-0.5)).then_inc(self.sem("Ac"), 1)
            ac2 = self.bump("Ac")
            ph.s(fs2)

            def fv2(q, ac2=ac2):
                q.wait_ge(self.sem("Ac"), ac2)
                q.tensor_tensor(mr[:, :], mr[:, :], rstd[:, :], ALU.mult)
            ph.v(fv2)
        else:
            def fs(q):
                q.activation(s2r[:, :], s2r[:, :], AF.Ln, bias=IM(EPS_LN), scale=IM(1.0 / D))
                q.activation(rstd[:, :], s2r[:, :], AF.Exp, scale=IM(-0.5))
            ph.s(fs)
        ph.run()

        for c, (cc0, cw) in enumerate(chunks):
            ph = Ph(self)

            def ft(q, cc0=cc0, cw=cw):
                mm = q.matmul(psA[:, 0:cw], self.ones_r128_f[:, :],
                              rstd[0:1, cc0:cc0 + cw], start=True, stop=True)
                if not rms_mode:
                    mm = q.matmul(psB[:, 0:cw], self.ones_r128_f[:, :],
                                  mr[0:1, cc0:cc0 + cw], start=True,
                                  stop=True)
                mm.then_inc(self.sem("Pe"), 1)
            pe = self.bump("Pe")
            ph.t(ft)

            def fv(q, cc0=cc0, cw=cw, pe=pe):
                q.wait_ge(self.sem("Pe"), pe)
                sl = slice(cc0, cc0 + cw)
                last = None
                for f in range(nf):
                    if rms_mode:
                        last = q.tensor_tensor(out_bf[f][:, sl], x_tiles[f][:, sl],
                                               psA[:, 0:cw], ALU.mult)
                        if out_f32 is not None:
                            last = q.tensor_tensor(out_f32[f][:, sl],
                                                   x_tiles[f][:, sl],
                                                   psA[:, 0:cw], ALU.mult)
                    else:
                        q.tensor_tensor(tmp[:, 0:cw], x_tiles[f][:, sl],
                                        psA[:, 0:cw], ALU.mult)
                        last = q.tensor_tensor(out_bf[f][:, sl], tmp[:, 0:cw],
                                               psB[:, 0:cw], ALU.subtract)
                        if out_f32 is not None:
                            last = q.tensor_tensor(out_f32[f][:, sl], tmp[:, 0:cw],
                                                   psB[:, 0:cw], ALU.subtract)
                last.then_inc(self.sem("Ve"), 1)
            venorm = self.bump("Ve")
            ph.v(fv)
            if gamma_cols is not None:
                def fs3(q, cc0=cc0, cw=cw, venorm=venorm):
                    q.wait_ge(self.sem("Ve"), venorm)
                    sl = slice(cc0, cc0 + cw)
                    for f in range(nf):
                        q.activation(out_bf[f][:, sl], out_bf[f][:, sl], AF.Identity,
                                     scale=gamma_cols[f],
                                     bias=(beta_cols[f] if beta_cols is not None
                                           else 0.0))
                        if out_f32 is not None:
                            q.activation(out_f32[f][:, sl], out_f32[f][:, sl],
                                         AF.Identity, scale=gamma_cols[f],
                                         bias=(beta_cols[f] if beta_cols is not None
                                               else 0.0))
                ph.s(fs3)
            ph.run()
        loc.close()

    def proj(self, *, rhs_tiles, Tn, K, M, w_dram, bias_dram=None, evac,
             mblock=512, dequant_scales=None, rhs_off=0, nps=4):
        Ktiles = K // 128
        chunks = []
        _c0 = 0
        while _c0 < Tn:
            chunks.append((_c0, min(512, Tn - _c0)))
            _c0 += 512
        NCH = len(chunks)
        assert K % 128 == 0 and M % 128 == 0
        loc = contextlib.ExitStack()
        psums = [self.ps("pj_ps", (128, 512), stack=loc) for _ in range(nps)]
        wb = [self.sb("pj_w", (128, Ktiles * mblock), BF16, loc) for _ in range(2)]
        dq = dequant_scales is not None
        if dq:
            assert Ktiles * mblock <= 4096
            wraw = [self.sb("pj_wr", (128, Ktiles * mblock), BF16, loc)
                    for _ in range(2)]
            ps_scl = [self.ps("pj_pscl", (128, 512), stack=loc) for _ in range(2)]
            scl_row = [self.sb("pj_scl", (1, Ktiles * mblock), BF16, loc)
                       for _ in range(2)]
        bias_row = None
        nmb = (M + mblock - 1) // mblock
        sl_sem = ["LdA", "LdB"]

        def wsrc(mb):
            m0 = mb * mblock
            mw = min(mblock, M - m0)
            return w_dram[:, m0:m0 + mw].rearrange("(s p) m -> p s m", p=128)

        def ssrc(mb):
            m0 = mb * mblock
            mw = min(mblock, M - m0)
            return dequant_scales[:, m0:m0 + mw].rearrange("(o k) m -> o k m", o=1)

        def issue_block(ph, mb):
            par = mb % 2
            mw = min(mblock, M - mb * mblock)
            sem = sl_sem[par]
            tgt = self.bump(sem, 16)
            s = self.sem(sem)
            dst = (wraw if dq else wb)[par][:, 0:Ktiles * mw].rearrange(
                "p (s m) -> p s m", s=Ktiles)
            srcw = wsrc(mb)

            def fg(q, dst=dst, srcw=srcw, s=s):
                q.dma_start(out=dst, in_=srcw).then_inc(s, 16)
            ph.g(fg)
            if dq:
                tgt2 = self.bump(sem, 16)
                srcs = ssrc(mb)
                dsts = scl_row[par][0:1, 0:Ktiles * mw].rearrange(
                    "o (k m) -> o k m", k=Ktiles)

                def fg2(q, dsts=dsts, srcs=srcs, s=s):
                    q.dma_start(out=dsts, in_=srcs).then_inc(s, 16)
                ph.g(fg2)
                return tgt2
            return tgt

        ph = Ph(self)
        if bias_dram is not None:
            bias_row = self.sb("pj_bias", (1, M), BF16, loc)
            bsrc = (bias_dram if len(bias_dram.shape) == 2
                    else bias_dram.rearrange("(o m) -> o m", o=1))
            self.load(ph, bias_row[0:1, 0:M], bsrc)
        ld_general = self.cnt("Ld")
        ld_tgts = {0: issue_block(ph, 0)}
        ph.run()

        groups_pe = {}
        dq_ve = {}
        ev_of_bank = {}
        for mb in range(nmb):
            m0 = mb * mblock
            mw = min(mblock, M - m0)
            msub = mw // 128
            par = mb % 2
            ph = Ph(self)
            if mb + 1 < nmb:
                def fg(q, mb=mb):
                    if mb >= 1:
                        q.wait_ge(self.sem("Pe"), groups_pe[mb - 1])
                        if dq:
                            q.wait_ge(self.sem("Ve"), dq_ve[mb - 1])
                ph.g(fg)
                ld_tgts[mb + 1] = issue_block(ph, mb + 1)
            if dq:
                scl_ve = {}
                for k in range(Ktiles):
                    def ftq(q, k=k, mw=mw, par=par, mb=mb):
                        if k == 0:
                            q.wait_ge(self.sem(sl_sem[par]), ld_tgts[mb])
                        if k >= 2:
                            q.wait_ge(self.sem("Ve"), scl_ve[k - 2])
                        q.matmul(ps_scl[k % 2][:, 0:mw], self.ones_r128_b[:, :],
                                 scl_row[par][0:1, k * mw:(k + 1) * mw],
                                 start=True, stop=True).then_inc(self.sem("Pe"), 1)
                    pet = self.bump("Pe")
                    ph.t(ftq)

                    def fvq(q, k=k, mw=mw, par=par, pet=pet):
                        q.wait_ge(self.sem("Pe"), pet)
                        q.scalar_tensor_tensor(
                            wb[par][:, k * mw:(k + 1) * mw],
                            wraw[par][:, k * mw:(k + 1) * mw],
                            -128.0, ps_scl[k % 2][:, 0:mw],
                            ALU.add, ALU.mult).then_inc(self.sem("Ve"), 1)
                    scl_ve[k] = self.bump("Ve")
                    ph.v(fvq)
                dq_ve[mb] = self.cnt("Ve")
            for c, (cc0, cw) in enumerate(chunks):
                for ms in range(msub):
                    bank = (mb * NCH * msub + c * msub + ms) % nps

                    def ft(q, cc0=cc0, cw=cw, ms=ms, bank=bank, mb=mb, mw=mw,
                           par=par, first=(c == 0 and ms == 0),
                           evprev=ev_of_bank.get(bank)):
                        if first:
                            if dq:
                                q.wait_ge(self.sem("Ve"), dq_ve[mb])
                            else:
                                q.wait_ge(self.sem(sl_sem[par]), ld_tgts[mb])
                            if mb == 0:
                                q.wait_ge(self.sem("Ld"), ld_general)
                        if evprev:
                            q.wait_ge(self.sem("Ev"), evprev)
                        nk = Ktiles
                        for k in range(nk):
                            mm = q.matmul(
                                psums[bank][:, 0:cw],
                                wb[par][:, k * mw + ms * 128:k * mw + (ms + 1) * 128],
                                rhs_tiles[k][:, rhs_off + cc0:rhs_off + cc0 + cw],
                                start=(k == 0),
                                stop=(k == nk - 1) and bias_row is None)
                        if bias_row is not None:
                            mm = q.matmul(psums[bank][:, 0:cw],
                                          bias_row[0:1, m0 + ms * 128:m0 + (ms + 1) * 128],
                                          self.ones_row_b[0:1, 0:cw],
                                          start=False, stop=True)
                        mm.then_inc(self.sem("Pe"), 1)
                    pe = self.bump("Pe")
                    ph.t(ft)
                    mi = (m0 + ms * 128) // 128
                    evtgt = evac(mi, c, psums[bank], ph, pe)
                    ev_of_bank[bank] = evtgt
            groups_pe[mb] = self.cnt("Pe")
            ph.run()
        loc.close()

    def ev_act(self, ph, pe, ps_ap, out_ap, func, scale=1.0, bias=0.0, w=512):
        ev = self.bump("Ev")

        def f(q):
            q.wait_ge(self.sem("Pe"), pe)
            if func == AF.Copy:
                sc, bi = float(scale), float(bias)
            else:
                sc = IM(scale) if isinstance(scale, float) else scale
                bi = IM(bias) if isinstance(bias, float) else bias
            q.activation(out_ap, ps_ap[:, 0:w], func, scale=sc,
                         bias=bi).then_inc(self.sem("Ev"), 1)
        ph.s(f)
        return ev

    def ev_addx(self, ph, pe, ps_ap, x_ap, coef, w=512):
        ev = self.bump("Ev")

        def f(q):
            q.wait_ge(self.sem("Pe"), pe)
            q.scalar_tensor_tensor(x_ap, ps_ap[:, 0:w], float(coef), x_ap,
                                   ALU.mult, ALU.add).then_inc(self.sem("Ev"), 1)
        ph.v(f)
        return ev



def build_l3ffn():
    L = Launch("l3ffn")
    x_in = L.dram_in("x_in", (DD, TC))
    x_out = L.dram_out("x_out", (DD, TC))
    fc1_q = L.dram_in("fc1_q", (1024, 4096), I32)
    fc1_s = L.dram_in("fc1_s", (8, 4096))
    fc1_b = L.dram_in("fc1_b", (4096,))
    fc2_q = L.dram_in("fc2_q", (4096, 1024), I32)
    fc2_s = L.dram_in("fc2_s", (32, 1024))
    fc2_b = L.dram_in("fc2_b", (1024,))
    gb = L.dram_in("gb", (128, 16))

    L.emit_consts()
    gb_sb = L.sb("gb", (128, 16), F32)
    ph = Ph(L)
    t = L.load(ph, gb_sb[:, :], gb)
    L.wait(ph, "s", "Ld", t)
    ph.run()

    CH = 512
    x_sb = [L.sb("x", (128, CH), F32) for _ in range(8)]
    x0 = [L.sb("x0", (128, CH), BF16) for _ in range(8)]
    xh = [L.sb("xh", (128, CH), BF16) for _ in range(8)]
    y_sb = [L.sb("y", (128, CH), BF16) for _ in range(32)]
    sq = L.sb("sq", (128, 8 * 512), F32)

    for ch in range(TC // CH):
        c0 = ch * CH
        ph = Ph(L)
        L.wait(ph, "y", "St", L.cnt("St"))
        tgts = [L.load(ph, x_sb[f][:, :], x_in[f * 128:(f + 1) * 128, c0:c0 + CH],
                       queue="sync") for f in range(8)]
        L.wait(ph, "v", "Ld", tgts[-1])
        ph.run()
        L.ln_window([x[:, :] for x in x_sb], CH, [x[:, :] for x in x0], sq[:, :])
        L.ln_window([x[:, :] for x in x0], CH, [x[:, :] for x in xh], sq[:, :],
                    in_bf16=True,
                    gamma_cols=[gb_sb[:, f:f + 1] for f in range(8)],
                    beta_cols=[gb_sb[:, 8 + f:8 + f + 1] for f in range(8)])
        ph = Ph(L)

        def fv(q):
            for f in range(8):
                q.scalar_tensor_tensor(x_sb[f][:, :], x0[f][:, :], 0.5,
                                       x_sb[f][:, :], ALU.mult, ALU.add)
        ph.v(fv)
        ph.run()
        def ev1(mi, c, ps_ap, ph, pe):
            return L.ev_act(ph, pe, ps_ap, y_sb[mi][:, c * 512:(c + 1) * 512],
                            AF.Silu)
        L.proj(rhs_tiles=[t[:, :] for t in xh], Tn=CH, K=1024, M=4096,
               w_dram=fc1_q, bias_dram=fc1_b, evac=ev1, mblock=512,
               dequant_scales=fc1_s)
        def ev2(mi, c, ps_ap, ph, pe):
            return L.ev_addx(ph, pe, ps_ap, x_sb[mi][:, c * 512:(c + 1) * 512],
                             0.25)
        L.proj(rhs_tiles=[t[:, :] for t in y_sb], Tn=CH, K=4096, M=1024,
               w_dram=fc2_q, bias_dram=fc2_b, evac=ev2, mblock=128,
               dequant_scales=fc2_s)
        ph = Ph(L)
        for f in range(8):
            L.store(ph, x_out[f * 128:(f + 1) * 128, c0:c0 + CH], x_sb[f][:, :])
        ph.run()
    L.finish()
    return L



def build_l2ffn():
    L = Launch("l2ffn")
    x_in = L.dram_in("x_in", (DD, TC))
    x_out = L.dram_out("x_out", (DD, TC))
    w1 = L.dram_in("w1", (1024, 4096))
    w2 = L.dram_in("w2", (1024, 4096))
    h1 = L.dram_in("h1", (4096, 4096))
    h2 = L.dram_in("h2", (4096, 4096))
    w3 = L.dram_in("w3", (4096, 1024))
    scols = L.dram_in("scols", (128, 32 * 4 + 8))

    L.emit_consts()
    sc_sb = L.sb("scols", (128, 136), F32)
    ph = Ph(L)
    t = L.load(ph, sc_sb[:, :], scols)
    L.wait(ph, "s", "Ld", t)
    L.wait(ph, "v", "Ld", t)
    ph.run()

    def scol(group, mi):
        return sc_sb[:, group * 32 + mi:group * 32 + mi + 1]

    CH = 512
    x_sb = [L.sb("x", (128, CH), F32) for _ in range(8)]
    xh = [L.sb("xh", (128, CH), BF16) for _ in range(8)]
    hA = [L.sb("hA", (128, CH), BF16) for _ in range(32)]
    hB = [L.sb("hB", (128, CH), BF16) for _ in range(32)]
    sq = L.sb("sq", (128, 8 * 512), F32)

    for ch in range(TC // CH):
        c0 = ch * CH
        ph = Ph(L)
        L.wait(ph, "y", "St", L.cnt("St"))
        tgts = [L.load(ph, x_sb[f][:, :], x_in[f * 128:(f + 1) * 128, c0:c0 + CH],
                       queue="sync") for f in range(8)]
        L.wait(ph, "v", "Ld", tgts[-1])
        ph.run()
        L.ln_window([x[:, :] for x in x_sb], CH, [x[:, :] for x in xh], sq[:, :])

        def ev1(mi, c, ps_ap, ph, pe):
            return L.ev_act(ph, pe, ps_ap, hA[mi][:, c * 512:(c + 1) * 512],
                            AF.Silu, scale=scol(0, mi))
        L.proj(rhs_tiles=[t[:, :] for t in xh], Tn=CH, K=1024, M=4096,
               w_dram=w1, evac=ev1, mblock=512)

        def ev2(mi, c, ps_ap, ph, pe):
            ev = L.bump("Ev")

            def f(q, mi=mi, c=c, ps_ap=ps_ap, pe=pe):
                q.wait_ge(L.sem("Pe"), pe)
                q.scalar_tensor_tensor(
                    hB[mi][:, c * 512:(c + 1) * 512], ps_ap[:, 0:512],
                    scol(1, mi), hA[mi][:, c * 512:(c + 1) * 512],
                    ALU.mult, ALU.mult).then_inc(L.sem("Ev"), 1)
            ph.v(f)
            return ev
        L.proj(rhs_tiles=[t[:, :] for t in xh], Tn=CH, K=1024, M=4096,
               w_dram=w2, evac=ev2, mblock=512)

        def ev3(mi, c, ps_ap, ph, pe):
            return L.ev_act(ph, pe, ps_ap, hA[mi][:, c * 512:(c + 1) * 512],
                            AF.Gelu, scale=scol(2, mi))
        L.proj(rhs_tiles=[t[:, :] for t in hB], Tn=CH, K=4096, M=4096,
               w_dram=h1, evac=ev3, mblock=256)

        def ev4(mi, c, ps_ap, ph, pe):
            return L.ev_act(ph, pe, ps_ap, hB[mi][:, c * 512:(c + 1) * 512],
                            AF.Gelu, scale=scol(3, mi))
        L.proj(rhs_tiles=[t[:, :] for t in hA], Tn=CH, K=4096, M=4096,
               w_dram=h2, evac=ev4, mblock=256)

        def ev5(mi, c, ps_ap, ph, pe):
            ev = L.bump("Ev")

            def f(q, mi=mi, c=c, ps_ap=ps_ap, pe=pe):
                q.wait_ge(L.sem("Pe"), pe)
                sl = slice(c * 512, (c + 1) * 512)
                q.tensor_scalar(hA[mi][:, sl], ps_ap[:, 0:512],
                                sc_sb[:, 128 + mi:128 + mi + 1], None, ALU.mult)
                q.scalar_tensor_tensor(
                    x_sb[mi][:, sl], hA[mi][:, sl], 0.5, x_sb[mi][:, sl],
                    ALU.mult, ALU.add).then_inc(L.sem("Ev"), 1)
            ph.v(f)
            return ev
        L.proj(rhs_tiles=[t[:, :] for t in hB], Tn=CH, K=4096, M=1024,
               w_dram=w3, evac=ev5, mblock=512)

        ph = Ph(L)
        for f in range(8):
            L.store(ph, x_out[f * 128:(f + 1) * 128, c0:c0 + CH], x_sb[f][:, :])
        ph.run()
    L.finish()
    return L


def build_l0ffn():
    L = Launch("l0ffn")
    x_in = L.dram_in("x_in", (DD, TC))
    x_out = L.dram_out("x_out", (DD, TC))
    wi = L.dram_in("wi", (1024, 4096))
    bi = L.dram_in("bi", (4096,))
    wo = L.dram_in("wo", (4096, 1024))
    bo = L.dram_in("bo", (1024,))
    gw = L.dram_in("gw", (1024, 8))
    gb = L.dram_in("gbias", (1, 8))
    A_sb_d = L.dram_in("A_sb", (128, 1024))
    B_d = L.dram_in("B_all", (8 * 16, 4096))

    L.emit_consts()
    gw_sb = L.sb("gw", (128, 64), BF16)
    gb_sb = L.sb("gbias", (1, 8), F32)
    A_sb = L.sb("A", (128, 1024), BF16)
    oneh_sb = L.sb("oneh", (8, 8 * 128), BF16)
    oneh_d = L.dram_in("ONEHOT8", (8, 8 * 128))
    ph = Ph(L)
    L.load(ph, oneh_sb[:, :], oneh_d)
    L.load(ph, gw_sb[:, :].rearrange("p (k m) -> p k m", k=8),
           gw.rearrange("(k p) m -> p k m", p=128))
    L.load(ph, gb_sb[:, :], gb)
    t = L.load(ph, A_sb[:, :], A_sb_d)
    L.wait(ph, "v", "Ld", t)
    L.wait(ph, "s", "Ld", t)
    ph.run()

    CH = 512
    x_sb = [L.sb("x", (128, CH), F32) for _ in range(8)]
    xh = [L.sb("xh", (128, CH), BF16) for _ in range(8)]
    base = [L.sb("base", (128, CH), BF16) for _ in range(32)]
    hsum = [L.sb("hsum", (128, CH), BF16) for _ in range(32)]
    gbc = [L.sb("gbc", (128, CH), BF16) for _ in range(8)]
    a_e = [L.sb("a", (32, CH), BF16) for _ in range(8)]
    r_sb = [L.sb("r", (128, CH), BF16) for _ in range(2)]
    bsl = [L.sb("bsl", (16, 4096), BF16) for _ in range(2)]
    sq = L.sb("sq", (128, 8 * 512), F32)
    tkA = L.sb("tkA", (128, 8), F32)
    tkB128 = L.sb("tkB128", (128, 128), F32)
    tkC = L.sb("tkC", (128, 1), F32)
    m1c = L.sb("m1c", (128, 1), F32)
    m2c = L.sb("m2c", (128, 1), F32)
    g8T = L.sb("g8T", (8, CH), BF16)
    tkD = L.sb("tkD", (128, 8), F32)
    tkE = L.sb("tkE", (128, 8), F32)
    tkCn = L.sb("tkCn", (128, 1), F32)
    tkR = L.sb("tkR", (128, 1), F32)
    vs = L.sb("vs", (128, 1), F32)

    dbg = {}
    if DEBUG_L0:
        for nm, shp in [("d_xh", (128, 512)), ("d_base", (128, 512)),
                        ("d_hsum", (128, 512)), ("d_gT", (8, 512)),
                        ("d_gbc", (128, 512)), ("d_a0", (32, 512)),
                        ("d_tkA", (128, 8)), ("d_tkB", (128, 8)),
                        ("d_psr", (128, 8)), ("d_tkC", (128, 1)),
                        ("d_m1", (128, 1)), ("d_m2", (128, 1))]:
            dbg[nm] = L.dram_out(nm, shp)

    ph = Ph(L)

    def fz(q):
        for e in range(8):
            q.memset(a_e[e][:, :], 0.0)
        q.memset(tkB128[:, :], 0.0)
    ph.g(fz)
    ph.run()

    for ch in range(TC // CH):
        c0 = ch * CH
        ph = Ph(L)
        L.wait(ph, "y", "St", L.cnt("St"))
        tgts = [L.load(ph, x_sb[f][:, :], x_in[f * 128:(f + 1) * 128, c0:c0 + CH],
                       queue="sync") for f in range(8)]
        L.wait(ph, "v", "Ld", tgts[-1])
        ph.run()
        L.ln_window([x[:, :] for x in x_sb], CH, [x[:, :] for x in xh], sq[:, :])

        loc = contextlib.ExitStack()
        psr = L.ps("psr", (128, 8), stack=loc)
        pst = L.ps("pst", (128, 128), stack=loc)
        for tt in range(CH // 128):
            ph = Ph(L)

            def ft(q, tt=tt):
                for k in range(8):
                    q.matmul(psr[:, :], xh[k][:, tt * 128:(tt + 1) * 128],
                             gw_sb[:, k * 8:(k + 1) * 8],
                             start=(k == 0), stop=False)
                q.matmul(psr[:, :], L.ones_r128_f[:, :], gb_sb[:, :],
                         start=False, stop=True).then_inc(L.sem("Pe"), 1)
            pe = L.bump("Pe")
            ph.t(ft)

            def fv(q, tt=tt, pe=pe):
                q.wait_ge(L.sem("Pe"), pe)
                q.tensor_copy(tkD[:, :], psr[:, :])
                L.vsync(q)
                q.reduce_max(tkC[:, :], tkD[:, :], axis=AX.X)
                L.vsync(q)
                q.tensor_scalar(tkCn[:, :], tkC[:, :], -1.0, None,
                                ALU.mult).then_inc(L.sem("Ve"), 1)
            ve = L.bump("Ve")
            ph.v(fv)

            def fs(q, tt=tt, ve=ve):
                q.wait_ge(L.sem("Ve"), ve)
                q.activation(tkA[:, :], tkD[:, :], AF.Exp,
                             bias=tkCn[:, :]).then_inc(L.sem("Ac"), 1)
            ac = L.bump("Ac")
            ph.s(fs)

            def fv2(q, tt=tt, ac=ac):
                q.wait_ge(L.sem("Ac"), ac)
                q.reduce_sum(tkC[:, :], tkA[:, :], axis=AX.X)
                L.vsync(q)
                q.reciprocal(tkR[:, :], tkC[:, :])
                L.vsync(q)
                q.tensor_scalar(tkE[:, :], tkA[:, :], tkR[:, :], None, ALU.mult)
                L.vsync(q)
                q.reduce_max(m1c[:, :], tkE[:, :], axis=AX.X)
                L.vsync(q)
                tkB = tkB128[:, 0:8]
                q.tensor_scalar(tkA[:, :], tkE[:, :], m1c[:, :], None, ALU.is_ge)
                L.vsync(q)
                q.tensor_tensor(tkB, tkE[:, :], tkA[:, :], ALU.subtract)
                L.vsync(q)
                q.reduce_max(m2c[:, :], tkB, axis=AX.X)
                L.vsync(q)
                q.tensor_tensor(vs[:, :], m1c[:, :], m2c[:, :], ALU.add)
                q.tensor_scalar(tkA[:, :], tkE[:, :], m2c[:, :], None, ALU.is_ge)
                L.vsync(q)
                q.tensor_scalar(tkB, tkA[:, :], vs[:, :], None,
                                ALU.mult).then_inc(L.sem("Ve"), 1)
            ve2 = L.bump("Ve")
            ph.v(fv2)

            def ft2(q, tt=tt, ve2=ve2):
                q.wait_ge(L.sem("Ve"), ve2)
                if tt > 0:
                    q.wait_ge(L.sem("Ve"), L._gt_ve[tt - 1])
                q.transpose(pst[:, :], tkB128[:, :], L.ident_f[:, :]).then_inc(
                    L.sem("Pe"), 1)
            pe2 = L.bump("Pe")
            ph.t(ft2)

            def fv3(q, tt=tt, pe2=pe2):
                q.wait_ge(L.sem("Pe"), pe2)
                q.tensor_copy(g8T[:, tt * 128:(tt + 1) * 128],
                              pst[0:8, :]).then_inc(L.sem("Ve"), 1)
            L._gt_ve = getattr(L, "_gt_ve", {})
            L._gt_ve[tt] = L.bump("Ve")
            ph.v(fv3)
            ph.run()
        loc.close()

        loc = contextlib.ExitStack()
        psb = [L.ps("psb", (128, 512), stack=loc) for _ in range(2)]
        ph = Ph(L)
        for e in range(8):
            def ft(q, e=e):
                if e >= 2:
                    q.wait_ge(L.sem("Ac"), L._gbc_ac[e - 2])
                mm = None
                for tt in range(CH // 128):
                    mm = q.matmul(psb[e % 2][:, tt * 128:(tt + 1) * 128],
                                  oneh_sb[:, e * 128:(e + 1) * 128],
                                  g8T[:, tt * 128:(tt + 1) * 128],
                                  start=True, stop=True)
                mm.then_inc(L.sem("Pe"), 1)
            pe = L.bump("Pe")
            ph.t(ft)

            def fs(q, e=e, pe=pe):
                q.wait_ge(L.sem("Pe"), pe)
                q.activation(gbc[e][:, :], psb[e % 2][:, :], AF.Copy).then_inc(
                    L.sem("Ac"), 1)
            L._gbc_ac = getattr(L, "_gbc_ac", {})
            L._gbc_ac[e] = L.bump("Ac")
            ph.s(fs)
        ph.run()
        loc.close()

        loc = contextlib.ExitStack()
        psa = [L.ps("psa", (16, 512), stack=loc) for _ in range(2)]
        ph = Ph(L)
        for e in range(8):
            def ft(q, e=e):
                if e >= 2:
                    q.wait_ge(L.sem("Ac"), L._ae_ac[e - 2])
                mm = None
                for k in range(8):
                    mm = q.matmul(psa[e % 2][:, :],
                                  A_sb[:, k * 128 + e * 16:k * 128 + e * 16 + 16],
                                  xh[k][:, 0:512], start=(k == 0), stop=(k == 7))
                mm.then_inc(L.sem("Pe"), 1)
            pe = L.bump("Pe")
            ph.t(ft)

            def fs(q, e=e, pe=pe):
                q.wait_ge(L.sem("Pe"), pe)
                q.activation(a_e[e][0:16, :], psa[e % 2][:, :], AF.Copy).then_inc(
                    L.sem("Ac"), 1)
            L._ae_ac = getattr(L, "_ae_ac", {})
            L._ae_ac[e] = L.bump("Ac")
            ph.s(fs)
        ph.run()
        loc.close()

        def evb(mi, c, ps_ap, ph, pe):
            return L.ev_act(ph, pe, ps_ap, base[mi][:, c * 512:(c + 1) * 512],
                            AF.Copy)
        L.proj(rhs_tiles=[t[:, :] for t in xh], Tn=CH, K=1024, M=4096,
               w_dram=wi, bias_dram=bi, evac=evb, mblock=512)

        loc = contextlib.ExitStack()
        pse = [L.ps("pse", (128, 512), stack=loc) for _ in range(4)]
        for e in range(8):
            ph = Ph(L)
            bt = L.load(ph, bsl[e % 2][:, :],
                        B_d[e * 16:(e + 1) * 16, :])
            for m in range(32):
                bank = m % 4

                def ft(q, e=e, m=m, bank=bank, bt=bt):
                    if m == 0:
                        q.wait_ge(L.sem("Ld"), bt)
                    if m >= 4:
                        q.wait_ge(L.sem("Ac"), L._ex_ac[(e, m - 4)])
                    q.matmul(pse[bank][:, :], L.ident_b[:, :], base[m][:, 0:512],
                             start=True, stop=False)
                    q.matmul(pse[bank][:, :], bsl[e % 2][0:16, m * 128:(m + 1) * 128],
                             a_e[e][0:16, 0:512],
                             start=False, stop=True).then_inc(L.sem("Pe"), 1)
                pe = L.bump("Pe")
                ph.t(ft)

                def fs(q, e=e, m=m, bank=bank, pe=pe):
                    q.wait_ge(L.sem("Pe"), pe)
                    if m >= 2:
                        q.wait_ge(L.sem("Ve"), L._ex_ve[(e, m - 2)])
                    q.activation(r_sb[m % 2][:, :], pse[bank][:, :],
                                 AF.Relu).then_inc(L.sem("Ac"), 1)
                L._ex_ac = getattr(L, "_ex_ac", {})
                L._ex_ac[(e, m)] = L.bump("Ac")
                ph.s(fs)

                def fv(q, e=e, m=m, pe=pe):
                    q.wait_ge(L.sem("Ac"), L._ex_ac[(e, m)])
                    if e == 0:
                        q.tensor_tensor(hsum[m][:, 0:512], r_sb[m % 2][:, :],
                                        gbc[e][:, :], ALU.mult).then_inc(
                                            L.sem("Ve"), 1)
                    else:
                        q.tensor_tensor(r_sb[m % 2][:, :], r_sb[m % 2][:, :],
                                        gbc[e][:, :], ALU.mult)
                        q.tensor_tensor(hsum[m][:, 0:512], hsum[m][:, 0:512],
                                        r_sb[m % 2][:, :], ALU.add).then_inc(
                                            L.sem("Ve"), 1)
                L._ex_ve = getattr(L, "_ex_ve", {})
                L._ex_ve[(e, m)] = L.bump("Ve")
                ph.v(fv)
            ph.run()
        loc.close()

        if DEBUG_L0 and ch == 0:
            ph = Ph(L)
            L.store(ph, dbg["d_xh"], xh[0][:, :], queue="gpsimd")
            L.store(ph, dbg["d_base"], base[0][:, :], queue="gpsimd")
            L.store(ph, dbg["d_hsum"], hsum[0][:, :], queue="gpsimd")
            L.store(ph, dbg["d_gT"], g8T[:, 0:512], queue="gpsimd")
            L.store(ph, dbg["d_gbc"], gbc[0][:, :], queue="gpsimd")
            L.store(ph, dbg["d_a0"], a_e[0][:, :], queue="gpsimd")
            L.store(ph, dbg["d_tkA"], tkA[:, :], queue="gpsimd")
            L.store(ph, dbg["d_tkB"], tkB128[:, 0:8], queue="gpsimd")
            L.store(ph, dbg["d_psr"], tkD[:, :], queue="gpsimd")
            L.store(ph, dbg["d_tkC"], tkC[:, :], queue="gpsimd")
            L.store(ph, dbg["d_m1"], m1c[:, :], queue="gpsimd")
            L.store(ph, dbg["d_m2"], m2c[:, :], queue="gpsimd")
            L.wait(ph, "y", "St", L.cnt("St"))
            ph.run()

        def evo(mi, c, ps_ap, ph, pe):
            return L.ev_addx(ph, pe, ps_ap, x_sb[mi][:, c * 512:(c + 1) * 512], 0.5)
        L.proj(rhs_tiles=[t[:, :] for t in hsum], Tn=CH, K=4096, M=1024,
               w_dram=wo, bias_dram=bo, evac=evo, mblock=128)

        ph = Ph(L)
        for f in range(8):
            L.store(ph, x_out[f * 128:(f + 1) * 128, c0:c0 + CH], x_sb[f][:, :])
        ph.run()
    L.finish()
    return L


def build_l1ffn():
    L = Launch("l1ffn")
    x_in = L.dram_in("x_in", (DD, TC))
    x_out = L.dram_out("x_out", (DD, TC))
    bd1 = L.dram_in("bd1", (4 * 8 * 128, 128))
    bd2 = L.dram_in("bd2", (4 * 32 * 128, 128))
    cw = L.dram_in("cw", (16384, 1024))

    L.emit_consts()
    CH = 512
    x_sb = [L.sb("x", (128, CH), F32) for _ in range(8)]
    xh = [L.sb("xh", (128, CH), BF16) for _ in range(8)]
    yh = [L.sb("yh", (128, CH), BF16) for _ in range(8)]
    hz = [L.sb("hz", (128, CH), BF16) for _ in range(32)]
    acc = [L.sb("acc", (128, CH), F32) for _ in range(8)]
    sq = L.sb("sq", (128, 8 * 512), F32)
    bd1_sb = L.sb("bd1", (128, 32 * 128), BF16)
    bd2_sb = [L.sb("bd2", (128, 32 * 128), BF16) for _ in range(2)]
    csl = [L.sb("csl", (128, 4 * 1024), BF16) for _ in range(2)]

    ph = Ph(L)
    t = L.load(ph, bd1_sb[:, :].rearrange("p (k m) -> p k m", k=32),
               bd1.rearrange("(k p) m -> p k m", p=128))
    ph.run()

    for ch in range(TC // CH):
        c0 = ch * CH
        ph = Ph(L)
        L.wait(ph, "y", "St", L.cnt("St"))
        tgts = [L.load(ph, x_sb[f][:, :], x_in[f * 128:(f + 1) * 128, c0:c0 + CH],
                       queue="sync") for f in range(8)]
        L.wait(ph, "v", "Ld", tgts[-1])
        if ch == 0:
            L.wait(ph, "t", "Ld", t)
        ph.run()
        L.ln_window([x[:, :] for x in x_sb], CH, [x[:, :] for x in xh], sq[:, :])

        ph = Ph(L)

        def fz(q):
            for f in range(8):
                q.memset(acc[f][:, :], 0.0)
        ph.g(fz)
        ph.run()

        for h in range(4):
            loc = contextlib.ExitStack()
            ph = Ph(L)
            bt = L.load(ph, bd2_sb[h % 2][:, :].rearrange("p (k m) -> p k m", k=32),
                        bd2[h * 32 * 128:(h + 1) * 32 * 128, :].rearrange(
                            "(k p) m -> p k m", p=128))
            ph.run()

            psy = [L.ps("psy", (128, 512), stack=loc) for _ in range(4)]
            ph = Ph(L)
            for f in range(8):
                for cc in range(CH // 512):
                    bank = (f * 2 + cc) % 4

                    def ft(q, f=f, cc=cc, bank=bank, h=h):
                        if f >= 2:
                            q.wait_ge(L.sem("Ac"), L._y_ac[(h, f - 2, cc)])
                        q.matmul(psy[bank][:, :],
                                 bd1_sb[:, (h * 8 + f) * 128:(h * 8 + f + 1) * 128],
                                 xh[f][:, cc * 512:(cc + 1) * 512],
                                 start=True, stop=True).then_inc(L.sem("Pe"), 1)
                    pe = L.bump("Pe")
                    ph.t(ft)

                    def fs(q, f=f, cc=cc, bank=bank, h=h, pe=pe):
                        q.wait_ge(L.sem("Pe"), pe)
                        q.activation(yh[f][:, cc * 512:(cc + 1) * 512],
                                     psy[bank][:, :], AF.Copy).then_inc(
                                         L.sem("Ac"), 1)
                    L._y_ac = getattr(L, "_y_ac", {})
                    L._y_ac[(h, f, cc)] = L.bump("Ac")
                    ph.s(fs)
            for m in range(32):
                for cc in range(CH // 512):
                    bank = (m * 2 + cc) % 4

                    def ft2(q, m=m, cc=cc, bank=bank, h=h, bt=bt):
                        if m == 0 and cc == 0:
                            q.wait_ge(L.sem("Ld"), bt)
                            q.wait_ge(L.sem("Ac"), L._y_ac[(h, 7, CH // 512 - 1)])
                        if True:
                            q.wait_ge(L.sem("Ac"), L._z_ac.get((h, m - 2, cc), 0))                                 if m >= 2 else None
                        q.matmul(psy[bank][:, :],
                                 bd2_sb[h % 2][:, m * 128:(m + 1) * 128],
                                 yh[m // 4][:, cc * 512:(cc + 1) * 512],
                                 start=True, stop=True).then_inc(L.sem("Pe"), 1)
                    pe = L.bump("Pe")
                    ph.t(ft2)

                    def fs2(q, m=m, cc=cc, bank=bank, h=h, pe=pe):
                        q.wait_ge(L.sem("Pe"), pe)
                        q.activation(hz[m][:, cc * 512:(cc + 1) * 512],
                                     psy[bank][:, :], AF.Gelu).then_inc(
                                         L.sem("Ac"), 1)
                    L._z_ac = getattr(L, "_z_ac", {})
                    L._z_ac[(h, m, cc)] = L.bump("Ac")
                    ph.s(fs2)
            ph.run()

            pso = [L.ps("pso", (128, 512), stack=loc) for _ in range(4)]
            slab_t = {}
            bank_ve = {}
            ph = Ph(L)
            for sg in range(8):
                def fg(q, sg=sg, h=h):
                    if sg >= 2:
                        q.wait_ge(L.sem("Pe"), L._cp_pe[(h, sg - 2)])
                ph.g(fg)
                slab_t[sg] = L.load(
                    ph, csl[sg % 2][:, :].rearrange("p (k m) -> p k m", k=4),
                    cw[h * 4096 + sg * 512:h * 4096 + (sg + 1) * 512, :].rearrange(
                        "(k p) m -> p k m", p=128))
            for sg in range(8):
                for mo in range(8):
                    for cc in range(CH // 512):
                        bank = (mo * (CH // 512) + cc) % 4

                        def ft3(q, sg=sg, mo=mo, cc=cc, bank=bank, h=h,
                                bt2=slab_t[sg], evprev=bank_ve.get(bank)):
                            if mo == 0 and cc == 0:
                                q.wait_ge(L.sem("Ld"), bt2)
                            if evprev:
                                q.wait_ge(L.sem("Ve"), evprev)
                            mm = None
                            for a in range(4):
                                k = sg * 4 + a
                                mm = q.matmul(
                                    pso[bank][:, :],
                                    csl[sg % 2][:, a * 1024 + mo * 128:a * 1024 + (mo + 1) * 128],
                                    hz[k][:, cc * 512:(cc + 1) * 512],
                                    start=(a == 0), stop=(a == 3))
                            mm.then_inc(L.sem("Pe"), 1)
                        pe = L.bump("Pe")
                        ph.t(ft3)

                        def fv(q, mo=mo, cc=cc, bank=bank, pe=pe):
                            q.wait_ge(L.sem("Pe"), pe)
                            q.tensor_tensor(acc[mo][:, cc * 512:(cc + 1) * 512],
                                            acc[mo][:, cc * 512:(cc + 1) * 512],
                                            pso[bank][:, :], ALU.add).then_inc(
                                                L.sem("Ve"), 1)
                        bank_ve[bank] = L.bump("Ve")
                        ph.v(fv)
                L._cp_pe = getattr(L, "_cp_pe", {})
                L._cp_pe[(h, sg)] = L.cnt("Pe")
            ph.run()
            loc.close()

        ph = Ph(L)

        def fv4(q):
            for f in range(8):
                q.scalar_tensor_tensor(x_sb[f][:, :], acc[f][:, :], 0.5,
                                       x_sb[f][:, :], ALU.mult, ALU.add)
        ph.v(fv4)
        ph.run()
        ph = Ph(L)
        for f in range(8):
            L.store(ph, x_out[f * 128:(f + 1) * 128, c0:c0 + CH], x_sb[f][:, :])
        ph.run()
    L.finish()
    return L



def build_l2attnA():
    L = Launch("l2attnA")
    x_in = L.dram_in("x_in", (DD, TC))
    if DEBUG_L2A:
        d_n2a = L.dram_out("d_n2a", (128, 512))
        d_n2b = L.dram_out("d_n2b", (128, 512))
        d_n2c = L.dram_out("d_n2c", (128, 512))
        d_keys = L.dram_out("d_keys", (128, 512))
    wk = L.dram_in("wk", (1024, 1024))
    bk = L.dram_in("bk", (1024,))
    wv = L.dram_in("wv", (1024, 1024))
    bv = L.dram_in("bv", (1024,))
    bound_o = L.dram_out("bound", (DD, TC))
    kn_o = L.dram_out("kn", (DD, TC), BF16)

    L.emit_consts()
    CH = 512
    x_sb = [L.sb("x", (128, CH), F32) for _ in range(8)]
    xh = [L.sb("xh", (128, CH), BF16) for _ in range(8)]
    kf = [L.sb("kf", (128, CH), F32) for _ in range(8)]
    vf = [L.sb("vf", (128, CH), F32) for _ in range(8)]
    kn = [L.sb("kn", (128, CH), F32) for _ in range(8)]
    bnd = [L.sb("bnd", (128, CH), F32) for _ in range(8)]
    n2 = [L.sb("n2", (128, CH), F32) for _ in range(2)]
    tq = [L.sb("tq", (128, CH), F32) for _ in range(4)]
    sq = L.sb("sq", (128, 8 * 512), F32)

    for ch in range(TC // CH):
        c0 = ch * CH
        ph = Ph(L)
        L.wait(ph, "y", "St", L.cnt("St"))
        tgts = [L.load(ph, x_sb[f][:, :], x_in[f * 128:(f + 1) * 128, c0:c0 + CH],
                       queue="sync") for f in range(8)]
        L.wait(ph, "v", "Ld", tgts[-1])
        ph.run()
        L.ln_window([x[:, :] for x in x_sb], CH, [x[:, :] for x in xh], sq[:, :])

        def evk(mi, c, ps_ap, ph, pe):
            return L.ev_act(ph, pe, ps_ap, kf[mi][:, c * 512:(c + 1) * 512], AF.Copy)
        L.proj(rhs_tiles=[t[:, :] for t in xh], Tn=CH, K=1024, M=1024,
               w_dram=wk, bias_dram=bk, evac=evk, mblock=512)

        def evv(mi, c, ps_ap, ph, pe):
            return L.ev_act(ph, pe, ps_ap, vf[mi][:, c * 512:(c + 1) * 512], AF.Copy)
        L.proj(rhs_tiles=[t[:, :] for t in xh], Tn=CH, K=1024, M=1024,
               w_dram=wv, bias_dram=bv, evac=evv, mblock=512)

        ph = Ph(L)

        def fv(q):
            last = None
            for h in range(2):
                kr, ki, kj, kk = kf[h], kf[2 + h], kf[4 + h], kf[6 + h]
                vr, vi, vj, vk = vf[h], vf[2 + h], vf[4 + h], vf[6 + h]
                q.tensor_tensor(n2[h][:, :], kr[:, :], kr[:, :], ALU.mult)
                q.tensor_tensor(tq[0][:, :], ki[:, :], ki[:, :], ALU.mult)
                q.tensor_tensor(tq[1][:, :], kj[:, :], kj[:, :], ALU.mult)
                q.tensor_tensor(tq[2][:, :], kk[:, :], kk[:, :], ALU.mult)
                q.tensor_tensor(n2[h][:, :], n2[h][:, :], tq[0][:, :], ALU.add)
                q.tensor_tensor(tq[1][:, :], tq[1][:, :], tq[2][:, :], ALU.add)
                last = q.tensor_tensor(n2[h][:, :], n2[h][:, :], tq[1][:, :],
                                       ALU.add)
            last.then_inc(L.sem("Ve"), 1)
        ve = L.bump("Ve")
        ph.v(fv)

        def fs(q, ve=ve):
            q.wait_ge(L.sem("Ve"), ve)
            q.activation(n2[0][:, :], n2[0][:, :], AF.Sqrt)
            q.activation(n2[1][:, :], n2[1][:, :], AF.Sqrt).then_inc(
                L.sem("Ac"), 1)
        ac = L.bump("Ac")
        ph.s(fs)

        def fv2(q, ac=ac):
            q.wait_ge(L.sem("Ac"), ac)
            for h in range(2):
                q.tensor_scalar(tq[h][:, :], n2[h][:, :], 1e-12, None, ALU.max)
            L.vsync(q)
            for h in range(2):
                q.reciprocal(n2[h][:, :], tq[h][:, :])
            L.vsync(q)
            for h in range(2):
                for j in range(4):
                    q.tensor_tensor(kn[2 * j + h][:, :], kf[2 * j + h][:, :],
                                    n2[h][:, :], ALU.mult)
            for h in range(2):
                kr, ki, kj, kk = kn[h], kn[2 + h], kn[4 + h], kn[6 + h]
                vr, vi, vj, vk = vf[h], vf[2 + h], vf[4 + h], vf[6 + h]

                def qm(dst, s1, a1, b1, s2, a2, b2, s3, a3, b3, s4, a4, b4):
                    q.tensor_tensor(tq[0][:, :], a1[:, :], b1[:, :], ALU.mult)
                    q.tensor_tensor(tq[1][:, :], a2[:, :], b2[:, :], ALU.mult)
                    q.tensor_tensor(tq[2][:, :], a3[:, :], b3[:, :], ALU.mult)
                    q.tensor_tensor(tq[3][:, :], a4[:, :], b4[:, :], ALU.mult)
                    q.tensor_tensor(tq[0][:, :], tq[0][:, :], tq[1][:, :],
                                    ALU.add if s2 > 0 else ALU.subtract)
                    q.tensor_tensor(tq[2][:, :], tq[2][:, :], tq[3][:, :],
                                    ALU.add if (s4 * s3 > 0) else ALU.subtract)
                    if s3 > 0:
                        q.tensor_tensor(dst[:, :], tq[0][:, :], tq[2][:, :], ALU.add)
                    else:
                        q.tensor_tensor(dst[:, :], tq[0][:, :], tq[2][:, :],
                                        ALU.subtract)
                qm(bnd[h], 1, vr, kr, -1, vi, ki, -1, vj, kj, -1, vk, kk)
                qm(bnd[2 + h], 1, vr, ki, 1, vi, kr, 1, vj, kk, -1, vk, kj)
                qm(bnd[4 + h], 1, vr, kj, -1, vi, kk, 1, vj, kr, 1, vk, ki)
                qm(bnd[6 + h], 1, vr, kk, 1, vi, kj, -1, vj, ki, 1, vk, kr)
            q.engine_nop().then_inc(L.sem("Ve"), 1)
        ve2 = L.bump("Ve")
        ph.v(fv2)
        ph.run()

        if DEBUG_L2A and ch == 0:
            ph = Ph(L)
            L.store(ph, d_n2c, n2[0][:, :], queue="gpsimd")
            L.store(ph, d_keys, kf[0][:, :], queue="gpsimd")
            L.store(ph, d_n2a, kn[0][:, :], queue="gpsimd")
            L.store(ph, d_n2b, bnd[0][:, :], queue="gpsimd")
            L.wait(ph, "y", "St", L.cnt("St"))
            ph.run()

        ph = Ph(L)
        L.wait(ph, "g", "Ve", ve2)
        for f in range(8):
            L.store(ph, bound_o[f * 128:(f + 1) * 128, c0:c0 + CH], bnd[f][:, :])
            L.store(ph, kn_o[f * 128:(f + 1) * 128, c0:c0 + CH], kn[f][:, :],
                    queue="gpsimd")
        ph.run()
    L.finish()
    return L


def build_l2attnB():
    L = Launch("l2attnB")
    bound_i = L.dram_in("bound", (256, SS))
    kn_i = L.dram_in("kn", (256, SS), BF16)
    rsq_i = L.dram_in("rsq", (1, SS))
    retr_o = L.dram_out("retr", (256, SS))
    L.emit_consts()
    CH = 2048
    bnd = [L.sb("bnd", (64, CH), F32) for _ in range(4)]
    knt = [L.sb("knt", (64, CH), BF16) for _ in range(4)]
    mem = [L.sb("mem", (64, CH), F32) for _ in range(4)]
    ret = [L.sb("ret", (64, CH), F32) for _ in range(4)]
    carry = L.sb("carry", (64, 4), F32)
    onesb = L.sb("onesb", (64, CH), F32)
    rsq = L.sb("rsq", (1, CH), F32)
    rsqb = L.sb("rsqb", (64, CH), F32)
    tq = [L.sb("tq", (64, CH), F32) for _ in range(4)]
    psb = [L.ps("psb", (64, 512)) for _ in range(2)]

    ph = Ph(L)

    def fz(q):
        q.memset(onesb[:, :], 1.0)
        q.memset(carry[:, :], 0.0)
    ph.g(fz)
    ph.run()

    for ch in range(SS // CH):
        c0 = ch * CH
        ph = Ph(L)
        L.wait(ph, "y", "St", L.cnt("St"))
        for j in range(4):
            L.load(ph, bnd[j][:, :], bound_i[j * 64:(j + 1) * 64, c0:c0 + CH],
                   queue="sync")
            L.load(ph, knt[j][:, :], kn_i[j * 64:(j + 1) * 64, c0:c0 + CH],
                   queue="gpsimd")
        L.load(ph, rsq[0:1, :], rsq_i[0:1, c0:c0 + CH], queue="gpsimd")
        L.wait(ph, "v", "Ld", L.cnt("Ld"))
        ph.run()

        ph = Ph(L)
        for cc in range(CH // 512):
            def ft(q, cc=cc):
                if cc >= 2:
                    q.wait_ge(L.sem("Ve"), L._rsq_ve[cc - 2])
                q.matmul(psb[cc % 2][:, :], L.ones_r128_f[0:1, 0:64],
                         rsq[0:1, cc * 512:(cc + 1) * 512],
                         start=True, stop=True).then_inc(L.sem("Pe"), 1)
            pe = L.bump("Pe")
            ph.t(ft)

            def fv0(q, cc=cc, pe=pe):
                q.wait_ge(L.sem("Pe"), pe)
                q.tensor_copy(rsqb[:, cc * 512:(cc + 1) * 512],
                              psb[cc % 2][:, :]).then_inc(L.sem("Ve"), 1)
            L._rsq_ve = getattr(L, "_rsq_ve", {})
            L._rsq_ve[cc] = L.bump("Ve")
            ph.v(fv0)
        ph.run()

        ph = Ph(L)

        def fv(q, ch=ch):
            for j in range(4):
                q.tensor_tensor_scan(mem[j][:, :], onesb[:, :], bnd[j][:, :],
                                     carry[:, j:j + 1], ALU.mult, ALU.add)
            L.vsync(q)
            for j in range(4):
                q.tensor_copy(carry[:, j:j + 1], mem[j][:, CH - 1:CH])
            L.vsync(q)
            mr, mi, mj, mk = (mem[j][:, :] for j in range(4))
            kr, ki, kj, kk = (knt[j][:, :] for j in range(4))
            rr, ri, rj, rk = (ret[j][:, :] for j in range(4))

            def qm(dst, terms):
                for i, (s, aa, bb) in enumerate(terms):
                    q.tensor_tensor(tq[i][:, :], aa, bb, ALU.mult)
                (s1, _, _), (s2, _, _), (s3, _, _), (s4, _, _) = terms
                q.tensor_tensor(tq[0][:, :], tq[0][:, :], tq[1][:, :],
                                ALU.add if s1 * s2 > 0 else ALU.subtract)
                q.tensor_tensor(tq[2][:, :], tq[2][:, :], tq[3][:, :],
                                ALU.add if s3 * s4 > 0 else ALU.subtract)
                q.tensor_tensor(dst, tq[0][:, :], tq[2][:, :],
                                ALU.add if s1 * s3 > 0 else ALU.subtract)
                if s1 < 0:
                    q.tensor_scalar(dst, dst, -1.0, None, ALU.mult)
            qm(rr, [(1, mr, kr), (1, mi, ki), (1, mj, kj), (1, mk, kk)])
            qm(ri, [(-1, mr, ki), (1, mi, kr), (-1, mj, kk), (1, mk, kj)])
            qm(rj, [(-1, mr, kj), (1, mi, kk), (1, mj, kr), (-1, mk, ki)])
            qm(rk, [(-1, mr, kk), (-1, mi, kj), (1, mj, ki), (1, mk, kr)])
            for j in range(4):
                q.tensor_tensor(ret[j][:, :], ret[j][:, :], rsqb[:, :],
                                ALU.mult)
            q.engine_nop().then_inc(L.sem("Ve"), 1)
        ve = L.bump("Ve")
        ph.v(fv)
        ph.run()

        ph = Ph(L)
        L.wait(ph, "y", "Ve", ve)
        for j in range(4):
            L.store(ph, retr_o[j * 64:(j + 1) * 64, c0:c0 + CH], ret[j][:, :])
        ph.run()
    L.finish()
    return L


def build_l2attnC():
    L = Launch("l2attnC")
    x_in = L.dram_in("x_in", (DD, TC))
    retr_i = L.dram_in("retr", (DD, TC))
    wo = L.dram_in("wo", (1024, 1024))
    bo = L.dram_in("bo", (1024,))
    gb = L.dram_in("gb", (128, 16))
    x_out = L.dram_out("x_out", (DD, TC))
    L.emit_consts()
    gb_sb = L.sb("gb", (128, 16), F32)
    ph = Ph(L)
    t = L.load(ph, gb_sb[:, :], gb)
    L.wait(ph, "s", "Ld", t)
    ph.run()

    CH = 512
    x_sb = [L.sb("x", (128, CH), F32) for _ in range(8)]
    x0f = [L.sb("x0f", (128, CH), F32) for _ in range(8)]
    rt = [L.sb("rt", (128, CH), F32) for _ in range(8)]
    rn = [L.sb("rn", (128, CH), BF16) for _ in range(8)]
    sq = L.sb("sq", (128, 8 * 512), F32)
    for ch in range(TC // CH):
        c0 = ch * CH
        ph = Ph(L)
        L.wait(ph, "y", "St", L.cnt("St"))
        for f in range(8):
            L.load(ph, x_sb[f][:, :], x_in[f * 128:(f + 1) * 128, c0:c0 + CH],
                   queue="sync")
            L.load(ph, rt[f][:, :], retr_i[f * 128:(f + 1) * 128, c0:c0 + CH],
                   queue="sync")
        L.wait(ph, "v", "Ld", L.cnt("Ld"))
        ph.run()
        L.ln_window([x[:, :] for x in x_sb], CH, [x[:, :] for x in rn], sq[:, :],
                    out_f32=[x[:, :] for x in x0f])
        L.ln_window([x[:, :] for x in rt], CH, [x[:, :] for x in rn], sq[:, :],
                    gamma_cols=[gb_sb[:, f:f + 1] for f in range(8)],
                    beta_cols=[gb_sb[:, 8 + f:8 + f + 1] for f in range(8)])
        ph = Ph(L)

        def fv(q):
            for f in range(8):
                q.tensor_tensor(x_sb[f][:, :], x_sb[f][:, :], x0f[f][:, :],
                                ALU.add)
        ph.v(fv)
        ph.run()

        def evo(mi, c, ps_ap, ph, pe):
            return L.ev_addx(ph, pe, ps_ap, x_sb[mi][:, c * 512:(c + 1) * 512], 1.0)
        L.proj(rhs_tiles=[t2[:, :] for t2 in rn], Tn=CH, K=1024, M=1024,
               w_dram=wo, bias_dram=bo, evac=evo, mblock=512)

        ph = Ph(L)
        for f in range(8):
            L.store(ph, x_out[f * 128:(f + 1) * 128, c0:c0 + CH], x_sb[f][:, :])
        ph.run()
    L.finish()
    return L



def build_l0attn():
    L = Launch("l0attn")
    HLO = 128
    TH = TC + 2 * HLO
    x_in = L.dram_in("x_in", (DD, TH))
    x_out = L.dram_out("x_out", (DD, TC))
    wq = L.dram_in("wq", (1024, 1024))
    bq = L.dram_in("bq", (1024,))
    wk = L.dram_in("wk", (1024, 256))
    bk = L.dram_in("bk", (256,))
    wv = L.dram_in("wv", (1024, 256))
    bv = L.dram_in("bv", (256,))
    wo = L.dram_in("wo", (1024, 1024))
    bo = L.dram_in("bo", (1024,))
    sinks_d = L.dram_in("sinks", (1, 16))
    nsc = L.dram_in("nsc", (128, 16))
    ctab = L.dram_in("ctab", (128, TH))
    stab = L.dram_in("stab", (128, TH))
    perm_d = L.dram_in("PERM", (128, 128))
    mask_d = L.dram_in("mask", (TC, 384))
    rw = L.dram_in("rw", (1024, 8))
    rb = L.dram_in("rb", (1, 8))
    gup = L.dram_in("gup", (8 * 1024, 2048))
    gupb = L.dram_in("gupb", (8, 2048))
    dwn = L.dram_in("dwn", (8 * 1024, 1024))
    dwnb = L.dram_in("dwnb", (8, 1024))
    oneh_d = L.dram_in("ONEHOT8", (8, 8 * 128))
    h0_d = L.dram_tmp("h0_d", (DD, TH), BF16)
    att_d = L.dram_tmp("att_d", (DD, TC), BF16)
    h1_d = L.dram_tmp("h1_d", (DD, TC), BF16)

    L.emit_consts()
    nsc_sb = L.sb("nsc", (128, 16), F32)
    sink_bc = L.sb("sinkbc", (128, 16), F32)
    nsink_bc = L.sb("nsinkbc", (128, 16), F32)
    oneh_sb = L.sb("oneh", (8, 8 * 128), BF16)
    sk_row = L.sb("skrow", (1, 16), F32)

    ph = Ph(L)
    L.load(ph, nsc_sb[:, :], nsc)
    L.load(ph, oneh_sb[:, :], oneh_d, queue="gpsimd")
    t = L.load(ph, sk_row[0:1, :], sinks_d)
    ph.run()
    ph = Ph(L)
    pss = L.ps("pss", (128, 16))

    def ft0(q, t=t):
        q.wait_ge(L.sem("Ld"), L.cnt("Ld"))
        q.matmul(pss[:, :], L.ones_r128_f[:, :], sk_row[0:1, :],
                 start=True, stop=True).then_inc(L.sem("Pe"), 1)
    pe0 = L.bump("Pe")
    ph.t(ft0)

    def fv0(q, pe0=pe0):
        q.wait_ge(L.sem("Pe"), pe0)
        q.tensor_copy(sink_bc[:, :], pss[:, :])
        q.tensor_scalar(nsink_bc[:, :], pss[:, :], -1.0, None, ALU.mult)
    ph.v(fv0)
    ph.run()

    loc_qkv = contextlib.ExitStack()
    qf = [L.sb("qf", (128, TC), BF16, loc_qkv) for _ in range(8)]
    kf = [L.sb("kf", (128, TH), BF16, loc_qkv) for _ in range(2)]
    vt = [L.sb("vt", (128, 256), BF16, loc_qkv) for _ in range(18)]
    mask_sb = L.sb("mask", (128, 16 * 384), BF16, loc_qkv)
    att_st = L.sb("att_st", (128, TC), BF16, loc_qkv)
    ph = Ph(L)
    L.load(ph, mask_sb[:, :].rearrange("p (tt m) -> p tt m", tt=16),
           mask_d.rearrange("(tt p) m -> p tt m", p=128), queue="gpsimd")
    ph.run()

    loc_a1 = contextlib.ExitStack()
    a1 = [L.sb("a1", (128, TH), BF16, loc_a1) for _ in range(8)]
    perm_sb = L.sb("perm", (128, 128), BF16, loc_a1)
    ct = L.sb("ct", (128, TH), BF16, loc_a1)
    st = L.sb("st", (128, TH), BF16, loc_a1)
    ph = Ph(L)
    L.load(ph, perm_sb[:, :], perm_d, queue="gpsimd")
    L.load(ph, ct[:, :], ctab, queue="gpsimd")
    L.load(ph, st[:, :], stab, queue="gpsimd")
    ph.run()

    LNCH = 256
    loc_sq = contextlib.ExitStack()
    sq = L.sb("sq", (128, 8 * 512), F32, loc_sq)
    x_sc = [L.sb("xsc", (128, LNCH), F32, loc_sq) for _ in range(8)]
    h0st = [L.sb("h0st", (128, LNCH), BF16, loc_sq) for _ in range(8)]

    for c0 in range(0, TH, LNCH):
        cw = min(LNCH, TH - c0)
        ph = Ph(L)
        L.wait(ph, "y", "St", L.cnt("St"))
        tg = [L.load(ph, x_sc[f][:, 0:cw], x_in[f * 128:(f + 1) * 128, c0:c0 + cw],
                     queue="sync") for f in range(8)]
        L.wait(ph, "v", "Ld", tg[-1])
        ph.run()
        L.ln_window([x[:, 0:cw] for x in x_sc], cw,
                    [h[:, 0:cw] for h in h0st], sq[:, :])
        ph = Ph(L)
        for f in range(8):
            L.store(ph, h0_d[f * 128:(f + 1) * 128, c0:c0 + cw], h0st[f][:, 0:cw])
        ph.run()

    if L0_STOP <= 1:
        loc_sq.close(); loc_a1.close(); loc_qkv.close()
        L.finish()
        return L
    for c0 in range(0, TH, LNCH):
        cw = min(LNCH, TH - c0)
        ph = Ph(L)
        L.wait(ph, "y", "St", L.cnt("St"))
        tg = [L.load(ph, h0st[f][:, 0:cw], h0_d[f * 128:(f + 1) * 128, c0:c0 + cw],
                     queue="sync") for f in range(8)]
        L.wait(ph, "v", "Ld", tg[-1])
        L.wait(ph, "s", "Ld", tg[-1])
        ph.run()
        L.ln_window([h[:, 0:cw] for h in h0st], cw,
                    [a[:, c0:c0 + cw] for a in a1], sq[:, :],
                    rms_mode=True, in_bf16=True,
                    gamma_cols=[nsc_sb[:, f:f + 1] for f in range(8)])
    loc_sq.close()
    if L0_STOP <= 2:
        loc_a1.close(); loc_qkv.close()
        L.finish()
        return L

    def evq(mi, c, ps_ap, ph, pe):
        return L.ev_act(ph, pe, ps_ap, qf[mi][:, c * 512:(c + 1) * 512], AF.Copy)
    L.proj(rhs_tiles=[a[:, HLO:HLO + TC] for a in a1], Tn=TC, K=1024, M=1024,
           w_dram=wq, bias_dram=bq, evac=evq, mblock=512)

    def evk(mi, c, ps_ap, ph, pe):
        w = 512 if (c + 1) * 512 <= TH else TH - c * 512
        return L.ev_act(ph, pe, ps_ap, kf[mi][:, c * 512:c * 512 + w], AF.Copy,
                        w=w)
    L.proj(rhs_tiles=[a[:, 0:TH] for a in a1], Tn=TH, K=1024, M=256,
           w_dram=wk, bias_dram=bk, evac=evk, mblock=256)

    locv0 = contextlib.ExitStack()
    wv_sb = L.sb("wv", (128, 8 * 256), BF16, locv0)
    bvrow = L.sb("bvrow", (1, 256), BF16, locv0)
    ph = Ph(L)
    tw = L.load(ph, wv_sb[:, :].rearrange("p (k m) -> p k m", k=8),
                wv.rearrange("(k p) m -> p k m", p=128))
    tb = L.load(ph, bvrow[0:1, :], bv.rearrange("(o m) -> o m", o=1))
    ph.run()
    psv = [L.ps("psv", (128, 256), stack=locv0) for _ in range(4)]
    ph = Ph(L)
    for tt in range(18):
        bank = tt % 4
        cw = min(128, TH - tt * 128)

        def ftv(q, tt=tt, bank=bank, cw=cw, tw=tw, tb=tb):
            if tt == 0:
                q.wait_ge(L.sem("Ld"), tw)
                q.wait_ge(L.sem("Ld"), tb)
            if tt >= 4:
                q.wait_ge(L.sem("Ac"), L._vt_ac[tt - 4])
            for k in range(8):
                q.matmul(psv[bank][0:cw, :], a1[k][:, tt * 128:tt * 128 + cw],
                         wv_sb[:, k * 256:(k + 1) * 256],
                         start=(k == 0), stop=False)
            q.matmul(psv[bank][0:cw, :], L.ones_row_b[0:1, 0:cw], bvrow[0:1, :],
                     start=False, stop=True).then_inc(L.sem("Pe"), 1)
        pe = L.bump("Pe")
        ph.t(ftv)

        def fsv(q, tt=tt, bank=bank, cw=cw, pe=pe):
            q.wait_ge(L.sem("Pe"), pe)
            q.activation(vt[tt][0:cw, :], psv[bank][0:cw, :],
                         AF.Copy).then_inc(L.sem("Ac"), 1)
        L._vt_ac = getattr(L, "_vt_ac", {})
        L._vt_ac[tt] = L.bump("Ac")
        ph.s(fsv)
    ph.run()
    locv0.close()

    locr = contextlib.ExitStack()
    psr = [L.ps("psr", (128, 512), stack=locr) for _ in range(2)]
    rt1 = L.sb("rt1", (128, 512), F32, locr)
    rt2 = L.sb("rt2", (128, 512), F32, locr)
    ph = Ph(L)
    units = [("q", i, c) for i in range(8) for c in range(TC // 512)] + \
            [("k", i, c) for i in range(2) for c in range((TH + 511) // 512)]
    for ui, (kind, i, c) in enumerate(units):
        arr = qf if kind == "q" else kf
        off = HLO if kind == "q" else 0
        cw = 512 if kind == "q" or (c + 1) * 512 <= TH else TH - c * 512
        bank = ui % 2

        def ftr(q, i=i, c=c, cw=cw, bank=bank, ui=ui, arr=arr):
            if ui >= 2:
                q.wait_ge(L.sem("Ve"), L._rp_ve[ui - 2])
            q.matmul(psr[bank][:, 0:cw], perm_sb[:, :],
                     arr[i][:, c * 512:c * 512 + cw],
                     start=True, stop=True).then_inc(L.sem("Pe"), 1)
        pe = L.bump("Pe")
        ph.t(ftr)

        def fvr(q, i=i, c=c, cw=cw, bank=bank, arr=arr, off=off, pe=pe):
            q.wait_ge(L.sem("Pe"), pe)
            sl = slice(c * 512, c * 512 + cw)
            tsl = slice(off + c * 512, off + c * 512 + cw)
            q.tensor_tensor(rt1[:, 0:cw], arr[i][:, sl], ct[:, tsl], ALU.mult)
            q.tensor_tensor(rt2[:, 0:cw], psr[bank][:, 0:cw], st[:, tsl],
                            ALU.mult)
            q.tensor_tensor(arr[i][:, sl], rt1[:, 0:cw], rt2[:, 0:cw],
                            ALU.add).then_inc(L.sem("Ve"), 1)
        L._rp_ve = getattr(L, "_rp_ve", {})
        L._rp_ve[ui] = L.bump("Ve")
        ph.v(fvr)
    ph.run()
    locr.close()
    loc_a1.close()
    if L0_STOP <= 3:
        loc_qkv.close()
        L.finish()
        return L

    loca = contextlib.ExitStack()
    ps_s = [L.ps("ps_s", (128, 384), stack=loca) for _ in range(2)]
    ps_t = [L.ps("ps_t", (128, 128), BF16, stack=loca) for _ in range(2)]
    ps_o = [L.ps("ps_o", (64, 128), stack=loca) for _ in range(2)]
    p_sb = L.sb("p_sb", (128, 384), BF16, loca)
    pT_sb = [L.sb("pT", (128, 128), BF16, loca) for _ in range(3)]
    mrow = L.sb("mrow", (128, 1), F32, loca)
    mneg = L.sb("mneg", (128, 1), F32, loca)
    dsum = L.sb("dsum", (128, 1), F32, loca)
    esink = L.sb("esink", (128, 1), F32, loca)
    rcp = L.sb("rcp", (128, 1), F32, loca)

    Q0 = [0, 1, 2, 3, 8, 9, 10, 11]
    Q1 = [4, 5, 6, 7, 12, 13, 14, 15]
    for pair in range(8):
        ph = Ph(L)
        L.wait(ph, "s", "St", L.cnt("St"))
        for qh in (Q0[pair], Q1[pair]):
            kvh = qh // 4
            qtile = pair
            qbase = 0 if qh in Q0 else 64
            ktile, kbase = kvh // 2, (kvh % 2) * 64
            for tt in range(16):
                sbank = tt % 2

                def fta(q, qh=qh, tt=tt, sbank=sbank, qtile=qtile, qbase=qbase,
                        ktile=ktile, kbase=kbase,
                        prev=L.cnt("Ac")):
                    if prev:
                        q.wait_ge(L.sem("Ac"), prev)
                    q.matmul(ps_s[sbank][:, :],
                             qf[qtile][qbase:qbase + 64, tt * 128:(tt + 1) * 128],
                             kf[ktile][kbase:kbase + 64, tt * 128:tt * 128 + 384],
                             start=True, stop=True).then_inc(L.sem("Pe"), 1)
                pe1 = L.bump("Pe")
                ph.t(fta)

                def fva(q, qh=qh, tt=tt, sbank=sbank, pe1=pe1):
                    q.wait_ge(L.sem("Pe"), pe1)
                    q.reduce_max(mrow[:, :], ps_s[sbank][:, :], axis=AX.X)
                    L.vsync(q)
                    q.tensor_scalar(mneg[:, :], mrow[:, :], -1.0, None, ALU.mult)
                    L.vsync(q)
                    q.tensor_tensor(mneg[:, :], mneg[:, :],
                                    nsink_bc[:, qh:qh + 1], ALU.min).then_inc(
                                        L.sem("Ve"), 1)
                ve1 = L.bump("Ve")
                ph.v(fva)

                def fsa(q, qh=qh, sbank=sbank, ve1=ve1):
                    q.wait_ge(L.sem("Ve"), ve1)
                    q.activation(p_sb[:, :], ps_s[sbank][:, :], AF.Exp,
                                 bias=mneg[:, :])
                    q.activation(esink[:, :], sink_bc[:, qh:qh + 1], AF.Exp,
                                 bias=mneg[:, :]).then_inc(L.sem("Ac"), 1)
                L._at_ac = getattr(L, "_at_ac", {})
                ac1 = L.bump("Ac")
                L._at_ac[(qh, tt)] = ac1
                ph.s(fsa)

                def fvb(q, tt=tt, ac1=ac1):
                    q.wait_ge(L.sem("Ac"), ac1)
                    q.tensor_tensor(p_sb[:, :], p_sb[:, :],
                                    mask_sb[:, tt * 384:(tt + 1) * 384], ALU.mult)
                    L.vsync(q)
                    q.reduce_sum(dsum[:, :], p_sb[:, :], axis=AX.X)
                    L.vsync(q)
                    q.tensor_tensor(dsum[:, :], dsum[:, :], esink[:, :], ALU.add)
                    L.vsync(q)
                    q.reciprocal(rcp[:, :], dsum[:, :])
                    L.vsync(q)
                    q.engine_nop().then_inc(L.sem("Ve"), 1)
                ve2 = L.bump("Ve")
                ph.v(fvb)

                def fsb(q, ve2=ve2):
                    q.wait_ge(L.sem("Ve"), ve2)
                    q.activation(p_sb[:, :], p_sb[:, :], AF.Identity,
                                 scale=rcp[:, :]).then_inc(L.sem("Ac"), 1)
                ac2 = L.bump("Ac")
                ph.s(fsb)

                for j in range(3):
                    tbank = j % 2

                    def ftt(q, qh=qh, tt=tt, j=j, tbank=tbank, ac2=ac2):
                        if j == 0:
                            q.wait_ge(L.sem("Ac"), ac2)
                        if j >= 2:
                            q.wait_ge(L.sem("Ve"), L._tp_ve[(qh, tt, j - 2)])
                        q.transpose(ps_t[tbank][:, :],
                                    p_sb[:, j * 128:(j + 1) * 128],
                                    L.ident_b[:, :]).then_inc(L.sem("Pe"), 1)
                    pe2 = L.bump("Pe")
                    ph.t(ftt)

                    def fvt(q, qh=qh, tt=tt, j=j, tbank=tbank, pe2=pe2):
                        q.wait_ge(L.sem("Pe"), pe2)
                        q.tensor_copy(pT_sb[j][:, :],
                                      ps_t[tbank][:, :]).then_inc(L.sem("Ve"), 1)
                    L._tp_ve = getattr(L, "_tp_ve", {})
                    L._tp_ve[(qh, tt, j)] = L.bump("Ve")
                    ph.v(fvt)
                velast = L._tp_ve[(qh, tt, 2)]

                def ftp(q, qh=qh, tt=tt, sbank=sbank, velast=velast, kvh=kvh):
                    q.wait_ge(L.sem("Ve"), velast)
                    if tt >= 2:
                        q.wait_ge(L.sem("Ac"), L._pv_ac[(qh, tt - 2)])
                    mm = None
                    for j in range(3):
                        kt = tt + j
                        mm = q.matmul(ps_o[sbank][:, :],
                                      vt[kt][:, kvh * 64:(kvh + 1) * 64],
                                      pT_sb[j][:, :],
                                      start=(j == 0), stop=(j == 2))
                    mm.then_inc(L.sem("Pe"), 1)
                pe3 = L.bump("Pe")
                ph.t(ftp)

                def fse(q, qh=qh, tt=tt, sbank=sbank, pe3=pe3, qbase=qbase):
                    q.wait_ge(L.sem("Pe"), pe3)
                    obase = qbase
                    q.activation(att_st[obase:obase + 64,
                                        tt * 128:(tt + 1) * 128],
                                 ps_o[sbank][:, :], AF.Copy).then_inc(
                                     L.sem("Ac"), 1)
                L._pv_ac = getattr(L, "_pv_ac", {})
                L._pv_ac[(qh, tt)] = L.bump("Ac")
                ph.s(fse)
            L._at_last = L._at_ac[(qh, 15)]
        ph.run()
        ph = Ph(L)
        L.store(ph, att_d[pair * 128:(pair + 1) * 128, :], att_st[:, :])
        ph.run()
    loca.close()
    loc_qkv.close()
    if L0_STOP <= 4:
        L.finish()
        return L

    loc5 = contextlib.ExitStack()
    att_full = [L.sb("attf", (128, TC), BF16, loc5) for _ in range(8)]
    h0loc = [L.sb("h0loc", (128, TC), BF16, loc5) for _ in range(8)]
    h1 = [L.sb("h1", (128, TC), BF16, loc5) for _ in range(8)]
    ph = Ph(L)
    L.wait(ph, "y", "St", L.cnt("St"))
    for f in range(8):
        L.load(ph, att_full[f][:, :], att_d[f * 128:(f + 1) * 128, :],
               queue="sync")
        L.load(ph, h0loc[f][:, :], h0_d[f * 128:(f + 1) * 128, HLO:HLO + TC],
               queue="sync")
    L.wait(ph, "v", "Ld", L.cnt("Ld"))
    L.wait(ph, "t", "Ld", L.cnt("Ld"))
    ph.run()

    def evo(mi, c, ps_ap, ph, pe):
        ev = L.bump("Ev")

        def f(q, mi=mi, c=c, ps_ap=ps_ap, pe=pe):
            q.wait_ge(L.sem("Pe"), pe)
            q.tensor_tensor(h1[mi][:, c * 512:(c + 1) * 512], ps_ap[:, 0:512],
                            h0loc[mi][:, c * 512:(c + 1) * 512],
                            ALU.add).then_inc(L.sem("Ev"), 1)
        ph.v(f)
        return ev
    L.proj(rhs_tiles=[a[:, 0:TC] for a in att_full], Tn=TC, K=1024, M=1024,
           w_dram=wo, bias_dram=bo, evac=evo, mblock=512)
    ph = Ph(L)
    for f in range(8):
        L.store(ph, h1_d[f * 128:(f + 1) * 128, :], h1[f][:, :])
    ph.run()
    loc5.close()
    if L0_STOP <= 5:
        L.finish()
        return L
    return _l0attn_moe(L, x_in, x_out, h1_d, nsc_sb, oneh_sb,
                       rw, rb, gup, gupb, dwn, dwnb, HLO)


def _l0attn_moe(L, x_in, x_out, h1_d, nsc_sb, oneh_sb,
                rw, rb, gup, gupb, dwn, dwnb, HLO):
    loc = contextlib.ExitStack()
    h1 = [L.sb("h1m", (128, 1024), BF16, loc) for _ in range(8)]
    a2 = [L.sb("a2", (128, 1024), BF16, loc) for _ in range(8)]
    moe_acc = [L.sb("macc", (128, 1024), F32, loc) for _ in range(8)]
    sq = L.sb("sqm", (128, 8 * 512), F32, loc)
    gcl = [L.sb("gcl", (128, 512), BF16, loc) for _ in range(8)]
    sg = [L.sb("sg", (128, 512), BF16, loc) for _ in range(8)]
    gtd = [L.sb("gtd", (128, 512), BF16, loc) for _ in range(8)]
    gbc = L.sb("gbc", (128, 512), BF16, loc)
    g8T = L.sb("g8T", (8, 1024), BF16, loc)
    dwnb_sb = L.sb("dwnb", (8, 1024), BF16, loc)
    rw_sb = L.sb("rw", (128, 64), BF16, loc)
    rb_sb = L.sb("rb", (1, 8), F32, loc)
    x_sc2 = [L.sb("xsc2", (128, 512), F32, loc) for _ in range(8)]
    ph = Ph(L)
    L.load(ph, dwnb_sb[:, :], dwnb)
    L.load(ph, rw_sb[:, :].rearrange("p (k m) -> p k m", k=8),
           rw.rearrange("(k p) m -> p k m", p=128))
    t = L.load(ph, rb_sb[0:1, :], rb)
    L.wait(ph, "t", "Ld", t)
    ph.run()

    locr = contextlib.ExitStack()
    tkD = L.sb("tkD", (128, 8), F32, locr)
    tkA = L.sb("tkA", (128, 8), F32, locr)
    tkB128 = L.sb("tkB128", (128, 128), F32, locr)
    tkC = L.sb("tkC", (128, 1), F32, locr)
    tkCn = L.sb("tkCn", (128, 1), F32, locr)
    tkE = L.sb("tkE", (128, 8), F32, locr)
    tkR = L.sb("tkR", (128, 1), F32, locr)
    mcur = L.sb("mcur", (128, 1), F32, locr)
    m1c = L.sb("m1c", (128, 1), F32, locr)
    ph = Ph(L)

    def fz(q):
        q.memset(tkB128[:, :], 0.0)
    ph.g(fz)
    ph.run()

    for half in range(2):
        H0 = half * 1024
        ph = Ph(L)
        L.wait(ph, "y", "St", L.cnt("St"))
        tg = [L.load(ph, h1[f][:, :], h1_d[f * 128:(f + 1) * 128, H0:H0 + 1024],
                     queue="sync") for f in range(8)]
        L.wait(ph, "v", "Ld", tg[-1])
        L.wait(ph, "s", "Ld", tg[-1])

        def fz2(q):
            for f in range(8):
                q.memset(moe_acc[f][:, :], 0.0)
        ph.g(fz2)
        ph.run()
        for c0 in range(0, 1024, 512):
            L.ln_window([h[:, c0:c0 + 512] for h in h1], 512,
                        [a[:, c0:c0 + 512] for a in a2], sq[:, :],
                        rms_mode=True, in_bf16=True,
                        gamma_cols=[nsc_sb[:, 8 + f:8 + f + 1] for f in range(8)])

        locps = contextlib.ExitStack()
        psr = L.ps("psr", (128, 8), stack=locps)
        pst = L.ps("pst", (128, 128), stack=locps)
        for tt in range(1024 // 128):
            ph = Ph(L)

            def ft(q, tt=tt):
                for k in range(8):
                    q.matmul(psr[:, :], a2[k][:, tt * 128:(tt + 1) * 128],
                             rw_sb[:, k * 8:(k + 1) * 8],
                             start=(k == 0), stop=False)
                q.matmul(psr[:, :], L.ones_r128_f[:, :], rb_sb[:, :],
                         start=False, stop=True).then_inc(L.sem("Pe"), 1)
            pe = L.bump("Pe")
            ph.t(ft)

            def fv(q, tt=tt, pe=pe):
                q.wait_ge(L.sem("Pe"), pe)
                q.tensor_copy(tkD[:, :], psr[:, :])
                L.vsync(q)
                q.tensor_copy(tkE[:, :], tkD[:, :])
                L.vsync(q)
                for r in range(4):
                    q.reduce_max(mcur[:, :], tkE[:, :], axis=AX.X)
                    L.vsync(q)
                    if r == 0:
                        q.tensor_copy(m1c[:, :], mcur[:, :])
                    if r < 3:
                        q.tensor_scalar(tkA[:, :], tkE[:, :], mcur[:, :], None,
                                        ALU.is_ge)
                        L.vsync(q)
                        q.tensor_scalar(tkA[:, :], tkA[:, :], 1e9, None,
                                        ALU.mult)
                        L.vsync(q)
                        q.tensor_tensor(tkE[:, :], tkE[:, :], tkA[:, :],
                                        ALU.subtract)
                        L.vsync(q)
                q.tensor_scalar(tkA[:, :], tkD[:, :], mcur[:, :], None,
                                ALU.is_ge)
                q.tensor_scalar(tkCn[:, :], m1c[:, :], -1.0, None,
                                ALU.mult).then_inc(L.sem("Ve"), 1)
            ve = L.bump("Ve")
            ph.v(fv)

            def fs(q, tt=tt, ve=ve):
                q.wait_ge(L.sem("Ve"), ve)
                q.activation(tkE[:, :], tkD[:, :], AF.Exp,
                             bias=tkCn[:, :]).then_inc(L.sem("Ac"), 1)
            ac = L.bump("Ac")
            ph.s(fs)

            def fv2(q, tt=tt, ac=ac):
                q.wait_ge(L.sem("Ac"), ac)
                q.tensor_tensor(tkE[:, :], tkE[:, :], tkA[:, :], ALU.mult)
                L.vsync(q)
                q.reduce_sum(tkC[:, :], tkE[:, :], axis=AX.X)
                L.vsync(q)
                q.tensor_scalar(tkC[:, :], tkC[:, :], 4.0, None, ALU.mult)
                L.vsync(q)
                q.reciprocal(tkR[:, :], tkC[:, :])
                L.vsync(q)
                q.tensor_scalar(tkB128[:, 0:8], tkE[:, :], tkR[:, :], None,
                                ALU.mult).then_inc(L.sem("Ve"), 1)
            ve2 = L.bump("Ve")
            ph.v(fv2)

            def ft2(q, tt=tt, ve2=ve2):
                q.wait_ge(L.sem("Ve"), ve2)
                q.transpose(pst[:, :], tkB128[:, :], L.ident_f[:, :]).then_inc(
                    L.sem("Pe"), 1)
            pe2 = L.bump("Pe")
            ph.t(ft2)

            def fv3(q, tt=tt, pe2=pe2):
                q.wait_ge(L.sem("Pe"), pe2)
                q.tensor_copy(g8T[:, tt * 128:(tt + 1) * 128], pst[0:8, :])
            ph.v(fv3)
            ph.run()

        locps.close()
        if L0_STOP <= 6:
            continue
        locps2 = contextlib.ExitStack()
        psg = L.ps("psg", (128, 512), stack=locps2)
        psd = [L.ps("psd", (128, 512), stack=locps2) for _ in range(2)]
        for c in range(2):
            for e in range(8):
                ph = Ph(L)

                def ftg(q, e=e, c=c):
                    q.matmul(psg[:, :], oneh_sb[:, e * 128:(e + 1) * 128],
                             g8T[:, c * 512:(c + 1) * 512],
                             start=True, stop=True).then_inc(L.sem("Pe"), 1)
                pe = L.bump("Pe")
                ph.t(ftg)

                def fvg(q, pe=pe):
                    q.wait_ge(L.sem("Pe"), pe)
                    q.tensor_copy(gbc[:, :], psg[:, :])
                ph.v(fvg)
                ph.run()

                def evgu(mi, cc, ps_ap, ph2, pe2):
                    ev1 = L.bump("Ev")

                    def f(q, mi=mi, ps_ap=ps_ap, pe2=pe2):
                        q.wait_ge(L.sem("Pe"), pe2)
                        if mi < 8:
                            q.activation(sg[mi][:, :], ps_ap[:, 0:512],
                                         AF.Sigmoid, scale=IM(1.702)).then_inc(
                                             L.sem("Ev"), 1)
                        else:
                            q.nop().then_inc(L.sem("Ev"), 1)
                    ph2.s(f)
                    ev = L.bump("Ev")

                    def f2(q, mi=mi, ps_ap=ps_ap, pe2=pe2, ev1=ev1):
                        q.wait_ge(L.sem("Ev"), ev1)
                        if mi < 8:
                            q.tensor_scalar(gcl[mi][:, :], ps_ap[:, 0:512], 7.0,
                                            None, ALU.min).then_inc(
                                                L.sem("Ev"), 1)
                        else:
                            q.tensor_scalar(gtd[mi - 8][:, :], ps_ap[:, 0:512],
                                            7.0, -7.0, ALU.min,
                                            ALU.max).then_inc(L.sem("Ev"), 1)
                    ph2.v(f2)
                    return ev
                L.proj(rhs_tiles=[a[:, c * 512:(c + 1) * 512] for a in a2],
                       Tn=512, K=1024, M=2048,
                       w_dram=gup[e * 1024:(e + 1) * 1024, :],
                       bias_dram=gupb[e:e + 1, :],
                       evac=evgu, mblock=512)
                ph2 = Ph(L)

                def fvm(q):
                    last = None
                    for mi in range(8):
                        q.tensor_tensor(gcl[mi][:, :], gcl[mi][:, :],
                                        sg[mi][:, :], ALU.mult)
                    for mi in range(8):
                        q.scalar_tensor_tensor(gtd[mi][:, :], gtd[mi][:, :],
                                               1.0, gcl[mi][:, :], ALU.add,
                                               ALU.mult)
                    for mi in range(8):
                        last = q.tensor_tensor(gtd[mi][:, :], gtd[mi][:, :],
                                               gbc[:, :], ALU.mult)
                    last.then_inc(L.sem("Ve"), 1)
                L.bump("Ve")
                ph2.v(fvm)
                ph2.run()

                def evd(mi, cc, ps_ap, ph3, pe3):
                    ev = L.bump("Ev")

                    def f(q, mi=mi, ps_ap=ps_ap, pe3=pe3, c=c):
                        q.wait_ge(L.sem("Pe"), pe3)
                        q.tensor_tensor(moe_acc[mi][:, c * 512:(c + 1) * 512],
                                        moe_acc[mi][:, c * 512:(c + 1) * 512],
                                        ps_ap[:, 0:512], ALU.add).then_inc(
                                            L.sem("Ev"), 1)
                    ph3.v(f)
                    return ev
                L.proj(rhs_tiles=[t2[:, :] for t2 in gtd], Tn=512, K=1024,
                       M=1024, w_dram=dwn[e * 1024:(e + 1) * 1024, :],
                       evac=evd, mblock=512)
            ph = Ph(L)
            for mi in range(8):
                bank = mi % 2

                def ftb(q, mi=mi, c=c, bank=bank):
                    if mi >= 2:
                        q.wait_ge(L.sem("Ve"), L._db_ve[(half, c, mi - 2)])
                    q.matmul(psd[bank][:, :],
                             dwnb_sb[:, mi * 128:(mi + 1) * 128],
                             g8T[:, c * 512:(c + 1) * 512],
                             start=True, stop=True).then_inc(L.sem("Pe"), 1)
                pe = L.bump("Pe")
                ph.t(ftb)

                def fvb(q, mi=mi, c=c, bank=bank, pe=pe):
                    q.wait_ge(L.sem("Pe"), pe)
                    q.tensor_tensor(moe_acc[mi][:, c * 512:(c + 1) * 512],
                                    moe_acc[mi][:, c * 512:(c + 1) * 512],
                                    psd[bank][:, 0:512], ALU.add).then_inc(
                                        L.sem("Ve"), 1)
                L._db_ve = getattr(L, "_db_ve", {})
                L._db_ve[(half, c, mi)] = L.bump("Ve")
                ph.v(fvb)
            ph.run()
        locps2.close()
        if L0_STOP <= 7:
            continue

        HLO_ = HLO
        for c in range(2):
            ph = Ph(L)
            L.wait(ph, "y", "St", L.cnt("St"))
            tg = [L.load(ph, x_sc2[f][:, :],
                         x_in[f * 128:(f + 1) * 128,
                              HLO_ + H0 + c * 512:HLO_ + H0 + (c + 1) * 512],
                         queue="sync") for f in range(8)]
            L.wait(ph, "v", "Ld", tg[-1])
            ph.run()
            ph = Ph(L)

            def fvx(q, c=c):
                last = None
                for f in range(8):
                    q.tensor_tensor(x_sc2[f][:, :], x_sc2[f][:, :],
                                    h1[f][:, c * 512:(c + 1) * 512], ALU.add)
                    last = q.tensor_tensor(x_sc2[f][:, :], x_sc2[f][:, :],
                                           moe_acc[f][:, c * 512:(c + 1) * 512],
                                           ALU.add)
                last.then_inc(L.sem("Ve"), 1)
            L.bump("Ve")
            ph.v(fvx)
            ph.run()
            ph = Ph(L)
            for f in range(8):
                L.store(ph, x_out[f * 128:(f + 1) * 128,
                               H0 + c * 512:H0 + (c + 1) * 512], x_sc2[f][:, :])
            ph.run()
    locr.close()
    loc.close()
    L.finish()
    return L



def build_l3attnA():
    L = Launch("l3attnA")
    x_i = L.dram_in("x_cb", (256, SS))
    gb = L.dram_in("gb", (128, 4))
    g4_d = L.dram_in("G4", (128, 4))
    g4t_d = L.dram_in("G4T", (4, 128))
    hn_o = L.dram_out("hn", (256, SS), BF16)
    L.emit_consts()
    EPSG = 1e-6
    g4 = L.sb("g4", (128, 4), F32)
    g4t = L.sb("g4t", (4, 128), F32)
    gb_sb = L.sb("gb", (128, 4), F32)
    xt = [L.sb("xt", (128, 512), F32) for _ in range(4)]
    hnt = L.sb("hnt", (128, 512), BF16)
    sqs = L.sb("sqs", (128, 512), F32)
    t4a = L.sb("t4a", (4, 8), F32)
    scol = [L.sb("scol", (128, 1), F32) for _ in range(2)]
    bcol = [L.sb("bcol", (128, 1), F32) for _ in range(2)]
    ph = Ph(L)
    L.load(ph, g4[:, :], g4_d)
    L.load(ph, g4t[:, :], g4t_d, queue="gpsimd")
    t = L.load(ph, gb_sb[:, :], gb)
    L.wait(ph, "t", "Ld", t)
    L.wait(ph, "v", "Ld", t)
    ph.run()

    loc = contextlib.ExitStack()
    ps1 = L.ps("ps1", (4, 512), stack=loc)
    ps2 = L.ps("ps2", (4, 512), stack=loc)
    pse = [L.ps("pse", (128, 1), stack=loc) for _ in range(2)]
    for t2 in range(2):
        ph = Ph(L)
        for cc in range(SS // 512):
            if cc >= 4:
                L.wait(ph, "y", "Pe", L._a_pe[(t2, cc - 4)])
            tg = L.load(ph, xt[cc % 4][:, :],
                        x_i[t2 * 128:(t2 + 1) * 128, cc * 512:(cc + 1) * 512],
                        queue="sync")

            def fsq(q, cc=cc, tg=tg):
                q.wait_ge(L.sem("Ld"), tg)
                if cc >= 1:
                    q.wait_ge(L.sem("Pe"), L._a_pe[(t2, cc - 1)])
                q.activation(sqs[:, :], xt[cc % 4][:, :],
                             AF.Square).then_inc(L.sem("Ac"), 1)
            ac = L.bump("Ac")
            ph.s(fsq)

            def ftm(q, t2=t2, cc=cc, ac=ac, tg=tg):
                q.wait_ge(L.sem("Ld"), tg)
                q.wait_ge(L.sem("Ac"), ac)
                q.matmul(ps1[:, :], g4[:, :], xt[cc % 4][:, :],
                         start=(cc == 0), stop=(cc == SS // 512 - 1))
                q.matmul(ps2[:, :], g4[:, :], sqs[:, :],
                         start=(cc == 0),
                         stop=(cc == SS // 512 - 1)).then_inc(L.sem("Pe"), 1)
            L._a_pe = getattr(L, "_a_pe", {})
            L._a_pe[(t2, cc)] = L.bump("Pe")
            ph.t(ftm)
        ph.run()

        ph = Ph(L)
        pe = L.cnt("Pe")

        def fvr(q, pe=pe):
            q.wait_ge(L.sem("Pe"), pe)
            q.reduce_sum(t4a[:, 0:1], ps1[:, :], axis=AX.X)
            L.vsync(q)
            q.reduce_sum(t4a[:, 1:2], ps2[:, :], axis=AX.X)
            L.vsync(q)
            N = float(SS * 32)
            q.tensor_scalar(t4a[:, 2:3], t4a[:, 0:1], 1.0 / N, None, ALU.mult)
            L.vsync(q)
            q.tensor_scalar(t4a[:, 3:4], t4a[:, 1:2], 1.0 / N, None, ALU.mult)
            L.vsync(q)
            q.tensor_tensor(t4a[:, 4:5], t4a[:, 2:3], t4a[:, 2:3], ALU.mult)
            L.vsync(q)
            q.tensor_tensor(t4a[:, 5:6], t4a[:, 3:4], t4a[:, 4:5], ALU.subtract)
            L.vsync(q)
            q.engine_nop().then_inc(L.sem("Ve"), 1)
        ve = L.bump("Ve")
        ph.v(fvr)

        def fsr(q, ve=ve):
            q.wait_ge(L.sem("Ve"), ve)
            q.activation(t4a[:, 6:7], t4a[:, 5:6], AF.Ln, bias=IM(EPSG))
            q.activation(t4a[:, 7:8], t4a[:, 6:7], AF.Exp,
                         scale=IM(-0.5)).then_inc(L.sem("Ac"), 1)
        ac2 = L.bump("Ac")
        ph.s(fsr)

        def fte(q, ac2=ac2):
            q.wait_ge(L.sem("Ac"), ac2)
            mm = q.matmul(pse[0][:, :], g4t[:, :], t4a[:, 7:8],
                          start=True, stop=True)
            mm = q.matmul(pse[1][:, :], g4t[:, :], t4a[:, 2:3],
                          start=True, stop=True)
            mm.then_inc(L.sem("Pe"), 1)
        pe2 = L.bump("Pe")
        ph.t(fte)

        def fvc(q, t2=t2, pe2=pe2):
            q.wait_ge(L.sem("Pe"), pe2)
            q.tensor_tensor(scol[t2][:, :], pse[0][:, :], gb_sb[:, t2:t2 + 1],
                            ALU.mult)
            L.vsync(q)
            q.tensor_tensor(bcol[t2][:, :], pse[1][:, :], scol[t2][:, :],
                            ALU.mult)
            L.vsync(q)
            q.tensor_tensor(bcol[t2][:, :], gb_sb[:, 2 + t2:3 + t2],
                            bcol[t2][:, :], ALU.subtract).then_inc(
                                L.sem("Ve"), 1)
        L.bump("Ve")
        ph.v(fvc)
        ph.run()

        for cc in range(SS // 512):
            ph = Ph(L)
            L.wait(ph, "y", "St", L.cnt("St"))
            tg = L.load(ph, xt[0][:, :],
                        x_i[t2 * 128:(t2 + 1) * 128, cc * 512:(cc + 1) * 512],
                        queue="sync")

            def fsn(q, t2=t2, tg=tg):
                q.wait_ge(L.sem("Ld"), tg)
                q.activation(hnt[:, :], xt[0][:, :], AF.Identity,
                             scale=scol[t2][:, :], bias=bcol[t2][:, :])
            ph.s(fsn)
            ph.run()
            ph = Ph(L)
            L.store(ph, hn_o[t2 * 128:(t2 + 1) * 128, cc * 512:(cc + 1) * 512],
                    hnt[:, :])
            ph.run()
    loc.close()
    L.finish()
    return L


def build_l3attnB():
    L = Launch("l3attnB")
    hn_i = L.dram_in("hn", (DD, 2 * SS), BF16)
    wq = L.dram_in("wq", (1024, 128))
    wk = L.dram_in("wk", (1024, 128))
    wv = L.dram_in("wv", (1024, 128))
    bqkv = L.dram_in("bqkv", (3, 128))
    att_o = L.dram_out("att", (128, 2 * SS), BF16)
    L.emit_consts()
    T = 2 * SS
    qfm = L.sb("qfm", (128, T), BF16)
    kfm = L.sb("kfm", (128, T), BF16)
    att = L.sb("att3", (128, T), BF16)
    vtm = [L.sb("vtm", (128, 128), BF16) for _ in range(T // 128)]
    wq_sb = L.sb("wq", (128, 8 * 128), BF16)
    wk_sb = L.sb("wk", (128, 8 * 128), BF16)
    wv_sb = L.sb("wv", (128, 8 * 128), BF16)
    bq_sb = L.sb("bq", (1, 128), BF16)
    bk_sb = L.sb("bk", (1, 128), BF16)
    bv_sb = L.sb("bv", (1, 128), BF16)
    sqs = L.sb("sqs3", (128, 512), F32)
    kmx = L.sb("kmx", (1, 8), F32)
    cmx = L.sb("cmx", (1, 1), F32)
    nb512 = L.sb("nb512", (1, 512), F32)
    nbb512 = L.sb("nbb512", (1, 512), BF16)
    ph = Ph(L)
    L.load(ph, wq_sb[:, :].rearrange("p (k m) -> p k m", k=8),
           wq.rearrange("(k p) m -> p k m", p=128))
    L.load(ph, wk_sb[:, :].rearrange("p (k m) -> p k m", k=8),
           wk.rearrange("(k p) m -> p k m", p=128))
    L.load(ph, wv_sb[:, :].rearrange("p (k m) -> p k m", k=8),
           wv.rearrange("(k p) m -> p k m", p=128))
    L.load(ph, bq_sb[0:1, :], bqkv[0:1, :])
    L.load(ph, bk_sb[0:1, :], bqkv[1:2, :])
    t = L.load(ph, bv_sb[0:1, :], bqkv[2:3, :])
    ph.run()

    loc = contextlib.ExitStack()
    hnc = [L.sb("hnc", (128, 512), BF16, loc) for _ in range(16)]
    psq = [L.ps("psq", (128, 512), stack=loc) for _ in range(2)]
    psk = [L.ps("psk", (128, 512), stack=loc) for _ in range(2)]
    psv = [L.ps("psv", (128, 128), stack=loc) for _ in range(4)]
    for cc in range(T // 512):
        par = cc % 2
        ph = Ph(L)

        def fgl(q, cc=cc, par=par):
            if cc >= 2:
                q.wait_ge(L.sem("Pe"), L._b_pe[cc - 2])
        ph.g(fgl)
        tgs = [L.load(ph, hnc[par * 8 + f][:, :],
                      hn_i[f * 128:(f + 1) * 128, cc * 512:(cc + 1) * 512])
               for f in range(8)]

        def ftp(q, cc=cc, par=par, tg=tgs[-1], t=t):
            q.wait_ge(L.sem("Ld"), tg)
            if cc == 0:
                q.wait_ge(L.sem("Ld"), t)
            if cc >= 1:
                q.wait_ge(L.sem("Ac"), L._b_ac[cc - 1])
            for k in range(8):
                q.matmul(psq[par][:, :], wq_sb[:, k * 128:(k + 1) * 128],
                         hnc[par * 8 + k][:, :], start=(k == 0), stop=False)
            q.matmul(psq[par][:, :], bq_sb[0:1, :], L.ones_row_b[0:1, :],
                     start=False, stop=True)
            for k in range(8):
                q.matmul(psk[par][:, :], wk_sb[:, k * 128:(k + 1) * 128],
                         hnc[par * 8 + k][:, :], start=(k == 0), stop=False)
            mm = q.matmul(psk[par][:, :], bk_sb[0:1, :], L.ones_row_b[0:1, :],
                          start=False, stop=True)
            for j in range(4):
                for k in range(8):
                    q.matmul(psv[j][:, :],
                             hnc[par * 8 + k][:, j * 128:(j + 1) * 128],
                             wv_sb[:, k * 128:(k + 1) * 128],
                             start=(k == 0), stop=False)
                mm = q.matmul(psv[j][:, :], L.ones_row_b[0:1, 0:128],
                              bv_sb[0:1, :], start=False, stop=True)
                mm.then_inc(L.sem("Pe"), 1)
            q.nop().then_inc(L.sem("Pe"), 1)
        for _ in range(5):
            pe = L.bump("Pe")
        L._b_pe = getattr(L, "_b_pe", {})
        L._b_pe[cc] = pe
        ph.t(ftp)

        def fsp(q, cc=cc, par=par, pe=pe):
            q.wait_ge(L.sem("Pe"), pe)
            q.activation(qfm[:, cc * 512:(cc + 1) * 512], psq[par][:, :],
                         AF.Identity, scale=IM(0.125))
            q.activation(kfm[:, cc * 512:(cc + 1) * 512], psk[par][:, :],
                         AF.Copy)
            for j in range(4):
                q.activation(vtm[cc * 4 + j][:, :], psv[j][:, :],
                             AF.Copy).then_inc(L.sem("Ac"), 1)
        L._b_ac = getattr(L, "_b_ac", {})
        for _ in range(4):
            ac = L.bump("Ac")
        L._b_ac[cc] = ac
        ph.s(fsp)
        ph.run()
    loc.close()

    loc2 = contextlib.ExitStack()
    ps_r = L.ps("ps_r", (1, 512), stack=loc2)
    ph = Ph(L)

    def fz(q):
        q.memset(kmx[:, :], 0.0)
    ph.g(fz)
    ph.run()
    for h in range(2):
        base = h * 64
        for b in range(2):
            ph = Ph(L)
            for cc in range(SS // 512):
                c0 = b * SS + cc * 512

                def fsq2(q, c0=c0, base=base, cc=cc):
                    if cc >= 1:
                        q.wait_ge(L.sem("Pe"), L._n_pe[cc - 1])
                    q.activation(sqs[base:base + 64, :],
                                 kfm[base:base + 64, c0:c0 + 512],
                                 AF.Square).then_inc(L.sem("Ac"), 1)
                ac = L.bump("Ac")
                ph.s(fsq2)

                def ftn(q, cc=cc, ac=ac, base=base):
                    q.wait_ge(L.sem("Ac"), ac)
                    q.matmul(ps_r[:, :], L.ones_col_f[base:base + 64, :],
                             sqs[base:base + 64, :],
                             start=True, stop=True).then_inc(L.sem("Pe"), 1)
                L._n_pe = getattr(L, "_n_pe", {})
                pe = L.bump("Pe")
                L._n_pe[cc] = pe
                ph.t(ftn)

                def fvn(q, cc=cc, pe=pe, h=h, b=b):
                    q.wait_ge(L.sem("Pe"), pe)
                    q.reduce_max(cmx[:, :], ps_r[:, :], axis=AX.X)
                    L.vsync(q)
                    q.tensor_tensor(kmx[0:1, 2 * b + h:2 * b + h + 1],
                                    kmx[0:1, 2 * b + h:2 * b + h + 1],
                                    cmx[:, :], ALU.max)
                    L.vsync(q)
                    q.engine_nop().then_inc(L.sem("Ve"), 1)
                L.bump("Ve")
                ph.v(fvn)
            ph.run()
    loc2.close()

    loc3 = contextlib.ExitStack()
    ps_sT = [L.ps("ps_sT", (128, 512), stack=loc3) for _ in range(2)]
    ps_den = L.ps("ps_den", (1, 512), stack=loc3)
    ps_pv = L.ps("ps_pv", (64, 512), stack=loc3)
    ps_bc = L.ps("ps_bc", (64, 512), stack=loc3)
    ps_r0 = L.ps("ps_r0", (1, 512), stack=loc3)
    pT = [L.sb("pT3", (128, 512), BF16, loc3) for _ in range(2)]
    o32 = L.sb("o32", (64, 512), F32, loc3)
    denr = L.sb("denr", (1, 512), F32, loc3)
    denr2 = L.sb("denr2", (1, 512), F32, loc3)
    rcpb = L.sb("rcpb", (1, 512), BF16, loc3)
    for b in range(2):
        for h in range(2):
            base = h * 64
            for qg in range(SS // 512):
                q0 = b * SS + qg * 512
                ph = Ph(L)

                def fsqg(q, base=base, q0=q0):
                    q.activation(sqs[base:base + 64, :],
                                 qfm[base:base + 64, q0:q0 + 512],
                                 AF.Square).then_inc(L.sem("Ac"), 1)
                acg = L.bump("Ac")
                ph.s(fsqg)

                def ftg0(q, base=base, acg=acg):
                    q.wait_ge(L.sem("Ac"), acg)
                    q.matmul(ps_r0[:, :], L.ones_col_f[base:base + 64, :],
                             sqs[base:base + 64, :],
                             start=True, stop=True).then_inc(L.sem("Pe"), 1)
                peg = L.bump("Pe")
                ph.t(ftg0)

                def fvg0(q, peg=peg, b=b, h=h):
                    q.wait_ge(L.sem("Pe"), peg)
                    q.tensor_scalar(nb512[:, :], ps_r0[:, :],
                                    kmx[0:1, 2 * b + h:2 * b + h + 1], None,
                                    ALU.mult).then_inc(L.sem("Ve"), 1)
                veg = L.bump("Ve")
                ph.v(fvg0)

                def fsg0(q, veg=veg):
                    q.wait_ge(L.sem("Ve"), veg)
                    q.activation(nb512[:, :], nb512[:, :], AF.Sqrt)
                    q.activation(nbb512[:, :], nb512[:, :], AF.Identity,
                                 scale=IM(-1.0)).then_inc(L.sem("Ac"), 1)
                acg2 = L.bump("Ac")
                ph.s(fsg0)
                for kt in range(SS // 128):
                    kc = b * SS + kt * 128
                    sb = kt % 2

                    def ftf(q, base=base, q0=q0, kc=kc, sb=sb, kt=kt, h=h,
                            acg2=acg2):
                        if kt == 0:
                            q.wait_ge(L.sem("Ac"), acg2)
                        if kt >= 2:
                            q.wait_ge(L.sem("Ac"), L._f_ac[kt - 2])
                        q.matmul(ps_sT[sb][:, :],
                                 kfm[base:base + 64, kc:kc + 128],
                                 qfm[base:base + 64, q0:q0 + 512],
                                 start=True, stop=False)
                        q.matmul(ps_sT[sb][:, :], L.ones_r128_b[0:1, :],
                                 nbb512[0:1, :],
                                 start=False, stop=True).then_inc(L.sem("Pe"), 1)
                    pe = L.bump("Pe")
                    ph.t(ftf)

                    def fsf(q, sb=sb, kt=kt, pe=pe):
                        q.wait_ge(L.sem("Pe"), pe)
                        if kt >= 2:
                            q.wait_ge(L.sem("Pe"), L._f_pe2[kt - 2])
                        q.activation(pT[sb][:, :], ps_sT[sb][:, :],
                                     AF.Exp).then_inc(L.sem("Ac"), 1)
                    L._f_ac = getattr(L, "_f_ac", {})
                    L._f_ac[kt] = L.bump("Ac")
                    ph.s(fsf)

                    def ftf2(q, base=base, kc=kc, sb=sb, kt=kt, h=h,
                             ac=L._f_ac[kt], nkt=SS // 128):
                        q.wait_ge(L.sem("Ac"), ac)
                        q.matmul(ps_den[:, :], L.ones_col_b[:, :], pT[sb][:, :],
                                 start=(kt == 0), stop=(kt == nkt - 1))
                        vt_idx = kc // 128
                        q.matmul(ps_pv[:, :],
                                 vtm[vt_idx][:, base:base + 64], pT[sb][:, :],
                                 start=(kt == 0),
                                 stop=(kt == nkt - 1)).then_inc(L.sem("Pe"), 1)
                    L._f_pe2 = getattr(L, "_f_pe2", {})
                    L._f_pe2[kt] = L.bump("Pe")
                    ph.t(ftf2)
                pelast = L._f_pe2[SS // 128 - 1]

                def fvf(q, pelast=pelast):
                    q.wait_ge(L.sem("Pe"), pelast)
                    q.tensor_copy(denr2[:, :], ps_den[:, :])
                    L.vsync(q)
                    q.reciprocal(denr[:, :], denr2[:, :])
                    L.vsync(q)
                    q.engine_nop().then_inc(L.sem("Ve"), 1)
                ve = L.bump("Ve")
                ph.v(fvf)

                def fsf2(q, ve=ve):
                    q.wait_ge(L.sem("Ve"), ve)
                    q.activation(rcpb[:, :], denr[:, :], AF.Copy)
                    q.activation(o32[:, :], ps_pv[:, :],
                                 AF.Copy).then_inc(L.sem("Ac"), 1)
                ac3 = L.bump("Ac")
                ph.s(fsf2)

                def ftbc(q, ac3=ac3):
                    q.wait_ge(L.sem("Ac"), ac3)
                    q.matmul(ps_bc[:, :], L.ones_r128_b[0:1, 0:64],
                             rcpb[0:1, :], start=True, stop=True).then_inc(
                                 L.sem("Pe"), 1)
                pe4 = L.bump("Pe")
                ph.t(ftbc)

                def fvf2(q, base=base, q0=q0, pe4=pe4):
                    q.wait_ge(L.sem("Pe"), pe4)
                    q.tensor_tensor(att[base:base + 64, q0:q0 + 512],
                                    o32[:, :], ps_bc[:, :],
                                    ALU.mult).then_inc(L.sem("Ve"), 1)
                L.bump("Ve")
                ph.v(fvf2)
                ph.run()
    loc3.close()
    ph = Ph(L)
    L.store(ph, att_o[:, :], att[:, :])
    ph.run()
    L.finish()
    return L


def build_l3attnC():
    L = Launch("l3attnC")
    x_in = L.dram_in("x_in", (DD, TC))
    att_i = L.dram_in("att", (DD, TC), BF16)
    wo = L.dram_in("wo", (1024, 1024))
    bo = L.dram_in("bo", (1024,))
    x_out = L.dram_out("x_out", (DD, TC))
    L.emit_consts()
    CH = 512
    x_sb = [L.sb("x", (128, CH), F32) for _ in range(8)]
    x0f = [L.sb("x0f", (128, CH), F32) for _ in range(8)]
    dmy = [L.sb("dmy", (128, CH), BF16) for _ in range(8)]
    at = [L.sb("at", (128, CH), BF16) for _ in range(8)]
    sq = L.sb("sq", (128, 8 * 512), F32)
    for ch in range(TC // CH):
        c0 = ch * CH
        ph = Ph(L)
        L.wait(ph, "y", "St", L.cnt("St"))
        for f in range(8):
            L.load(ph, x_sb[f][:, :], x_in[f * 128:(f + 1) * 128, c0:c0 + CH],
                   queue="sync")
            L.load(ph, at[f][:, :], att_i[f * 128:(f + 1) * 128, c0:c0 + CH],
                   queue="gpsimd")
        L.wait(ph, "v", "Ld", L.cnt("Ld"))
        L.wait(ph, "t", "Ld", L.cnt("Ld"))
        ph.run()
        L.ln_window([x[:, :] for x in x_sb], CH, [d[:, :] for d in dmy],
                    sq[:, :], out_f32=[x[:, :] for x in x0f])
        ph = Ph(L)

        def fv(q):
            for f in range(8):
                q.tensor_tensor(x_sb[f][:, :], x_sb[f][:, :], x0f[f][:, :],
                                ALU.add)
        ph.v(fv)
        ph.run()

        def evo(mi, c, ps_ap, ph, pe):
            return L.ev_addx(ph, pe, ps_ap, x_sb[mi][:, c * 512:(c + 1) * 512],
                             1.0)
        L.proj(rhs_tiles=[a[:, :] for a in at], Tn=CH, K=1024, M=1024,
               w_dram=wo, bias_dram=bo, evac=evo, mblock=512)
        ph = Ph(L)
        for f in range(8):
            L.store(ph, x_out[f * 128:(f + 1) * 128, c0:c0 + CH], x_sb[f][:, :])
        ph.run()
    L.finish()
    return L



def build_l1attnA():
    L = Launch("l1attnA")
    HLO = 512
    TH = TC + HLO
    x_in = L.dram_in("x_in", (DD, TH))
    wq = L.dram_in("wq", (1024, 1024))
    wk = L.dram_in("wk", (1024, 1024))
    wv = L.dram_in("wv", (1024, 1024))
    wb = L.dram_in("wb", (1024, 8))
    cw_d = L.dram_in("cw", (128, 8 * 4 * 3))
    fw_d = L.dram_in("flw", (1024, 8))
    fb_d = L.dram_in("flb", (1, 8))
    w1 = L.dram_in("w1", (1024, 2048))
    b1 = L.dram_in("b1", (2048,))
    w2 = L.dram_in("w2", (2048, 32))
    b2 = L.dram_in("b2", (1, 32))
    q_o = L.dram_out("q_o", (1024, TC), BF16)
    k_o = L.dram_out("k_o", (1024, TC), BF16)
    v_o = L.dram_out("v_o", (1024, TC), BF16)
    beta_o = L.dram_out("beta_o", (8, TC))
    flr_o = L.dram_out("flr_o", (8, TC))
    fus_o = L.dram_out("fus_o", (32, TC))

    L.emit_consts()
    cw_sb = L.sb("cw", (128, 96), F32)
    ph = Ph(L)
    t = L.load(ph, cw_sb[:, :], cw_d)
    L.wait(ph, "v", "Ld", t)
    ph.run()

    xh = [L.sb("xh", (128, TH), BF16) for _ in range(8)]
    loc0 = contextlib.ExitStack()
    sq = L.sb("sq", (128, 8 * 512), F32, loc0)
    x_sc = [L.sb("xsc", (128, 512), F32, loc0) for _ in range(8)]
    for c0 in range(0, TH, 512):
        ph = Ph(L)
        L.wait(ph, "y", "St", L.cnt("St"))
        tg = [L.load(ph, x_sc[f][:, :], x_in[f * 128:(f + 1) * 128, c0:c0 + 512],
                     queue="sync") for f in range(8)]
        L.wait(ph, "v", "Ld", tg[-1])
        ph.run()
        L.ln_window([x[:, :] for x in x_sc], 512,
                    [h[:, c0:c0 + 512] for h in xh], sq[:, :])
    loc0.close()

    loc1 = contextlib.ExitStack()
    lin = [L.sb("lin", (128, TH), BF16, loc1) for _ in range(8)]
    cso = [L.sb("cso", (128, TC), BF16, loc1) for _ in range(8)]
    cvt = [L.sb("cvt", (128, 512), F32, loc1) for _ in range(2)]
    for wi_, out_d, cwoff in ((wq, q_o, 0), (wk, k_o, 32), (wv, v_o, 64)):
        def evl(mi, c, ps_ap, ph, pe):
            return L.ev_act(ph, pe, ps_ap,
                            lin[mi][:, c * 512:(c + 1) * 512], AF.Copy)
        L.proj(rhs_tiles=[h[:, 0:TH] for h in xh], Tn=TH, K=1024, M=1024,
               w_dram=wi_, evac=evl, mblock=512)
        ph = Ph(L)
        for f in range(8):
            for c in range(TC // 512):
                lo = HLO + c * 512
                ui = f * (TC // 512) + c
                par = ui % 2

                def fvc(q, f=f, c=c, lo=lo, par=par, cwoff=cwoff, ui=ui):
                    if ui >= 2:
                        q.wait_ge(L.sem("Ac"), L._cv_ac[ui - 2])
                    q.tensor_scalar(cvt[par][:, :], lin[f][:, lo - 3:lo + 509],
                                    cw_sb[:, cwoff + f * 4 + 0:cwoff + f * 4 + 1],
                                    None, ALU.mult)
                    for i in (1, 2, 3):
                        q.scalar_tensor_tensor(
                            cvt[par][:, :], lin[f][:, lo - 3 + i:lo - 3 + i + 512],
                            cw_sb[:, cwoff + f * 4 + i:cwoff + f * 4 + i + 1],
                            cvt[par][:, :], ALU.mult, ALU.add)
                    q.engine_nop().then_inc(L.sem("Ve"), 1)
                ve = L.bump("Ve")
                ph.v(fvc)

                def fsc(q, f=f, c=c, par=par, ve=ve):
                    q.wait_ge(L.sem("Ve"), ve)
                    q.activation(cso[f][:, c * 512:(c + 1) * 512],
                                 cvt[par][:, :], AF.Silu).then_inc(
                                     L.sem("Ac"), 1)
                L._cv_ac = getattr(L, "_cv_ac", {})
                L._cv_ac[ui] = L.bump("Ac")
                ph.s(fsc)
        ph.run()
        ph = Ph(L)
        L.wait(ph, "g", "St", L.cnt("St"))
        for f in range(8):
            L.store(ph, out_d[f * 128:(f + 1) * 128, :], cso[f][:, :],
                    queue="gpsimd")
        ph.run()

    loc1.close()
    loc2 = contextlib.ExitStack()
    ps8 = [L.ps("ps8", (8, 512), stack=loc2) for _ in range(2)]
    wb_sb = L.sb("wb", (128, 64), BF16, loc2)
    fw_sb = L.sb("fw", (128, 64), BF16, loc2)
    fb_sb = L.sb("fb", (1, 8), BF16, loc2)
    brow = L.sb("brow", (8, TC), F32, loc2)
    frow = L.sb("frow", (8, TC), F32, loc2)
    ph = Ph(L)
    L.load(ph, wb_sb[:, :].rearrange("p (k m) -> p k m", k=8),
           wb.rearrange("(k p) m -> p k m", p=128))
    L.load(ph, fw_sb[:, :].rearrange("p (k m) -> p k m", k=8),
           fw_d.rearrange("(k p) m -> p k m", p=128))
    t = L.load(ph, fb_sb[0:1, :], fb_d)
    ph.run()
    ph = Ph(L)
    for which, wsb, dst, useb in (("b", wb_sb, brow, False),
                                  ("f", fw_sb, frow, True)):
        for c in range(TC // 512):
            par = c % 2

            def ftb(q, which=which, wsb=wsb, c=c, par=par, useb=useb, t=t):
                if (which, c) == ("b", 0):
                    q.wait_ge(L.sem("Ld"), t)
                prev = L._bf_ac.get((which, c - 2)) if c >= 2 else \
                    (L._bf_ac.get(("b", TC // 512 - 2)) if which == "f" else None)
                if prev:
                    q.wait_ge(L.sem("Ac"), prev)
                mm = None
                for k in range(8):
                    mm = q.matmul(ps8[par][:, :], wsb[:, k * 8:(k + 1) * 8],
                                  xh[k][:, HLO + c * 512:HLO + (c + 1) * 512],
                                  start=(k == 0), stop=(k == 7) and not useb)
                if useb:
                    mm = q.matmul(ps8[par][:, :], fb_sb[0:1, :],
                                  L.ones_row_b[0:1, :], start=False, stop=True)
                mm.then_inc(L.sem("Pe"), 1)
            pe = L.bump("Pe")
            ph.t(ftb)

            def fsb(q, c=c, par=par, dst=dst, pe=pe):
                q.wait_ge(L.sem("Pe"), pe)
                q.activation(dst[:, c * 512:(c + 1) * 512], ps8[par][:, :],
                             AF.Sigmoid).then_inc(L.sem("Ac"), 1)
            L._bf_ac = getattr(L, "_bf_ac", {})
            L._bf_ac[(which, c)] = L.bump("Ac")
            ph.s(fsb)
    ph.run()
    ph = Ph(L)
    L.store(ph, beta_o[:, :], brow[:, :])
    L.store(ph, flr_o[:, :], frow[:, :])
    ph.run()
    loc2.close()

    loc3 = contextlib.ExitStack()
    g1 = [L.sb("g1", (128, 512), BF16, loc3) for _ in range(16)]
    w2_sb = L.sb("w2", (128, 16 * 32), BF16, loc3)
    b2_sb = L.sb("b2", (1, 32), BF16, loc3)
    fusr = L.sb("fusr", (32, TC), F32, loc3)
    ps32 = [L.ps("ps32", (32, 512), stack=loc3) for _ in range(2)]
    ph = Ph(L)
    L.load(ph, w2_sb[:, :].rearrange("p (k m) -> p k m", k=16),
           w2.rearrange("(k p) m -> p k m", p=128))
    t2 = L.load(ph, b2_sb[0:1, :], b2)
    ph.run()
    for c in range(TC // 512):
        def evg(mi, cc, ps_ap, ph, pe):
            return L.ev_act(ph, pe, ps_ap, g1[mi][:, :], AF.Gelu)
        L.proj(rhs_tiles=[h[:, HLO + c * 512:HLO + (c + 1) * 512] for h in xh],
               Tn=512, K=1024, M=2048, w_dram=w1, bias_dram=b1, evac=evg,
               mblock=512)
        ph = Ph(L)
        par = c % 2

        def ftf(q, c=c, par=par, t2=t2):
            if c == 0:
                q.wait_ge(L.sem("Ld"), t2)
            if c >= 2:
                q.wait_ge(L.sem("Ac"), L._fu_ac[c - 2])
            mm = None
            for k in range(16):
                mm = q.matmul(ps32[par][:, :], w2_sb[:, k * 32:(k + 1) * 32],
                              g1[k][:, :], start=(k == 0), stop=False)
            mm = q.matmul(ps32[par][:, :], b2_sb[0:1, :], L.ones_row_b[0:1, :],
                          start=False, stop=True)
            mm.then_inc(L.sem("Pe"), 1)
        pe = L.bump("Pe")
        ph.t(ftf)

        def fsf(q, c=c, par=par, pe=pe):
            q.wait_ge(L.sem("Pe"), pe)
            q.activation(fusr[:, c * 512:(c + 1) * 512], ps32[par][:, :],
                         AF.Copy).then_inc(L.sem("Ac"), 1)
        L._fu_ac = getattr(L, "_fu_ac", {})
        L._fu_ac[c] = L.bump("Ac")
        ph.s(fsf)
        ph.run()
    ph = Ph(L)
    L.store(ph, fus_o[:, :], fusr[:, :])
    ph.run()
    loc3.close()
    L.finish()
    return L


def build_l1attnC():
    L = Launch("l1attnC")
    x_in = L.dram_in("x_in", (DD, TC))
    o_i = L.dram_in("o_i", (DD, TC), BF16)
    wo = L.dram_in("wo", (1024, 1024))
    x_out = L.dram_out("x_out", (DD, TC))
    L.emit_consts()
    CH = 512
    x_sb = [L.sb("x", (128, CH), F32) for _ in range(8)]
    ot = [L.sb("ot", (128, CH), BF16) for _ in range(8)]
    for ch in range(TC // CH):
        c0 = ch * CH
        ph = Ph(L)
        L.wait(ph, "y", "St", L.cnt("St"))
        for f in range(8):
            L.load(ph, x_sb[f][:, :], x_in[f * 128:(f + 1) * 128, c0:c0 + CH],
                   queue="sync")
            L.load(ph, ot[f][:, :], o_i[f * 128:(f + 1) * 128, c0:c0 + CH],
                   queue="gpsimd")
        L.wait(ph, "v", "Ld", L.cnt("Ld"))
        L.wait(ph, "t", "Ld", L.cnt("Ld"))
        ph.run()

        def evo(mi, c, ps_ap, ph, pe):
            return L.ev_addx(ph, pe, ps_ap, x_sb[mi][:, c * 512:(c + 1) * 512],
                             1.0)
        L.proj(rhs_tiles=[a[:, :] for a in ot], Tn=CH, K=1024, M=1024,
               w_dram=wo, evac=evo, mblock=512)
        ph = Ph(L)
        for f in range(8):
            L.store(ph, x_out[f * 128:(f + 1) * 128, c0:c0 + CH], x_sb[f][:, :])
        ph.run()
    L.finish()
    return L



def build_l1attnB():
    L = Launch("l1attnB")
    T = 2 * SS
    PW = 2048
    NCHK = PW // 32
    q_i = L.dram_in("q_i", (128, T), BF16)
    k_i = L.dram_in("k_i", (128, T), BF16)
    v_i = L.dram_in("v_i", (128, T), BF16)
    b_i = L.dram_in("b_i", (1, T))
    f0_i = L.dram_in("f0", (1, T))
    f1_i = L.dram_in("f1", (1, T))
    f2_i = L.dram_in("f2", (1, T))
    f3_i = L.dram_in("f3", (1, T))
    fl_i = L.dram_in("fl", (1, T))
    fw_i = L.dram_in("fw", (128, 34))
    nw_i = L.dram_in("nw", (128, 2))
    mp_i = L.dram_in("MPAIR", (64, 64))
    mpl_i = L.dram_in("MPAIRL", (64, 64))
    ml_i = L.dram_in("MLOC", (32, 32))
    ey_i = L.dram_in("EYE64", (64, 64))
    o_o = L.dram_out("o_o", (128, T), BF16)

    L.emit_consts()
    fw_sb = L.sb("fw", (128, 34), F32)
    nw_sb = L.sb("nw", (128, 2), F32)
    mp_sb = L.sb("mp", (64, 64), F32)
    mpl_sb = L.sb("mpl", (64, 64), F32)
    ml_sb = L.sb("ml", (32, 32), F32)
    ey_sb = L.sb("ey", (64, 64), F32)
    S32 = L.sb("S32", (128, 128), F32)
    Sbf = L.sb("Sbf", (128, 128), BF16)
    vhist = L.sb("vhist", (128, 30), BF16)
    ph = Ph(L)
    L.load(ph, fw_sb[:, :], fw_i)
    L.load(ph, nw_sb[:, :], nw_i, queue="gpsimd")
    L.load(ph, mp_sb[:, :], mp_i, queue="gpsimd")
    L.load(ph, mpl_sb[:, :], mpl_i, queue="gpsimd")
    L.load(ph, ml_sb[:, :], ml_i, queue="gpsimd")
    t0 = L.load(ph, ey_sb[:, :], ey_i, queue="gpsimd")
    L.wait(ph, "v", "Ld", t0)
    L.wait(ph, "s", "Ld", t0)
    ph.run()

    qn = L.sb("qn", (128, PW), BF16)
    kn = L.sb("kn", (128, PW), BF16)
    kb = L.sb("kb", (128, PW), BF16)
    vf = L.sb("vf", (128, PW), BF16)
    vext = L.sb("vext", (128, PW + 30), BF16)
    kn_tm = [L.sb("kntm", (64, 128), BF16) for _ in range(PW // 64)]
    kb_tm = [L.sb("kbtm", (64, 128), BF16) for _ in range(PW // 64)]
    vb_tm = [L.sb("vbtm", (64, 128), BF16) for _ in range(PW // 64)]
    attnT = [L.sb("attnT", (64, 64), BF16) for _ in range(PW // 64)]
    o_pair = [L.sb("opair", (64, 128), BF16) for _ in range(PW // 64)]
    o_fm = L.sb("ofm", (128, PW), BF16)
    ls_fm = L.sb("lsfm", (128, PW), BF16)
    ll_fm = L.sb("llfm", (128, PW), BF16)
    rowA = L.sb("rowA", (1, PW), F32)
    rowB = L.sb("rowB", (1, PW), F32)
    rowC = L.sb("rowC", (1, PW), F32)
    rowD = L.sb("rowD", (1, PW), F32)
    bbf = L.sb("bbf", (1, PW), BF16)
    ctxb = [L.sb("ctxb", (1, PW), BF16) for _ in range(4)]
    srow = L.sb("srow", (1, 512), F32)
    sqB = L.sb("sqB", (128, 512), F32)
    un64 = L.sb("un64", (64, 128), BF16)
    u64 = L.sb("u64", (64, 128), BF16)
    w_sb = [L.sb("w", (128, 32), BF16) for _ in range(2)]
    al64 = L.sb("al64", (64, 64), BF16)
    ppair = [L.sb("ppair", (64, 64), BF16) for _ in range(2)]
    qpair = [L.sb("qpair", (64, 64), BF16) for _ in range(2)]
    rpair = [L.sb("rpair", (64, 64), BF16) for _ in range(2)]
    rtpair = [L.sb("rtpair", (64, 64), BF16) for _ in range(2)]

    for p in range(8):
        b, seg = p // 4, p % 4
        c0g = p * PW

        ph = Ph(L)
        L.wait(ph, "y", "St", L.cnt("St"))
        L.load(ph, qn[:, :], q_i[:, c0g:c0g + PW], queue="sync")
        L.load(ph, kn[:, :], k_i[:, c0g:c0g + PW], queue="sync")
        L.load(ph, vf[:, :], v_i[:, c0g:c0g + PW], queue="sync")
        L.load(ph, bbf[0:1, :], b_i[0:1, c0g:c0g + PW], queue="gpsimd")
        for j, fi in enumerate((f0_i, f1_i, f2_i)):
            L.load(ph, rowA[0:1, :] if j == 0 else
                   (rowB[0:1, :] if j == 1 else rowC[0:1, :]),
                   fi[0:1, c0g:c0g + PW], queue="gpsimd")
        L.load(ph, rowD[0:1, :], f3_i[0:1, c0g:c0g + PW], queue="gpsimd")
        tfl = L.load(ph, srow[0:1, 0:512], fl_i[0:1, c0g:c0g + 512],
                     queue="gpsimd") if False else None
        L.wait(ph, "v", "Ld", L.cnt("Ld"))
        L.wait(ph, "s", "Ld", L.cnt("Ld"))
        L.wait(ph, "t", "Ld", L.cnt("Ld"))
        ph.run()

        locg = contextlib.ExitStack()
        flr = L.sb("flr", (1, PW), F32, locg)
        e0 = L.sb("e0", (1, PW), F32, locg)
        e1 = L.sb("e1", (1, PW), F32, locg)
        e2 = L.sb("e2", (1, PW), F32, locg)
        den = L.sb("den", (1, PW), F32, locg)
        pvr = L.sb("pvr", (1, PW), F32, locg)
        ph = Ph(L)
        tfl = L.load(ph, flr[0:1, :], fl_i[0:1, c0g:c0g + PW], queue="gpsimd")

        def fsg(q, tfl=tfl):
            q.wait_ge(L.sem("Ld"), tfl)
            q.activation(e0[0:1, :], rowA[0:1, :], AF.Exp)
            q.activation(e1[0:1, :], rowB[0:1, :], AF.Exp)
            q.activation(e2[0:1, :], rowC[0:1, :], AF.Exp)
            q.activation(pvr[0:1, :], rowD[0:1, :],
                         AF.Sigmoid).then_inc(L.sem("Ac"), 1)
        ac = L.bump("Ac")
        ph.s(fsg)

        def fvg(q, ac=ac):
            q.wait_ge(L.sem("Ac"), ac)
            q.tensor_tensor(den[0:1, :], e0[0:1, :], e1[0:1, :], ALU.add)
            L.vsync(q)
            q.tensor_tensor(den[0:1, :], den[0:1, :], e2[0:1, :], ALU.add)
            L.vsync(q)
            q.reciprocal(rowA[0:1, :], den[0:1, :])
            L.vsync(q)
            q.tensor_scalar(rowB[0:1, :], flr[0:1, :], -0.2, 1.0,
                            ALU.mult, ALU.add)
            L.vsync(q)
            q.tensor_tensor(pvr[0:1, :], pvr[0:1, :], rowB[0:1, :], ALU.mult)
            L.vsync(q)
            q.tensor_scalar(rowB[0:1, :], pvr[0:1, :], -1.0, 1.0,
                            ALU.mult, ALU.add)
            L.vsync(q)
            q.tensor_tensor(rowC[0:1, :], rowB[0:1, :], rowA[0:1, :], ALU.mult)
            L.vsync(q)
            q.tensor_tensor(ctxb[0][0:1, :], e0[0:1, :], rowC[0:1, :], ALU.mult)
            q.tensor_tensor(ctxb[1][0:1, :], e1[0:1, :], rowC[0:1, :], ALU.mult)
            q.tensor_tensor(ctxb[2][0:1, :], e2[0:1, :], rowC[0:1, :], ALU.mult)
            q.tensor_copy(ctxb[3][0:1, :], pvr[0:1, :]).then_inc(L.sem("Ve"), 1)
        L.bump("Ve")
        ph.v(fvg)
        ph.run()
        locg.close()

        locn = contextlib.ExitStack()
        ps_r = L.ps("ps_r", (1, 512), stack=locn)
        ps_b = [L.ps("ps_b", (128, 512), stack=locn) for _ in range(2)]
        ps_t = [L.ps("ps_tB", (64, 128), BF16, stack=locn) for _ in range(2)]
        vbt = L.sb("vbt", (128, 512), BF16, locn)
        for cc in range(PW // 512):
            sl = slice(cc * 512, (cc + 1) * 512)
            for which, arr in (("q", qn), ("k", kn)):
                ph = Ph(L)

                def fsn(q, arr=arr, sl=sl):
                    q.activation(sqB[:, :], arr[:, sl],
                                 AF.Square).then_inc(L.sem("Ac"), 1)
                ac = L.bump("Ac")
                ph.s(fsn)

                def ftn(q, ac=ac):
                    q.wait_ge(L.sem("Ac"), ac)
                    q.matmul(ps_r[:, :], L.ones_col_f[:, :], sqB[:, :],
                             start=True, stop=True).then_inc(L.sem("Pe"), 1)
                pe = L.bump("Pe")
                ph.t(ftn)

                def fvn(q, pe=pe):
                    q.wait_ge(L.sem("Pe"), pe)
                    q.tensor_copy(srow[0:1, :], ps_r[:, :])
                    L.vsync(q)
                    q.engine_nop().then_inc(L.sem("Ve"), 1)
                ve = L.bump("Ve")
                ph.v(fvn)

                def fsn2(q, ve=ve):
                    q.wait_ge(L.sem("Ve"), ve)
                    q.activation(srow[0:1, :], srow[0:1, :], AF.Ln,
                                 bias=IM(1e-6))
                    q.activation(srow[0:1, :], srow[0:1, :], AF.Exp,
                                 scale=IM(-0.5)).then_inc(L.sem("Ac"), 1)
                ac2 = L.bump("Ac")
                ph.s(fsn2)

                def ftn2(q, ac2=ac2):
                    q.wait_ge(L.sem("Ac"), ac2)
                    q.matmul(ps_b[0][:, :], L.ones_r128_f[:, :], srow[0:1, :],
                             start=True, stop=True).then_inc(L.sem("Pe"), 1)
                pe2 = L.bump("Pe")
                ph.t(ftn2)

                def fvn2(q, arr=arr, sl=sl, pe2=pe2):
                    q.wait_ge(L.sem("Pe"), pe2)
                    q.tensor_tensor(arr[:, sl], arr[:, sl], ps_b[0][:, :],
                                    ALU.mult).then_inc(L.sem("Ve"), 1)
                L.bump("Ve")
                ph.v(fvn2)
                ph.run()
            ph = Ph(L)

            def ftb(q, sl=sl, cc=cc):
                q.matmul(ps_b[1][:, :], L.ones_r128_b[0:1, :],
                         bbf[0:1, sl], start=True, stop=True).then_inc(
                             L.sem("Pe"), 1)
            pe3 = L.bump("Pe")
            ph.t(ftb)

            def fvb(q, sl=sl, pe3=pe3):
                q.wait_ge(L.sem("Pe"), pe3)
                q.tensor_tensor(kb[:, sl], kn[:, sl], ps_b[1][:, :], ALU.mult)
                q.tensor_tensor(vbt[:, :], vf[:, sl], ps_b[1][:, :],
                                ALU.mult).then_inc(L.sem("Ve"), 1)
            veb = L.bump("Ve")
            ph.v(fvb)

            for pp in range(8):
                gp = cc * 8 + pp
                csl = slice(cc * 512 + pp * 64, cc * 512 + (pp + 1) * 64)

                def ftt(q, gp=gp, csl=csl, pp=pp, veb=veb):
                    if pp == 0:
                        q.wait_ge(L.sem("Ve"), veb)
                    if gp % 8 >= 2 or True:
                        prev = L._t_ve.get(gp - 1)
                        if prev:
                            q.wait_ge(L.sem("Ve"), prev)
                    q.transpose(ps_t[0][:, :], kn[:, csl], L.ident_b[:, :])
                    q.transpose(ps_t[1][:, :], kb[:, csl],
                                L.ident_b[:, :]).then_inc(L.sem("Pe"), 1)
                pe4 = L.bump("Pe")
                ph.t(ftt)

                def fvt(q, gp=gp, pe4=pe4):
                    q.wait_ge(L.sem("Pe"), pe4)
                    q.tensor_copy(kn_tm[gp][:, :], ps_t[0][:, :])
                    q.tensor_copy(kb_tm[gp][:, :], ps_t[1][:, :]).then_inc(
                        L.sem("Ve"), 1)
                L._t_ve = getattr(L, "_t_ve", {})
                L._t_ve[gp] = L.bump("Ve")
                ph.v(fvt)

                def ftt2(q, gp=gp, pp=pp, veT=L._t_ve[gp]):
                    q.wait_ge(L.sem("Ve"), veT)
                    q.transpose(ps_t[0][:, :], vbt[:, pp * 64:(pp + 1) * 64],
                                L.ident_b[:, :]).then_inc(L.sem("Pe"), 1)
                pe5 = L.bump("Pe")
                ph.t(ftt2)

                def fvt2(q, gp=gp, pe5=pe5):
                    q.wait_ge(L.sem("Pe"), pe5)
                    q.tensor_copy(vb_tm[gp][:, :], ps_t[0][:, :]).then_inc(
                        L.sem("Ve"), 1)
                L._t_ve[gp] = L.bump("Ve")
                ph.v(fvt2)
            ph.run()
        locn.close()

        loci = contextlib.ExitStack()
        psi = [L.ps("psi", (64, 64), stack=loci) for _ in range(4)]
        for gp in range(PW // 64):
            psl = slice(gp * 64, (gp + 1) * 64)
            ph = Ph(L)

            def fti(q, psl=psl, gp=gp):
                q.matmul(psi[0][:, :], kn[:, psl], kb[:, psl],
                         start=True, stop=True)
                q.matmul(psi[1][:, :], kb[:, psl], kn[:, psl],
                         start=True, stop=True).then_inc(L.sem("Pe"), 1)
            pe = L.bump("Pe")
            ph.t(fti)

            def fvi(q, pe=pe):
                q.wait_ge(L.sem("Pe"), pe)
                q.tensor_tensor(ppair[0][:, :], psi[0][:, :], mp_sb[:, :],
                                ALU.mult)
                q.tensor_tensor(qpair[0][:, :], psi[1][:, :], mpl_sb[:, :],
                                ALU.mult)
                L.vsync(q)
                q.tensor_tensor(rpair[0][:, :], ppair[0][:, :], ey_sb[:, :],
                                ALU.add)
                q.tensor_tensor(rtpair[0][:, :], qpair[0][:, :], ey_sb[:, :],
                                ALU.add).then_inc(L.sem("Ve"), 1)
            ve = L.bump("Ve")
            ph.v(fvi)
            last_ve = ve
            for lev in range(4):
                def ftl(q, lev=lev, last_ve=last_ve):
                    q.wait_ge(L.sem("Ve"), last_ve)
                    q.matmul(psi[0][:, :], qpair[lev % 2][:, :],
                             ppair[lev % 2][:, :], start=True, stop=True)
                    q.matmul(psi[1][:, :], ppair[lev % 2][:, :],
                             qpair[lev % 2][:, :], start=True,
                             stop=True).then_inc(L.sem("Pe"), 1)
                pe = L.bump("Pe")
                ph.t(ftl)

                def fvl(q, lev=lev, pe=pe):
                    q.wait_ge(L.sem("Pe"), pe)
                    q.tensor_copy(ppair[(lev + 1) % 2][:, :], psi[0][:, :])
                    q.tensor_copy(qpair[(lev + 1) % 2][:, :],
                                  psi[1][:, :]).then_inc(L.sem("Ve"), 1)
                ve2 = L.bump("Ve")
                ph.v(fvl)

                def ftl2(q, lev=lev, ve2=ve2):
                    q.wait_ge(L.sem("Ve"), ve2)
                    q.matmul(psi[2][:, :], rtpair[lev % 2][:, :],
                             ppair[(lev + 1) % 2][:, :], start=True, stop=True)
                    q.matmul(psi[3][:, :], ppair[(lev + 1) % 2][:, :],
                             rtpair[lev % 2][:, :], start=True,
                             stop=True).then_inc(L.sem("Pe"), 1)
                pe2 = L.bump("Pe")
                ph.t(ftl2)

                def fvl2(q, lev=lev, pe2=pe2):
                    q.wait_ge(L.sem("Pe"), pe2)
                    q.tensor_tensor(rpair[(lev + 1) % 2][:, :],
                                    rpair[lev % 2][:, :], psi[2][:, :], ALU.add)
                    q.tensor_tensor(rtpair[(lev + 1) % 2][:, :],
                                    rtpair[lev % 2][:, :], psi[3][:, :],
                                    ALU.add).then_inc(L.sem("Ve"), 1)
                last_ve = L.bump("Ve")
                ph.v(fvl2)

            def fvfin(q, gp=gp, last_ve=last_ve):
                q.wait_ge(L.sem("Ve"), last_ve)
                q.tensor_copy(attnT[gp][:, :], rpair[0][:, :]).then_inc(
                    L.sem("Ve"), 1)
            L.bump("Ve")
            ph.v(fvfin)
            ph.run()
        loci.close()

        locs = contextlib.ExitStack()
        psU = L.ps("psU", (32, 128), stack=locs)
        psW = L.ps("psW", (128, 32), stack=locs)
        psWS = L.ps("psWS", (32, 128), stack=locs)
        psAL = L.ps("psAL", (32, 32), stack=locs)
        psO = L.ps("psO", (32, 128), stack=locs)
        psSU = L.ps("psSU", (128, 128), stack=locs)
        if seg == 0:
            ph = Ph(L)

            def fz(q):
                q.memset(S32[:, :], 0.0)
                q.memset(Sbf[:, :], 0.0)
            ph.g(fz)
            ph.run()
        ph = Ph(L)
        for c in range(NCHK):
            gp, s = c // 2, c % 2
            bs = 32 * s
            fsl = slice(c * 32, (c + 1) * 32)

            def ft1(q, c=c, gp=gp, bs=bs, fsl=fsl):
                if c >= 1:
                    q.wait_ge(L.sem("Ve"), L._s_ve[c - 1])
                q.matmul(psU[:, :], attnT[gp][bs:bs + 32, bs:bs + 32],
                         vb_tm[gp][bs:bs + 32, :], start=True, stop=True)
                q.matmul(psW[:, :], kb_tm[gp][bs:bs + 32, :],
                         attnT[gp][bs:bs + 32, bs:bs + 32],
                         start=True, stop=True)
                q.matmul(psAL[:, :], kn[:, fsl], qn[:, fsl],
                         start=True, stop=True).then_inc(L.sem("Pe"), 1)
            pe1 = L.bump("Pe")
            ph.t(ft1)

            def fv1(q, c=c, bs=bs, pe1=pe1):
                q.wait_ge(L.sem("Pe"), pe1)
                q.tensor_copy(u64[bs:bs + 32, :], psU[:, :])
                q.tensor_copy(w_sb[c % 2][:, :], psW[:, :])
                q.tensor_tensor(al64[bs:bs + 32, bs:bs + 32], psAL[:, :],
                                ml_sb[:, :], ALU.mult).then_inc(L.sem("Ve"), 1)
            ve1 = L.bump("Ve")
            ph.v(fv1)

            def ft2(q, c=c, ve1=ve1):
                q.wait_ge(L.sem("Ve"), ve1)
                q.matmul(psWS[:, :], w_sb[c % 2][:, :], Sbf[:, :],
                         start=True, stop=True).then_inc(L.sem("Pe"), 1)
            pe2 = L.bump("Pe")
            ph.t(ft2)

            def fv2(q, c=c, bs=bs, pe2=pe2):
                q.wait_ge(L.sem("Pe"), pe2)
                q.tensor_tensor(un64[bs:bs + 32, :], u64[bs:bs + 32, :],
                                psWS[:, :], ALU.subtract).then_inc(
                                    L.sem("Ve"), 1)
            ve2 = L.bump("Ve")
            ph.v(fv2)

            def ft3(q, c=c, gp=gp, bs=bs, fsl=fsl, ve2=ve2):
                q.wait_ge(L.sem("Ve"), ve2)
                q.matmul(psO[:, :], qn[:, fsl], Sbf[:, :],
                         start=True, stop=False)
                q.matmul(psO[:, :], al64[bs:bs + 32, bs:bs + 32],
                         un64[bs:bs + 32, :], start=False, stop=True)
                q.matmul(psSU[:, :], kn_tm[gp][bs:bs + 32, :],
                         un64[bs:bs + 32, :], start=True,
                         stop=True).then_inc(L.sem("Pe"), 1)
            pe3 = L.bump("Pe")
            ph.t(ft3)

            def fv3(q, c=c, gp=gp, bs=bs, pe3=pe3):
                q.wait_ge(L.sem("Pe"), pe3)
                q.tensor_tensor(S32[:, :], S32[:, :], psSU[:, :], ALU.add)
                L.vsync(q)
                q.tensor_copy(Sbf[:, :], S32[:, :]).then_inc(L.sem("Ve"), 1)
            ve3 = L.bump("Ve")
            ph.v(fv3)

            def fs3(q, c=c, gp=gp, bs=bs, pe3=pe3):
                q.wait_ge(L.sem("Pe"), pe3)
                q.activation(o_pair[gp][bs:bs + 32, :], psO[:, :],
                             AF.Copy).then_inc(L.sem("Ac"), 1)
            acS = L.bump("Ac")
            ph.s(fs3)
            L._s_ve = getattr(L, "_s_ve", {})
            L._s_ve[c] = ve3
        ph.run()
        locs.close()

        locf = contextlib.ExitStack()
        ph = Ph(L)

        def fvx(q, seg=seg):
            if seg == 0:
                q.memset(vext[:, 0:30], 0.0)
            else:
                q.tensor_copy(vext[:, 0:30], vhist[:, :])
            q.tensor_copy(vext[:, 30:30 + PW], vf[:, :])
            L.vsync(q)
            q.tensor_copy(vhist[:, :], vf[:, PW - 30:PW])
            L.vsync(q)
            for cc in range(PW // 512):
                ob = cc * 512
                q.tensor_scalar(ls_fm[:, ob:ob + 512],
                                vext[:, ob + 30:ob + 542], fw_sb[:, 2:3],
                                None, ALU.mult)
                for i in (1, 2):
                    q.scalar_tensor_tensor(
                        ls_fm[:, ob:ob + 512], vext[:, ob + 30 - i:ob + 542 - i],
                        fw_sb[:, 2 - i:3 - i], ls_fm[:, ob:ob + 512],
                        ALU.mult, ALU.add)
                q.tensor_scalar(ll_fm[:, ob:ob + 512],
                                vext[:, ob + 30:ob + 542], fw_sb[:, 33:34],
                                None, ALU.mult)
                for i in range(1, 31):
                    q.scalar_tensor_tensor(
                        ll_fm[:, ob:ob + 512], vext[:, ob + 30 - i:ob + 542 - i],
                        fw_sb[:, 33 - i:34 - i], ll_fm[:, ob:ob + 512],
                        ALU.mult, ALU.add)
            q.engine_nop().then_inc(L.sem("Ve"), 1)
        vef = L.bump("Ve")
        ph.v(fvx)
        ph.run()

        ps_t2 = [L.ps("ps_t2", (128, 64), BF16, stack=locf) for _ in range(2)]
        ph = Ph(L)
        for gp in range(PW // 64):
            def fto(q, gp=gp):
                if gp >= 2:
                    q.wait_ge(L.sem("Ve"), L._o_ve[gp - 2])
                q.transpose(ps_t2[gp % 2][:, :], o_pair[gp][:, :],
                            L.ident_b[0:64, 0:64]).then_inc(L.sem("Pe"), 1)
            pe = L.bump("Pe")
            ph.t(fto)

            def fvo(q, gp=gp, pe=pe):
                q.wait_ge(L.sem("Pe"), pe)
                q.tensor_copy(o_fm[:, gp * 64:(gp + 1) * 64],
                              ps_t2[gp % 2][:, :]).then_inc(L.sem("Ve"), 1)
            L._o_ve = getattr(L, "_o_ve", {})
            L._o_ve[gp] = L.bump("Ve")
            ph.v(fvo)
        ph.run()

        ps_g = [L.ps("ps_g", (128, 512), stack=locf) for _ in range(4)]
        ps_rr = L.ps("ps_rr", (1, 512), stack=locf)
        ps_bb = L.ps("ps_bb", (128, 512), stack=locf)
        cmb = L.sb("cmb", (128, 512), F32, locf)
        for cc in range(PW // 512):
            sl = slice(cc * 512, (cc + 1) * 512)
            ph = Ph(L)

            def ftc(q, sl=sl):
                mm = None
                for j in range(4):
                    mm = q.matmul(ps_g[j][:, :], L.ones_r128_b[0:1, :],
                                  ctxb[j][0:1, sl], start=True, stop=True)
                mm.then_inc(L.sem("Pe"), 1)
            pe = L.bump("Pe")
            ph.t(ftc)

            def fvc(q, sl=sl, cc=cc, pe=pe):
                q.wait_ge(L.sem("Pe"), pe)
                q.tensor_tensor(cmb[:, :], ls_fm[:, sl], ps_g[0][:, :],
                                ALU.mult)
                q.tensor_tensor(sqB[:, :], ll_fm[:, sl], ps_g[1][:, :],
                                ALU.mult)
                q.tensor_tensor(cmb[:, :], cmb[:, :], sqB[:, :], ALU.add)
                q.tensor_tensor(sqB[:, :], o_fm[:, sl], ps_g[2][:, :], ALU.mult)
                q.tensor_tensor(cmb[:, :], cmb[:, :], sqB[:, :], ALU.add)
                q.tensor_tensor(sqB[:, :], vf[:, sl], ps_g[3][:, :], ALU.mult)
                q.tensor_tensor(cmb[:, :], cmb[:, :], sqB[:, :],
                                ALU.add).then_inc(L.sem("Ve"), 1)
            ve = L.bump("Ve")
            ph.v(fvc)
            for r in range(2):
                def fsr(q, r=r, ve=ve):
                    if r == 0:
                        q.wait_ge(L.sem("Ve"), ve)
                    else:
                        q.wait_ge(L.sem("Ve"), L._r_ve[0])
                    q.activation(sqB[:, :], cmb[:, :],
                                 AF.Square).then_inc(L.sem("Ac"), 1)
                acr = L.bump("Ac")
                ph.s(fsr)

                def ftr(q, acr=acr):
                    q.wait_ge(L.sem("Ac"), acr)
                    q.matmul(ps_rr[:, :], L.ones_col_f[:, :], sqB[:, :],
                             start=True, stop=True).then_inc(L.sem("Pe"), 1)
                per = L.bump("Pe")
                ph.t(ftr)

                def fvr(q, per=per):
                    q.wait_ge(L.sem("Pe"), per)
                    q.tensor_copy(srow[0:1, :], ps_rr[:, :])
                    L.vsync(q)
                    q.engine_nop().then_inc(L.sem("Ve"), 1)
                ver = L.bump("Ve")
                ph.v(fvr)

                def fsr2(q, ver=ver):
                    q.wait_ge(L.sem("Ve"), ver)
                    q.activation(srow[0:1, :], srow[0:1, :], AF.Ln,
                                 bias=IM(EPS_LN), scale=IM(1.0 / 128.0))
                    q.activation(srow[0:1, :], srow[0:1, :], AF.Exp,
                                 scale=IM(-0.5)).then_inc(L.sem("Ac"), 1)
                acr2 = L.bump("Ac")
                ph.s(fsr2)

                def ftr2(q, acr2=acr2):
                    q.wait_ge(L.sem("Ac"), acr2)
                    q.matmul(ps_bb[:, :], L.ones_r128_f[:, :], srow[0:1, :],
                             start=True, stop=True).then_inc(L.sem("Pe"), 1)
                per2 = L.bump("Pe")
                ph.t(ftr2)

                def fvr2(q, r=r, per2=per2):
                    q.wait_ge(L.sem("Pe"), per2)
                    q.tensor_tensor(cmb[:, :], cmb[:, :], ps_bb[:, :],
                                    ALU.mult).then_inc(L.sem("Ve"), 1)
                vrr = L.bump("Ve")
                ph.v(fvr2)

                def fsr3(q, r=r, vrr=vrr):
                    q.wait_ge(L.sem("Ve"), vrr)
                    q.activation(cmb[:, :], cmb[:, :], AF.Identity,
                                 scale=nw_sb[:, r:r + 1]).then_inc(
                                     L.sem("Ac"), 1)
                L._r_ve = getattr(L, "_r_ve", {})
                L._r_ve[0] = L.bump("Ac")
                ph.s(fsr3)
            lastac = L._r_ve[0]

            def fvfin(q, sl=sl, lastac=lastac):
                q.wait_ge(L.sem("Ac"), lastac)
                q.tensor_copy(o_fm[:, sl], cmb[:, :]).then_inc(L.sem("Ve"), 1)
            L.bump("Ve")
            ph.v(fvfin)
            ph.run()
        locf.close()

        ph = Ph(L)
        L.store(ph, o_o[:, c0g:c0g + PW], o_fm[:, :])
        ph.run()
    L.finish()
    return L



def _ident_np():
    return np.eye(128, dtype=np.float32)


def shard_x(x_full):
    out = []
    for c in range(NCORES):
        b, s = c // 4, (c % 4) * TC
        out.append(np.ascontiguousarray(x_full[b, s:s + TC, :].T))
    return out


def unshard_x(xs):
    out = np.empty((BB, SS, DD), np.float32)
    for c in range(NCORES):
        b, s = c // 4, (c % 4) * TC
        out[b, s:s + TC, :] = xs[c].T
    return out


def gbcols(vec_g, vec_b):
    g = np.ascontiguousarray(vec_g.reshape(8, 128).T)
    b = np.ascontiguousarray(vec_b.reshape(8, 128).T)
    return np.concatenate([g, b], axis=1).astype(np.float32)


_PROGS = {}


def get_prog(name, builder):
    if name not in _PROGS:
        _PROGS[name] = builder()
    return _PROGS[name]


def run_l3ffn(xs, inputs):
    L = get_prog("l3ffn", build_l3ffn)
    common = {
        "IDENT": _ident_np(),
        "fc1_q": inputs["L3_ffn_fc1_q"],
        "fc1_s": inputs["L3_ffn_fc1_scales"],
        "fc1_b": inputs["L3_ffn_fc1_bias"],
        "fc2_q": inputs["L3_ffn_fc2_q"],
        "fc2_s": inputs["L3_ffn_fc2_scales"],
        "fc2_b": inputs["L3_ffn_fc2_bias"],
        "gb": gbcols(inputs["L3_ffn_ln_gamma"], inputs["L3_ffn_ln_beta"]),
    }
    in_maps = [dict(common, x_in=xs[c]) for c in range(NCORES)]
    res = run_bass_kernel_spmd(L.nc, in_maps, core_ids=list(range(NCORES)))
    return [np.asarray(r["x_out"]) for r in res.results]


def run_l2ffn(xs, inputs):
    L = get_prog("l2ffn", build_l2ffn)
    scols = np.concatenate(
        [np.ascontiguousarray(inputs["L2_ffn_" + k].reshape(-1, 128).T)
         for k in ["w1_s", "w2_s", "h1_s", "h2_s", "w3_s"]], axis=1)
    common = {
        "IDENT": _ident_np(),
        "w1": np.ascontiguousarray(inputs["L2_ffn_w1_w"].T),
        "w2": np.ascontiguousarray(inputs["L2_ffn_w2_w"].T),
        "h1": np.ascontiguousarray(inputs["L2_ffn_h1_w"].T),
        "h2": np.ascontiguousarray(inputs["L2_ffn_h2_w"].T),
        "w3": np.ascontiguousarray(inputs["L2_ffn_w3_w"].T),
        "scols": np.ascontiguousarray(scols, dtype=np.float32),
    }
    in_maps = [dict(common, x_in=xs[c]) for c in range(NCORES)]
    res = run_bass_kernel_spmd(L.nc, in_maps, core_ids=list(range(NCORES)))
    return [np.asarray(r["x_out"]) for r in res.results]


def run_l0ffn(xs, inputs):
    L = get_prog("l0ffn", build_l0ffn)
    lA = inputs["L0_ffn_lora_A"]
    A_sb = np.ascontiguousarray(
        lA.transpose(2, 0, 1).reshape(8, 128, 128).transpose(1, 0, 2).reshape(128, 1024))
    lB = inputs["L0_ffn_lora_B"]
    B_all = np.ascontiguousarray(
        lB.transpose(0, 2, 1).reshape(128, 4096))
    oneh = np.zeros((8, 8 * 128), np.float32)
    for e in range(8):
        oneh[e, e * 128:(e + 1) * 128] = 1.0
    common = {
        "IDENT": _ident_np(),
        "ONEHOT8": oneh,
        "wi": np.ascontiguousarray(inputs["L0_ffn_wi"].T),
        "bi": inputs["L0_ffn_bi"],
        "wo": np.ascontiguousarray(inputs["L0_ffn_wo"].T),
        "bo": inputs["L0_ffn_bo"],
        "gw": np.ascontiguousarray(inputs["L0_ffn_gate_w"].T),
        "gbias": np.ascontiguousarray(inputs["L0_ffn_gate_b"].reshape(1, 8)),
        "A_sb": A_sb.astype(np.float32),
        "B_all": B_all.astype(np.float32),
    }
    in_maps = [dict(common, x_in=xs[c]) for c in range(NCORES)]
    res = run_bass_kernel_spmd(L.nc, in_maps, core_ids=list(range(NCORES)))
    return [np.asarray(r["x_out"]) for r in res.results]


def run_l1ffn(xs, inputs):
    L = get_prog("l1ffn", build_l1ffn)
    rm = inputs["L1_ffn_rail_maker"]
    rup = inputs["L1_ffn_rail_up_proj"]
    bd1 = np.zeros((4, 8, 128, 128), np.float32)
    for h in range(4):
        for f in range(8):
            for rr in range(8):
                r = f * 8 + rr
                bd1[h, f, rr * 16:(rr + 1) * 16, rr * 16:(rr + 1) * 16] = rm[h, r]
    bd1 = bd1.reshape(4 * 8 * 128, 128)
    bd2 = np.zeros((4, 32, 128, 128), np.float32)
    for h in range(4):
        for m in range(32):
            for rr in range(2):
                r = m * 2 + rr
                f = m // 4
                krow = (r - f * 8) * 16
                bd2[h, m, krow:krow + 16, rr * 64:(rr + 1) * 64] = \
                    rup[h, r].T
    bd2 = bd2.reshape(4 * 32 * 128, 128)
    common = {
        "IDENT": _ident_np(),
        "bd1": np.ascontiguousarray(bd1),
        "bd2": np.ascontiguousarray(bd2),
        "cw": np.ascontiguousarray(inputs["L1_ffn_w_cproj"].T),
    }
    in_maps = [dict(common, x_in=xs[c]) for c in range(NCORES)]
    res = run_bass_kernel_spmd(L.nc, in_maps, core_ids=list(range(NCORES)))
    return [np.asarray(r["x_out"]) for r in res.results]


def _l2perm():
    n = np.arange(1024)
    j, qt = n // 256, n % 256
    return qt * 4 + j


def run_l2attn(xs, inputs):
    perm = _l2perm()
    LA = get_prog("l2attnA", build_l2attnA)
    wkT = np.ascontiguousarray(inputs["L2_attn_Wk"].T[:, perm])
    wvT = np.ascontiguousarray(inputs["L2_attn_Wv"].T[:, perm])
    commonA = {
        "IDENT": _ident_np(),
        "wk": wkT, "bk": np.ascontiguousarray(inputs["L2_attn_bk"][perm]),
        "wv": wvT, "bv": np.ascontiguousarray(inputs["L2_attn_bv"][perm]),
    }
    in_maps = [dict(commonA, x_in=xs[c]) for c in range(NCORES)]
    resA = run_bass_kernel_spmd(LA.nc, in_maps, core_ids=list(range(NCORES)))
    bound = [np.asarray(r["bound"]) for r in resA.results]
    kn = [np.asarray(r["kn"]) for r in resA.results]

    LB = get_prog("l2attnB", build_l2attnB)
    rsq = (1.0 / np.sqrt(np.arange(1, SS + 1, dtype=np.float64))).astype(
        np.float32).reshape(1, SS)
    in_mapsB = []
    for c in range(NCORES):
        b, qr = c // 4, c % 4
        bf = np.concatenate([bound[4 * b + i] for i in range(4)], axis=1)
        kf = np.concatenate([kn[4 * b + i] for i in range(4)], axis=1)
        rows = np.concatenate(
            [np.arange(j * 256 + qr * 64, j * 256 + qr * 64 + 64)
             for j in range(4)])
        in_mapsB.append({"IDENT": _ident_np(),
                         "bound": np.ascontiguousarray(bf[rows]),
                         "kn": np.ascontiguousarray(kf[rows]),
                         "rsq": rsq})
    resB = run_bass_kernel_spmd(LB.nc, in_mapsB, core_ids=list(range(NCORES)))
    retr = [np.asarray(r["retr"]) for r in resB.results]

    LC = get_prog("l2attnC", build_l2attnC)
    woT = np.ascontiguousarray(inputs["L2_attn_Wo"].T[perm, :])
    gbq = gbcols(inputs["L2_attn_ln_g"][perm], inputs["L2_attn_ln_b"][perm])
    commonC = {"IDENT": _ident_np(), "wo": woT,
               "bo": inputs["L2_attn_bo"], "gb": gbq}
    in_mapsC = []
    for c in range(NCORES):
        b, sc = c // 4, c % 4
        rfull = np.empty((1024, SS), np.float32)
        for qr in range(4):
            rb = retr[4 * b + qr]
            for j in range(4):
                rfull[j * 256 + qr * 64:j * 256 + qr * 64 + 64] = \
                    rb[j * 64:(j + 1) * 64]
        in_mapsC.append(dict(commonC, x_in=xs[c],
                             retr=np.ascontiguousarray(
                                 rfull[:, sc * TC:(sc + 1) * TC])))
    resC = run_bass_kernel_spmd(LC.nc, in_mapsC, core_ids=list(range(NCORES)))
    return [np.asarray(r["x_out"]) for r in resC.results]


def _rope_perm(nheads):
    perm = np.zeros(nheads * 64, dtype=np.int64)
    for h in range(nheads):
        for d in range(64):
            new = (h // 2) * 128 + (h % 2) * 64 + (d % 2) * 32 + d // 2
            perm[new] = h * 64 + d
    return perm


def _halo_x(xs, c, hlo):
    b, sc = c // 4, c % 4
    left = (np.zeros((DD, hlo), np.float32) if sc == 0
            else xs[c - 1][:, TC - hlo:])
    right = (np.zeros((DD, hlo), np.float32) if sc == 3
             else xs[c + 1][:, :hlo])
    return np.ascontiguousarray(np.concatenate([left, xs[c], right], axis=1))


def run_l0attn(xs, inputs):
    L = get_prog("l0attn", build_l0attn)
    HLO = 128
    Q0 = [0, 1, 2, 3, 8, 9, 10, 11]
    Q1 = [4, 5, 6, 7, 12, 13, 14, 15]
    pq = np.zeros(1024, dtype=np.int64)
    wo_perm = np.zeros(1024, dtype=np.int64)
    for j in range(8):
        for b, qh in ((0, Q0[j]), (64, Q1[j])):
            for d in range(64):
                new_col = j * 128 + b + (d % 2) * 32 + d // 2
                pq[new_col] = qh * 64 + d
                wo_perm[j * 128 + b + d] = qh * 64 + d
    pk = _rope_perm(4)
    wq = inputs["L0_attn_Wq"].reshape(1024, 1024)[:, pq]
    bq = inputs["L0_attn_bq"].reshape(1024)[pq]
    wk = inputs["L0_attn_Wk"].reshape(1024, 256)[:, pk]
    bk = inputs["L0_attn_bk"].reshape(256)[pk]
    wv = inputs["L0_attn_Wv"].reshape(1024, 256)
    bv = inputs["L0_attn_bv"].reshape(256)
    wo = inputs["L0_attn_Wo"].reshape(1024, 1024)[wo_perm, :]
    SCALE = np.float64(64.0) ** -0.25
    oneh = np.zeros((8, 8 * 128), np.float32)
    for e in range(8):
        oneh[e, e * 128:(e + 1) * 128] = 1.0
    P = np.zeros((128, 128), np.float32)
    for m in range(128):
        blk, local = m // 64, m % 64
        P[blk * 64 + (local + 32) % 64, m] = 1.0
    common = {
        "IDENT": _ident_np(), "PERM": P, "ONEHOT8": oneh,
        "wq": np.ascontiguousarray(wq), "bq": np.ascontiguousarray(bq),
        "wk": np.ascontiguousarray(wk), "bk": np.ascontiguousarray(bk),
        "wv": np.ascontiguousarray(wv), "bv": np.ascontiguousarray(bv),
        "wo": np.ascontiguousarray(wo), "bo": inputs["L0_attn_bo"],
        "sinks": np.ascontiguousarray(inputs["L0_attn_sinks"].reshape(1, 16)),
        "nsc": gbcols(inputs["L0_attn_attn_norm_scale"],
                      inputs["L0_attn_moe_norm_scale"]),
        "rw": np.ascontiguousarray(inputs["L0_attn_router_W"]),
        "rb": np.ascontiguousarray(inputs["L0_attn_router_b"].reshape(1, 8)),
        "gup": np.ascontiguousarray(
            inputs["L0_attn_gate_up_proj"].reshape(8 * 1024, 2048)),
        "gupb": inputs["L0_attn_gate_up_bias"],
        "dwn": np.ascontiguousarray(
            inputs["L0_attn_down_proj"].reshape(8 * 1024, 1024)),
        "dwnb": inputs["L0_attn_down_bias"],
    }
    in_maps = []
    half = 32
    inv_freq = 1.0 / (150000.0 ** (np.arange(half, dtype=np.float64) / half))
    for c in range(NCORES):
        b, sc = c // 4, c % 4
        s0 = sc * TC
        pos = np.arange(s0 - HLO, s0 + TC + HLO, dtype=np.float64)
        posc = np.clip(pos, 0, SS - 1)
        ang = posc[None, :] * inv_freq[:, None]
        cosv = (np.cos(ang) * SCALE).astype(np.float32)
        sinv = (np.sin(ang) * SCALE).astype(np.float32)
        ctab = np.empty((128, TC + 2 * HLO), np.float32)
        stab = np.empty((128, TC + 2 * HLO), np.float32)
        for blk in range(2):
            ctab[blk * 64:blk * 64 + 32] = cosv
            ctab[blk * 64 + 32:blk * 64 + 64] = cosv
            stab[blk * 64:blk * 64 + 32] = -sinv
            stab[blk * 64 + 32:blk * 64 + 64] = sinv
        tloc = np.arange(TC)
        tabs = s0 + tloc
        mask = np.zeros((TC, 384), np.float32)
        for tt in range(16):
            rows = slice(tt * 128, (tt + 1) * 128)
            kabs = s0 + tt * 128 - HLO + np.arange(384)
            qa = tabs[rows][:, None]
            mask[rows] = ((np.abs(qa - kabs[None, :]) <= 128) &
                          (kabs[None, :] >= 0) & (kabs[None, :] < SS))
        in_maps.append(dict(common, x_in=_halo_x(xs, c, HLO),
                            ctab=ctab, stab=stab,
                            mask=np.ascontiguousarray(mask)))
    res = run_bass_kernel_spmd(L.nc, in_maps, core_ids=list(range(NCORES)))
    return [np.asarray(r["x_out"]) for r in res.results]


def run_l3attn(xs, inputs):
    LA = get_prog("l3attnA", build_l3attnA)
    G4 = np.zeros((128, 4), np.float32)
    for p in range(128):
        G4[p, p // 32] = 1.0
    gam = inputs["L3_attn_gn_gamma"]
    bet = inputs["L3_attn_gn_beta"]
    in_mapsA = []
    for c in range(NCORES):
        b, cb = c // 4, c % 4
        xf = np.concatenate([xs[4 * b + i] for i in range(4)], axis=1)
        ch0 = cb * 256
        g2 = np.stack([gam[ch0:ch0 + 128], gam[ch0 + 128:ch0 + 256],
                       bet[ch0:ch0 + 128], bet[ch0 + 128:ch0 + 256]], axis=1)
        in_mapsA.append({"IDENT": _ident_np(), "G4": G4,
                         "G4T": np.ascontiguousarray(G4.T),
                         "x_cb": np.ascontiguousarray(xf[ch0:ch0 + 256]),
                         "gb": np.ascontiguousarray(g2, dtype=np.float32)})
    resA = run_bass_kernel_spmd(LA.nc, in_mapsA, core_ids=list(range(NCORES)))
    hn = [np.asarray(r["hn"]) for r in resA.results]

    import ml_dtypes
    hnf = np.empty((1024, 2 * SS), dtype=ml_dtypes.bfloat16)
    for c in range(NCORES):
        b, cb = c // 4, c % 4
        hnf[cb * 256:(cb + 1) * 256, b * SS:(b + 1) * SS] = hn[c]
    LB = get_prog("l3attnB", build_l3attnB)
    wqkv = inputs["L3_attn_W_qkv"]
    bqkv = inputs["L3_attn_b_qkv"]
    in_mapsB = []
    for c in range(NCORES):
        h0c = 2 * c * 64
        in_mapsB.append({
            "IDENT": _ident_np(),
            "hn": hnf,
            "wq": np.ascontiguousarray(wqkv[:, h0c:h0c + 128]),
            "wk": np.ascontiguousarray(wqkv[:, 1024 + h0c:1024 + h0c + 128]),
            "wv": np.ascontiguousarray(wqkv[:, 2048 + h0c:2048 + h0c + 128]),
            "bqkv": np.ascontiguousarray(np.stack(
                [bqkv[h0c:h0c + 128], bqkv[1024 + h0c:1024 + h0c + 128],
                 bqkv[2048 + h0c:2048 + h0c + 128]])),
        })
    resB = run_bass_kernel_spmd(LB.nc, in_mapsB, core_ids=list(range(NCORES)))
    attp = [np.asarray(r["att"]) for r in resB.results]
    attf = np.concatenate(attp, axis=0)

    LC = get_prog("l3attnC", build_l3attnC)
    commonC = {"IDENT": _ident_np(),
               "wo": np.ascontiguousarray(inputs["L3_attn_W_o"]),
               "bo": inputs["L3_attn_b_o"]}
    in_mapsC = []
    for c in range(NCORES):
        b, sc = c // 4, c % 4
        cols = slice(b * SS + sc * TC, b * SS + sc * TC + TC)
        in_mapsC.append(dict(commonC, x_in=xs[c],
                             att=np.ascontiguousarray(attf[:, cols])))
    resC = run_bass_kernel_spmd(LC.nc, in_mapsC, core_ids=list(range(NCORES)))
    return [np.asarray(r["x_out"]) for r in resC.results]


def run_l1attn(xs, inputs):
    LA = get_prog("l1attnA", build_l1attnA)
    HLO = 512
    cw = np.zeros((128, 96), np.float32)
    for off, key in ((0, "conv_q_w"), (32, "conv_k_w"), (64, "conv_v_w")):
        w = inputs["L1_attn_" + key]
        cw[:, off:off + 32] = w.reshape(8, 128, 4).transpose(1, 0, 2).reshape(
            128, 32)
    commonA = {
        "IDENT": _ident_np(), "cw": cw,
        "wq": np.ascontiguousarray(inputs["L1_attn_Wq"]),
        "wk": np.ascontiguousarray(inputs["L1_attn_Wk"]),
        "wv": np.ascontiguousarray(inputs["L1_attn_Wv"]),
        "wb": np.ascontiguousarray(inputs["L1_attn_Wb"]),
        "flw": np.ascontiguousarray(inputs["L1_attn_floor_W"]),
        "flb": np.ascontiguousarray(inputs["L1_attn_floor_b"].reshape(1, 8)),
        "w1": np.ascontiguousarray(inputs["L1_attn_fus_W1"]),
        "b1": inputs["L1_attn_fus_b1"],
        "w2": np.ascontiguousarray(inputs["L1_attn_fus_W2"]),
        "b2": np.ascontiguousarray(inputs["L1_attn_fus_b2"].reshape(1, 32)),
    }
    in_mapsA = []
    for c in range(NCORES):
        b, sc = c // 4, c % 4
        left = (np.zeros((DD, HLO), np.float32) if sc == 0
                else xs[c - 1][:, TC - HLO:])
        in_mapsA.append(dict(commonA, x_in=np.ascontiguousarray(
            np.concatenate([left, xs[c]], axis=1))))
    resA = run_bass_kernel_spmd(LA.nc, in_mapsA, core_ids=list(range(NCORES)))
    qf = np.concatenate([np.asarray(r["q_o"]) for r in resA.results], axis=1)
    kf = np.concatenate([np.asarray(r["k_o"]) for r in resA.results], axis=1)
    vf = np.concatenate([np.asarray(r["v_o"]) for r in resA.results], axis=1)
    bet = np.concatenate([np.asarray(r["beta_o"]) for r in resA.results], axis=1)
    flr = np.concatenate([np.asarray(r["flr_o"]) for r in resA.results], axis=1)
    fus = np.concatenate([np.asarray(r["fus_o"]) for r in resA.results], axis=1)

    LB = get_prog("l1attnB", build_l1attnB)
    MP = np.zeros((64, 64), np.float32)
    MPL = np.zeros((64, 64), np.float32)
    for blk in range(2):
        for m in range(32):
            for n in range(32):
                if n > m:
                    MP[blk * 32 + m, blk * 32 + n] = -1.0
                if m > n:
                    MPL[blk * 32 + m, blk * 32 + n] = -1.0
    MLOC = np.zeros((32, 32), np.float32)
    for m in range(32):
        MLOC[m, m:] = 1.0
    EYE = np.eye(64, dtype=np.float32)
    in_mapsB = []
    for c in range(NCORES):
        h = c
        rows = slice(h * 128, (h + 1) * 128)
        in_mapsB.append({
            "IDENT": _ident_np(), "MPAIR": MP, "MPAIRL": MPL, "MLOC": MLOC,
            "EYE64": EYE,
            "q_i": np.ascontiguousarray(qf[rows]),
            "k_i": np.ascontiguousarray(kf[rows]),
            "v_i": np.ascontiguousarray(vf[rows]),
            "b_i": np.ascontiguousarray(bet[h:h + 1]),
            "f0": np.ascontiguousarray(fus[4 * h:4 * h + 1]),
            "f1": np.ascontiguousarray(fus[4 * h + 1:4 * h + 2]),
            "f2": np.ascontiguousarray(fus[4 * h + 2:4 * h + 3]),
            "f3": np.ascontiguousarray(fus[4 * h + 3:4 * h + 4]),
            "fl": np.ascontiguousarray(flr[h:h + 1]),
            "fw": np.ascontiguousarray(np.concatenate(
                [inputs["L1_attn_fir_short_w"][h],
                 inputs["L1_attn_fir_long_w"][h]], axis=1)),
            "nw": np.ascontiguousarray(np.stack(
                [inputs["L1_attn_norm_pf_w"],
                 inputs["L1_attn_norm_o_w"]], axis=1)),
        })
    resB = run_bass_kernel_spmd(LB.nc, in_mapsB, core_ids=list(range(NCORES)))
    of = np.concatenate([np.asarray(r["o_o"]) for r in resB.results], axis=0)

    LC = get_prog("l1attnC", build_l1attnC)
    commonC = {"IDENT": _ident_np(),
               "wo": np.ascontiguousarray(inputs["L1_attn_Wo"])}
    in_mapsC = []
    for c in range(NCORES):
        b, sc = c // 4, c % 4
        cols = slice(b * SS + sc * TC, b * SS + sc * TC + TC)
        in_mapsC.append(dict(commonC, x_in=xs[c],
                             o_i=np.ascontiguousarray(of[:, cols])))
    resC = run_bass_kernel_spmd(LC.nc, in_mapsC, core_ids=list(range(NCORES)))
    return [np.asarray(r["x_out"]) for r in resC.results]


def kernel(**inputs):
    xs = shard_x(np.ascontiguousarray(inputs["L0_attn_hidden_states"],
                                      dtype=np.float32))
    xs = run_l0ffn(xs, inputs)
    xs = run_l0attn(xs, inputs)
    xs = run_l0ffn(xs, inputs)
    xs = run_l1ffn(xs, inputs)
    xs = run_l1attn(xs, inputs)
    xs = run_l1ffn(xs, inputs)
    xs = run_l2ffn(xs, inputs)
    xs = run_l2attn(xs, inputs)
    xs = run_l2ffn(xs, inputs)
    xs = run_l3ffn(xs, inputs)
    xs = run_l3attn(xs, inputs)
    xs = run_l3ffn(xs, inputs)
    return unshard_x(xs)


# version 02 (final)
# speedup vs baseline: 1.0595x; 1.0595x over previous
import contextlib
import numpy as np
import concourse.bass as bass
import concourse.mybir as mybir
from concourse.bass_utils import run_bass_kernel_spmd

F32 = mybir.dt.float32
BF16 = mybir.dt.bfloat16
I32 = mybir.dt.int32
AF = mybir.ActivationFunctionType
ALU = mybir.AluOpType
AX = mybir.AxisListType

NCORES = 8
DEBUG_L0 = False
DEBUG_L2A = False
L0_STOP = 99
BB, SS, DD = 2, 8192, 1024
TC = 2048
EPS_LN = 1e-5


def IM(x):
    return np.float32(x)

ENGINES = ("tensor", "vector", "scalar", "gpsimd", "sync")


class Ph:

    def __init__(self, L):
        self.L = L
        self.ops = {e: [] for e in ENGINES}

    def t(self, fn):
        self.ops["tensor"].append(fn)

    def v(self, fn):
        self.ops["vector"].append(fn)

    def s(self, fn):
        self.ops["scalar"].append(fn)

    def g(self, fn):
        self.ops["gpsimd"].append(fn)

    def y(self, fn):
        self.ops["sync"].append(fn)

    def eng(self, name):
        return {"tensor": self.t, "vector": self.v, "scalar": self.s,
                "gpsimd": self.g, "sync": self.y,
                "t": self.t, "v": self.v, "s": self.s, "g": self.g,
                "y": self.y}[name]

    def run(self):
        nc = self.L.nc
        with nc.Block() as blk:
            for eng in ENGINES:
                fns = self.ops[eng]
                if not fns:
                    continue

                def body(q, fns=fns):
                    for f in fns:
                        f(q)

                getattr(blk, eng)(body)


class Launch:

    def __init__(self, name):
        self.name = name
        self.nc = bass.Bass("TRN2", target_bir_lowering=False)
        self.stack = contextlib.ExitStack()
        self._sems = {}
        self._cnt = {}
        self._uid = 0
        self.prefix = ""
        self.io_overrides = {}
        self._stack_stack = []
        self._consts_done = False

    def push_scope(self):
        s = contextlib.ExitStack()
        self._stack_stack.append(s)
        return s

    def pop_scope(self):
        self._stack_stack.pop().close()

    def uid(self):
        self._uid += 1
        return self._uid

    def sem(self, name):
        if name not in self._sems:
            self._sems[name] = self.stack.enter_context(self.nc.semaphore(name))
            self._cnt[name] = 0
        return self._sems[name]

    def bump(self, name, n=1):
        self.sem(name)
        self._cnt[name] += n
        return self._cnt[name]

    def cnt(self, name):
        self.sem(name)
        return self._cnt[name]

    def sb(self, name, shape, dt, stack=None):
        if stack is None and self._stack_stack:
            stack = self._stack_stack[-1]
        return (stack or self.stack).enter_context(
            self.nc.sbuf_tensor(f"{name}_{self.uid()}", list(shape), dt))

    def ps(self, name, shape, dt=F32, stack=None):
        if stack is None and self._stack_stack:
            stack = self._stack_stack[-1]
        return (stack or self.stack).enter_context(
            self.nc.psum_tensor(f"{name}_{self.uid()}", list(shape), dt))

    def dram_in(self, name, shape, dt=F32):
        if name in self.io_overrides:
            return self.io_overrides[name]
        return self.nc.dram_tensor(self.prefix + name, list(shape), dt,
                                   kind="ExternalInput").ap()

    def dram_out(self, name, shape, dt=F32):
        if name in self.io_overrides:
            return self.io_overrides[name]
        return self.nc.dram_tensor(self.prefix + name, list(shape), dt,
                                   kind="ExternalOutput").ap()

    def dram_tmp(self, name, shape, dt=F32):
        if name in self.io_overrides:
            return self.io_overrides[name]
        return self.nc.dram_tensor(self.prefix + name, list(shape), dt,
                                   kind="Internal").ap()

    def emit_consts(self):
        if self._consts_done:
            return
        self._consts_done = True
        sv = self.prefix
        self.prefix = ""
        self._emit_consts_inner()
        self.prefix = sv

    def _emit_consts_inner(self):
        self.ones_col_f = self.sb("ones_col_f", (128, 1), F32, stack=self.stack)
        self.ones_col_b = self.sb("ones_col_b", (128, 1), BF16, stack=self.stack)
        self.ones_row_f = self.sb("ones_row_f", (1, 512), F32, stack=self.stack)
        self.ones_row_b = self.sb("ones_row_b", (1, 512), BF16, stack=self.stack)
        self.ones_r128_f = self.sb("ones_r128_f", (1, 128), F32, stack=self.stack)
        self.ones_r128_b = self.sb("ones_r128_b", (1, 128), BF16, stack=self.stack)
        self.ident_b = self.sb("ident_b", (128, 128), BF16, stack=self.stack)
        self.ident_f = self.sb("ident_f", (128, 128), F32, stack=self.stack)
        ident = self.dram_in("IDENT", (128, 128), F32)
        ph = Ph(self)
        sem = self.sem("Ld")
        self.bump("Ld", 32)

        def g(q):
            q.memset(self.ones_col_f[:, :], 1.0)
            q.memset(self.ones_col_b[:, :], 1.0)
            q.memset(self.ones_row_f[:, :], 1.0)
            q.memset(self.ones_row_b[:, :], 1.0)
            q.memset(self.ones_r128_f[:, :], 1.0)
            q.memset(self.ones_r128_b[:, :], 1.0)
            q.dma_start(out=self.ident_b[:, :], in_=ident).then_inc(sem, 16)
            q.dma_start(out=self.ident_f[:, :], in_=ident).then_inc(sem, 16)
        ph.g(g)
        tgt = self.cnt("Ld")

        def w(q):
            q.wait_ge(sem, tgt)
        ph.v(w)
        ph.run()

    def vsync(self, q):
        t = self.bump("Vx")
        q.engine_nop().then_inc(self.sem("Vx"), 1)
        q.wait_ge(self.sem("Vx"), t)

    def load(self, ph, dst_ap, src_ap, queue="gpsimd"):
        tgt = self.bump("Ld", 16)
        sem = self.sem("Ld")

        def f(q):
            q.dma_start(out=dst_ap, in_=src_ap).then_inc(sem, 16)
        getattr(ph, {"gpsimd": "g", "sync": "y", "scalar": "s"}[queue])(f)
        return tgt

    def store(self, ph, dst_ap, src_ap, queue="sync"):
        tgt = self.bump("St", 16)
        sem = self.sem("St")

        def f(q):
            q.dma_start(out=dst_ap, in_=src_ap).then_inc(sem, 16)
        getattr(ph, {"gpsimd": "g", "sync": "y", "scalar": "s"}[queue])(f)
        return tgt

    def wait(self, ph, eng, sname, tgt):
        sem = self.sem(sname)

        def f(q):
            q.wait_ge(sem, tgt)
        ph.eng(eng)(f)

    def barrier_wait_all_dma(self):
        ph = Ph(self)
        ld, st = self.cnt("Ld"), self.cnt("St")
        lsem, ssem = self.sem("Ld"), self.sem("St")

        def f(q):
            q.wait_ge(lsem, ld)
            q.wait_ge(ssem, st)
        ph.y(f)
        ph.run()

    def finish(self):
        self.barrier_wait_all_dma()
        self.stack.close()


    def ln_window(self, x_tiles, Tn, out_bf, sq_scratch, out_f32=None,
                  rms_mode=False, gamma_cols=None, beta_cols=None, nf=8,
                  in_bf16=False):
        D = nf * 128
        chunks = []
        c0 = 0
        while c0 < Tn:
            chunks.append((c0, min(512, Tn - c0)))
            c0 += 512
        NCH = len(chunks)
        loc = contextlib.ExitStack()
        s1r = self.sb("ln_s1", (1, Tn), F32, loc) if not rms_mode else None
        s2r = self.sb("ln_s2", (1, Tn), F32, loc)
        rstd = self.sb("ln_rstd", (1, Tn), F32, loc)
        mr = self.sb("ln_mr", (1, Tn), F32, loc) if not rms_mode else None
        tmp = self.sb("ln_tmp", (128, 512), F32, loc)
        ps2 = self.ps("ln_ps2", (1, 512), stack=loc)
        ps1 = self.ps("ln_ps1", (1, 512), stack=loc) if not rms_mode else None
        psA = self.ps("ln_psA", (128, 512), stack=loc)
        psB = self.ps("ln_psB", (128, 512), stack=loc) if not rms_mode else None

        for c, (cc0, cw) in enumerate(chunks):
            ph = Ph(self)

            def fs(q, cc0=cc0, cw=cw):
                last = None
                for f in range(nf):
                    last = q.activation(sq_scratch[:, f * 512:f * 512 + cw],
                                        x_tiles[f][:, cc0:cc0 + cw],
                                        AF.Square)
                last.then_inc(self.sem("Ac"), 1)
            ac = self.bump("Ac")
            ph.s(fs)

            def ft(q, cc0=cc0, cw=cw, ac=ac):
                q.wait_ge(self.sem("Ac"), ac)
                last = None
                onescol = self.ones_col_b if in_bf16 else self.ones_col_f
                for f in range(nf):
                    if not rms_mode:
                        q.matmul(ps1[:, 0:cw], onescol[:, :],
                                 x_tiles[f][:, cc0:cc0 + cw],
                                 start=(f == 0), stop=(f == nf - 1))
                    last = q.matmul(ps2[:, 0:cw], self.ones_col_f[:, :],
                                    sq_scratch[:, f * 512:f * 512 + cw],
                                    start=(f == 0), stop=(f == nf - 1))
                last.then_inc(self.sem("Pe"), 1)
            pe = self.bump("Pe")
            ph.t(ft)

            def fv(q, cc0=cc0, cw=cw, pe=pe):
                q.wait_ge(self.sem("Pe"), pe)
                if not rms_mode:
                    q.tensor_copy(s1r[:, cc0:cc0 + cw], ps1[:, 0:cw])
                q.tensor_copy(s2r[:, cc0:cc0 + cw], ps2[:, 0:cw])
            ph.v(fv)
            ph.run()

        ph = Ph(self)
        if not rms_mode:
            def fs(q):
                q.activation(mr[:, :], s1r[:, :], AF.Copy, scale=IM(1.0 / D))
                q.activation(s1r[:, :], s1r[:, :], AF.Square, scale=IM(1.0 / D))
                q.activation(s2r[:, :], s2r[:, :], AF.Copy,
                             scale=IM(1.0 / D)).then_inc(self.sem("Ac"), 1)
            ac = self.bump("Ac")
            ph.s(fs)

            def fv(q, ac=ac):
                q.wait_ge(self.sem("Ac"), ac)
                q.tensor_tensor(s2r[:, :], s2r[:, :], s1r[:, :],
                                ALU.subtract).then_inc(self.sem("Ve"), 1)
            ve = self.bump("Ve")
            ph.v(fv)

            def fs2(q, ve=ve):
                q.wait_ge(self.sem("Ve"), ve)
                q.activation(s2r[:, :], s2r[:, :], AF.Ln, bias=IM(EPS_LN))
                q.activation(rstd[:, :], s2r[:, :], AF.Exp,
                             scale=IM(-0.5)).then_inc(self.sem("Ac"), 1)
            ac2 = self.bump("Ac")
            ph.s(fs2)

            def fv2(q, ac2=ac2):
                q.wait_ge(self.sem("Ac"), ac2)
                q.tensor_tensor(mr[:, :], mr[:, :], rstd[:, :], ALU.mult)
            ph.v(fv2)
        else:
            def fs(q):
                q.activation(s2r[:, :], s2r[:, :], AF.Ln, bias=IM(EPS_LN), scale=IM(1.0 / D))
                q.activation(rstd[:, :], s2r[:, :], AF.Exp, scale=IM(-0.5))
            ph.s(fs)
        ph.run()

        for c, (cc0, cw) in enumerate(chunks):
            ph = Ph(self)

            def ft(q, cc0=cc0, cw=cw):
                mm = q.matmul(psA[:, 0:cw], self.ones_r128_f[:, :],
                              rstd[0:1, cc0:cc0 + cw], start=True, stop=True)
                if not rms_mode:
                    mm = q.matmul(psB[:, 0:cw], self.ones_r128_f[:, :],
                                  mr[0:1, cc0:cc0 + cw], start=True,
                                  stop=True)
                mm.then_inc(self.sem("Pe"), 1)
            pe = self.bump("Pe")
            ph.t(ft)

            def fv(q, cc0=cc0, cw=cw, pe=pe):
                q.wait_ge(self.sem("Pe"), pe)
                sl = slice(cc0, cc0 + cw)
                last = None
                for f in range(nf):
                    if rms_mode:
                        last = q.tensor_tensor(out_bf[f][:, sl], x_tiles[f][:, sl],
                                               psA[:, 0:cw], ALU.mult)
                        if out_f32 is not None:
                            last = q.tensor_tensor(out_f32[f][:, sl],
                                                   x_tiles[f][:, sl],
                                                   psA[:, 0:cw], ALU.mult)
                    else:
                        q.tensor_tensor(tmp[:, 0:cw], x_tiles[f][:, sl],
                                        psA[:, 0:cw], ALU.mult)
                        last = q.tensor_tensor(out_bf[f][:, sl], tmp[:, 0:cw],
                                               psB[:, 0:cw], ALU.subtract)
                        if out_f32 is not None:
                            last = q.tensor_tensor(out_f32[f][:, sl], tmp[:, 0:cw],
                                                   psB[:, 0:cw], ALU.subtract)
                last.then_inc(self.sem("Ve"), 1)
            venorm = self.bump("Ve")
            ph.v(fv)
            if gamma_cols is not None:
                def fs3(q, cc0=cc0, cw=cw, venorm=venorm):
                    q.wait_ge(self.sem("Ve"), venorm)
                    sl = slice(cc0, cc0 + cw)
                    for f in range(nf):
                        q.activation(out_bf[f][:, sl], out_bf[f][:, sl], AF.Identity,
                                     scale=gamma_cols[f],
                                     bias=(beta_cols[f] if beta_cols is not None
                                           else 0.0))
                        if out_f32 is not None:
                            q.activation(out_f32[f][:, sl], out_f32[f][:, sl],
                                         AF.Identity, scale=gamma_cols[f],
                                         bias=(beta_cols[f] if beta_cols is not None
                                               else 0.0))
                ph.s(fs3)
            ph.run()
        loc.close()

    def proj(self, *, rhs_tiles, Tn, K, M, w_dram, bias_dram=None, evac,
             mblock=512, dequant_scales=None, rhs_off=0, nps=4):
        Ktiles = K // 128
        chunks = []
        _c0 = 0
        while _c0 < Tn:
            chunks.append((_c0, min(512, Tn - _c0)))
            _c0 += 512
        NCH = len(chunks)
        assert K % 128 == 0 and M % 128 == 0
        loc = contextlib.ExitStack()
        psums = [self.ps("pj_ps", (128, 512), stack=loc) for _ in range(nps)]
        wb = [self.sb("pj_w", (128, Ktiles * mblock), BF16, loc) for _ in range(2)]
        dq = dequant_scales is not None
        if dq:
            assert Ktiles * mblock <= 4096
            wraw = [self.sb("pj_wr", (128, Ktiles * mblock), BF16, loc)
                    for _ in range(2)]
            ps_scl = [self.ps("pj_pscl", (128, 512), stack=loc) for _ in range(2)]
            scl_row = [self.sb("pj_scl", (1, Ktiles * mblock), BF16, loc)
                       for _ in range(2)]
        bias_row = None
        nmb = (M + mblock - 1) // mblock
        sl_sem = ["LdA", "LdB"]

        def wsrc(mb):
            m0 = mb * mblock
            mw = min(mblock, M - m0)
            return w_dram[:, m0:m0 + mw].rearrange("(s p) m -> p s m", p=128)

        def ssrc(mb):
            m0 = mb * mblock
            mw = min(mblock, M - m0)
            return dequant_scales[:, m0:m0 + mw].rearrange("(o k) m -> o k m", o=1)

        def issue_block(ph, mb):
            par = mb % 2
            mw = min(mblock, M - mb * mblock)
            sem = sl_sem[par]
            tgt = self.bump(sem, 16)
            s = self.sem(sem)
            dst = (wraw if dq else wb)[par][:, 0:Ktiles * mw].rearrange(
                "p (s m) -> p s m", s=Ktiles)
            srcw = wsrc(mb)

            def fg(q, dst=dst, srcw=srcw, s=s):
                q.dma_start(out=dst, in_=srcw).then_inc(s, 16)
            ph.g(fg)
            if dq:
                tgt2 = self.bump(sem, 16)
                srcs = ssrc(mb)
                dsts = scl_row[par][0:1, 0:Ktiles * mw].rearrange(
                    "o (k m) -> o k m", k=Ktiles)

                def fg2(q, dsts=dsts, srcs=srcs, s=s):
                    q.dma_start(out=dsts, in_=srcs).then_inc(s, 16)
                ph.g(fg2)
                return tgt2
            return tgt

        ph = Ph(self)
        if bias_dram is not None:
            bias_row = self.sb("pj_bias", (1, M), BF16, loc)
            bsrc = (bias_dram if len(bias_dram.shape) == 2
                    else bias_dram.rearrange("(o m) -> o m", o=1))
            self.load(ph, bias_row[0:1, 0:M], bsrc)
        ld_general = self.cnt("Ld")
        ld_tgts = {0: issue_block(ph, 0)}
        ph.run()

        groups_pe = {}
        dq_ve = {}
        ev_of_bank = {}
        for mb in range(nmb):
            m0 = mb * mblock
            mw = min(mblock, M - m0)
            msub = mw // 128
            par = mb % 2
            ph = Ph(self)
            if mb + 1 < nmb:
                def fg(q, mb=mb):
                    if mb >= 1:
                        q.wait_ge(self.sem("Pe"), groups_pe[mb - 1])
                        if dq:
                            q.wait_ge(self.sem("Ve"), dq_ve[mb - 1])
                ph.g(fg)
                ld_tgts[mb + 1] = issue_block(ph, mb + 1)
            if dq:
                scl_ve = {}
                for k in range(Ktiles):
                    def ftq(q, k=k, mw=mw, par=par, mb=mb):
                        if k == 0:
                            q.wait_ge(self.sem(sl_sem[par]), ld_tgts[mb])
                        if k >= 2:
                            q.wait_ge(self.sem("Ve"), scl_ve[k - 2])
                        q.matmul(ps_scl[k % 2][:, 0:mw], self.ones_r128_b[:, :],
                                 scl_row[par][0:1, k * mw:(k + 1) * mw],
                                 start=True, stop=True).then_inc(self.sem("Pe"), 1)
                    pet = self.bump("Pe")
                    ph.t(ftq)

                    def fvq(q, k=k, mw=mw, par=par, pet=pet):
                        q.wait_ge(self.sem("Pe"), pet)
                        q.scalar_tensor_tensor(
                            wb[par][:, k * mw:(k + 1) * mw],
                            wraw[par][:, k * mw:(k + 1) * mw],
                            -128.0, ps_scl[k % 2][:, 0:mw],
                            ALU.add, ALU.mult).then_inc(self.sem("Ve"), 1)
                    scl_ve[k] = self.bump("Ve")
                    ph.v(fvq)
                dq_ve[mb] = self.cnt("Ve")
            for c, (cc0, cw) in enumerate(chunks):
                for ms in range(msub):
                    bank = (mb * NCH * msub + c * msub + ms) % nps

                    def ft(q, cc0=cc0, cw=cw, ms=ms, bank=bank, mb=mb, mw=mw,
                           par=par, first=(c == 0 and ms == 0),
                           evprev=ev_of_bank.get(bank)):
                        if first:
                            if dq:
                                q.wait_ge(self.sem("Ve"), dq_ve[mb])
                            else:
                                q.wait_ge(self.sem(sl_sem[par]), ld_tgts[mb])
                            if mb == 0:
                                q.wait_ge(self.sem("Ld"), ld_general)
                        if evprev:
                            q.wait_ge(self.sem("Ev"), evprev)
                        nk = Ktiles
                        for k in range(nk):
                            mm = q.matmul(
                                psums[bank][:, 0:cw],
                                wb[par][:, k * mw + ms * 128:k * mw + (ms + 1) * 128],
                                rhs_tiles[k][:, rhs_off + cc0:rhs_off + cc0 + cw],
                                start=(k == 0),
                                stop=(k == nk - 1) and bias_row is None)
                        if bias_row is not None:
                            mm = q.matmul(psums[bank][:, 0:cw],
                                          bias_row[0:1, m0 + ms * 128:m0 + (ms + 1) * 128],
                                          self.ones_row_b[0:1, 0:cw],
                                          start=False, stop=True)
                        mm.then_inc(self.sem("Pe"), 1)
                    pe = self.bump("Pe")
                    ph.t(ft)
                    mi = (m0 + ms * 128) // 128
                    evtgt = evac(mi, c, psums[bank], ph, pe)
                    ev_of_bank[bank] = evtgt
            groups_pe[mb] = self.cnt("Pe")
            ph.run()
        loc.close()

    def ev_act(self, ph, pe, ps_ap, out_ap, func, scale=1.0, bias=0.0, w=512):
        ev = self.bump("Ev")

        def f(q):
            q.wait_ge(self.sem("Pe"), pe)
            if func == AF.Copy:
                sc, bi = float(scale), float(bias)
            else:
                sc = IM(scale) if isinstance(scale, float) else scale
                bi = IM(bias) if isinstance(bias, float) else bias
            q.activation(out_ap, ps_ap[:, 0:w], func, scale=sc,
                         bias=bi).then_inc(self.sem("Ev"), 1)
        ph.s(f)
        return ev

    def ev_addx(self, ph, pe, ps_ap, x_ap, coef, w=512):
        ev = self.bump("Ev")

        def f(q):
            q.wait_ge(self.sem("Pe"), pe)
            q.scalar_tensor_tensor(x_ap, ps_ap[:, 0:w], float(coef), x_ap,
                                   ALU.mult, ALU.add).then_inc(self.sem("Ev"), 1)
        ph.v(f)
        return ev



def build_l3ffn(L=None):
    _own = L is None
    if _own:
        L = Launch("l3ffn")
    L.push_scope()
    x_in = L.dram_in("x_in", (DD, TC))
    x_out = L.dram_out("x_out", (DD, TC))
    fc1_q = L.dram_in("fc1_q", (1024, 4096), I32)
    fc1_s = L.dram_in("fc1_s", (8, 4096))
    fc1_b = L.dram_in("fc1_b", (4096,))
    fc2_q = L.dram_in("fc2_q", (4096, 1024), I32)
    fc2_s = L.dram_in("fc2_s", (32, 1024))
    fc2_b = L.dram_in("fc2_b", (1024,))
    gb = L.dram_in("gb", (128, 16))

    L.emit_consts()
    gb_sb = L.sb("gb", (128, 16), F32)
    ph = Ph(L)
    t = L.load(ph, gb_sb[:, :], gb)
    L.wait(ph, "s", "Ld", t)
    ph.run()

    CH = 512
    x_sb = [L.sb("x", (128, CH), F32) for _ in range(8)]
    x0 = [L.sb("x0", (128, CH), BF16) for _ in range(8)]
    xh = [L.sb("xh", (128, CH), BF16) for _ in range(8)]
    y_sb = [L.sb("y", (128, CH), BF16) for _ in range(32)]
    sq = L.sb("sq", (128, 8 * 512), F32)

    for ch in range(TC // CH):
        c0 = ch * CH
        ph = Ph(L)
        L.wait(ph, "y", "St", L.cnt("St"))
        tgts = [L.load(ph, x_sb[f][:, :], x_in[f * 128:(f + 1) * 128, c0:c0 + CH],
                       queue="sync") for f in range(8)]
        L.wait(ph, "v", "Ld", tgts[-1])
        ph.run()
        L.ln_window([x[:, :] for x in x_sb], CH, [x[:, :] for x in x0], sq[:, :])
        L.ln_window([x[:, :] for x in x0], CH, [x[:, :] for x in xh], sq[:, :],
                    in_bf16=True,
                    gamma_cols=[gb_sb[:, f:f + 1] for f in range(8)],
                    beta_cols=[gb_sb[:, 8 + f:8 + f + 1] for f in range(8)])
        ph = Ph(L)

        def fv(q):
            for f in range(8):
                q.scalar_tensor_tensor(x_sb[f][:, :], x0[f][:, :], 0.5,
                                       x_sb[f][:, :], ALU.mult, ALU.add)
        ph.v(fv)
        ph.run()
        def ev1(mi, c, ps_ap, ph, pe):
            return L.ev_act(ph, pe, ps_ap, y_sb[mi][:, c * 512:(c + 1) * 512],
                            AF.Silu)
        L.proj(rhs_tiles=[t[:, :] for t in xh], Tn=CH, K=1024, M=4096,
               w_dram=fc1_q, bias_dram=fc1_b, evac=ev1, mblock=512,
               dequant_scales=fc1_s)
        def ev2(mi, c, ps_ap, ph, pe):
            return L.ev_addx(ph, pe, ps_ap, x_sb[mi][:, c * 512:(c + 1) * 512],
                             0.25)
        L.proj(rhs_tiles=[t[:, :] for t in y_sb], Tn=CH, K=4096, M=1024,
               w_dram=fc2_q, bias_dram=fc2_b, evac=ev2, mblock=128,
               dequant_scales=fc2_s)
        ph = Ph(L)
        for f in range(8):
            L.store(ph, x_out[f * 128:(f + 1) * 128, c0:c0 + CH], x_sb[f][:, :])
        ph.run()
    L.pop_scope()
    if _own:
        L.finish()
    return L



def build_l2ffn(L=None):
    _own = L is None
    if _own:
        L = Launch("l2ffn")
    L.push_scope()
    x_in = L.dram_in("x_in", (DD, TC))
    x_out = L.dram_out("x_out", (DD, TC))
    w1 = L.dram_in("w1", (1024, 4096))
    w2 = L.dram_in("w2", (1024, 4096))
    h1 = L.dram_in("h1", (4096, 4096))
    h2 = L.dram_in("h2", (4096, 4096))
    w3 = L.dram_in("w3", (4096, 1024))
    scols = L.dram_in("scols", (128, 32 * 4 + 8))

    L.emit_consts()
    sc_sb = L.sb("scols", (128, 136), F32)
    ph = Ph(L)
    t = L.load(ph, sc_sb[:, :], scols)
    L.wait(ph, "s", "Ld", t)
    L.wait(ph, "v", "Ld", t)
    ph.run()

    def scol(group, mi):
        return sc_sb[:, group * 32 + mi:group * 32 + mi + 1]

    CH = 512
    x_sb = [L.sb("x", (128, CH), F32) for _ in range(8)]
    xh = [L.sb("xh", (128, CH), BF16) for _ in range(8)]
    hA = [L.sb("hA", (128, CH), BF16) for _ in range(32)]
    hB = [L.sb("hB", (128, CH), BF16) for _ in range(32)]
    sq = L.sb("sq", (128, 8 * 512), F32)

    for ch in range(TC // CH):
        c0 = ch * CH
        ph = Ph(L)
        L.wait(ph, "y", "St", L.cnt("St"))
        tgts = [L.load(ph, x_sb[f][:, :], x_in[f * 128:(f + 1) * 128, c0:c0 + CH],
                       queue="sync") for f in range(8)]
        L.wait(ph, "v", "Ld", tgts[-1])
        ph.run()
        L.ln_window([x[:, :] for x in x_sb], CH, [x[:, :] for x in xh], sq[:, :])

        def ev1(mi, c, ps_ap, ph, pe):
            return L.ev_act(ph, pe, ps_ap, hA[mi][:, c * 512:(c + 1) * 512],
                            AF.Silu, scale=scol(0, mi))
        L.proj(rhs_tiles=[t[:, :] for t in xh], Tn=CH, K=1024, M=4096,
               w_dram=w1, evac=ev1, mblock=512)

        def ev2(mi, c, ps_ap, ph, pe):
            ev = L.bump("Ev")

            def f(q, mi=mi, c=c, ps_ap=ps_ap, pe=pe):
                q.wait_ge(L.sem("Pe"), pe)
                q.scalar_tensor_tensor(
                    hB[mi][:, c * 512:(c + 1) * 512], ps_ap[:, 0:512],
                    scol(1, mi), hA[mi][:, c * 512:(c + 1) * 512],
                    ALU.mult, ALU.mult).then_inc(L.sem("Ev"), 1)
            ph.v(f)
            return ev
        L.proj(rhs_tiles=[t[:, :] for t in xh], Tn=CH, K=1024, M=4096,
               w_dram=w2, evac=ev2, mblock=512)

        def ev3(mi, c, ps_ap, ph, pe):
            return L.ev_act(ph, pe, ps_ap, hA[mi][:, c * 512:(c + 1) * 512],
                            AF.Gelu, scale=scol(2, mi))
        L.proj(rhs_tiles=[t[:, :] for t in hB], Tn=CH, K=4096, M=4096,
               w_dram=h1, evac=ev3, mblock=256)

        def ev4(mi, c, ps_ap, ph, pe):
            return L.ev_act(ph, pe, ps_ap, hB[mi][:, c * 512:(c + 1) * 512],
                            AF.Gelu, scale=scol(3, mi))
        L.proj(rhs_tiles=[t[:, :] for t in hA], Tn=CH, K=4096, M=4096,
               w_dram=h2, evac=ev4, mblock=256)

        def ev5(mi, c, ps_ap, ph, pe):
            ev = L.bump("Ev")

            def f(q, mi=mi, c=c, ps_ap=ps_ap, pe=pe):
                q.wait_ge(L.sem("Pe"), pe)
                sl = slice(c * 512, (c + 1) * 512)
                q.tensor_scalar(hA[mi][:, sl], ps_ap[:, 0:512],
                                sc_sb[:, 128 + mi:128 + mi + 1], None, ALU.mult)
                q.scalar_tensor_tensor(
                    x_sb[mi][:, sl], hA[mi][:, sl], 0.5, x_sb[mi][:, sl],
                    ALU.mult, ALU.add).then_inc(L.sem("Ev"), 1)
            ph.v(f)
            return ev
        L.proj(rhs_tiles=[t[:, :] for t in hB], Tn=CH, K=4096, M=1024,
               w_dram=w3, evac=ev5, mblock=512)

        ph = Ph(L)
        for f in range(8):
            L.store(ph, x_out[f * 128:(f + 1) * 128, c0:c0 + CH], x_sb[f][:, :])
        ph.run()
    L.pop_scope()
    if _own:
        L.finish()
    return L


def build_l0ffn(L=None):
    _own = L is None
    if _own:
        L = Launch("l0ffn")
    L.push_scope()
    x_in = L.dram_in("x_in", (DD, TC))
    x_out = L.dram_out("x_out", (DD, TC))
    wi = L.dram_in("wi", (1024, 4096))
    bi = L.dram_in("bi", (4096,))
    wo = L.dram_in("wo", (4096, 1024))
    bo = L.dram_in("bo", (1024,))
    gw = L.dram_in("gw", (1024, 8))
    gb = L.dram_in("gbias", (1, 8))
    A_sb_d = L.dram_in("A_sb", (128, 1024))
    B_d = L.dram_in("B_all", (8 * 16, 4096))

    L.emit_consts()
    gw_sb = L.sb("gw", (128, 64), BF16)
    gb_sb = L.sb("gbias", (1, 8), F32)
    A_sb = L.sb("A", (128, 1024), BF16)
    oneh_sb = L.sb("oneh", (8, 8 * 128), BF16)
    oneh_d = L.dram_in("ONEHOT8", (8, 8 * 128))
    ph = Ph(L)
    L.load(ph, oneh_sb[:, :], oneh_d)
    L.load(ph, gw_sb[:, :].rearrange("p (k m) -> p k m", k=8),
           gw.rearrange("(k p) m -> p k m", p=128))
    L.load(ph, gb_sb[:, :], gb)
    t = L.load(ph, A_sb[:, :], A_sb_d)
    L.wait(ph, "v", "Ld", t)
    L.wait(ph, "s", "Ld", t)
    ph.run()

    CH = 512
    x_sb = [L.sb("x", (128, CH), F32) for _ in range(8)]
    xh = [L.sb("xh", (128, CH), BF16) for _ in range(8)]
    base = [L.sb("base", (128, CH), BF16) for _ in range(32)]
    hsum = [L.sb("hsum", (128, CH), BF16) for _ in range(32)]
    gbc = [L.sb("gbc", (128, CH), BF16) for _ in range(8)]
    a_e = [L.sb("a", (32, CH), BF16) for _ in range(8)]
    r_sb = [L.sb("r", (128, CH), BF16) for _ in range(2)]
    bsl = [L.sb("bsl", (16, 4096), BF16) for _ in range(2)]
    sq = L.sb("sq", (128, 8 * 512), F32)
    tkA = L.sb("tkA", (128, 8), F32)
    tkB128 = L.sb("tkB128", (128, 128), F32)
    tkC = L.sb("tkC", (128, 1), F32)
    m1c = L.sb("m1c", (128, 1), F32)
    m2c = L.sb("m2c", (128, 1), F32)
    g8T = L.sb("g8T", (8, CH), BF16)
    tkD = L.sb("tkD", (128, 8), F32)
    tkE = L.sb("tkE", (128, 8), F32)
    tkCn = L.sb("tkCn", (128, 1), F32)
    tkR = L.sb("tkR", (128, 1), F32)
    vs = L.sb("vs", (128, 1), F32)

    dbg = {}
    if DEBUG_L0:
        for nm, shp in [("d_xh", (128, 512)), ("d_base", (128, 512)),
                        ("d_hsum", (128, 512)), ("d_gT", (8, 512)),
                        ("d_gbc", (128, 512)), ("d_a0", (32, 512)),
                        ("d_tkA", (128, 8)), ("d_tkB", (128, 8)),
                        ("d_psr", (128, 8)), ("d_tkC", (128, 1)),
                        ("d_m1", (128, 1)), ("d_m2", (128, 1))]:
            dbg[nm] = L.dram_out(nm, shp)

    ph = Ph(L)

    def fz(q):
        for e in range(8):
            q.memset(a_e[e][:, :], 0.0)
        q.memset(tkB128[:, :], 0.0)
    ph.g(fz)
    ph.run()

    for ch in range(TC // CH):
        c0 = ch * CH
        ph = Ph(L)
        L.wait(ph, "y", "St", L.cnt("St"))
        tgts = [L.load(ph, x_sb[f][:, :], x_in[f * 128:(f + 1) * 128, c0:c0 + CH],
                       queue="sync") for f in range(8)]
        L.wait(ph, "v", "Ld", tgts[-1])
        ph.run()
        L.ln_window([x[:, :] for x in x_sb], CH, [x[:, :] for x in xh], sq[:, :])

        loc = contextlib.ExitStack()
        psr = L.ps("psr", (128, 8), stack=loc)
        pst = L.ps("pst", (128, 128), stack=loc)
        for tt in range(CH // 128):
            ph = Ph(L)

            def ft(q, tt=tt):
                for k in range(8):
                    q.matmul(psr[:, :], xh[k][:, tt * 128:(tt + 1) * 128],
                             gw_sb[:, k * 8:(k + 1) * 8],
                             start=(k == 0), stop=False)
                q.matmul(psr[:, :], L.ones_r128_f[:, :], gb_sb[:, :],
                         start=False, stop=True).then_inc(L.sem("Pe"), 1)
            pe = L.bump("Pe")
            ph.t(ft)

            def fv(q, tt=tt, pe=pe):
                q.wait_ge(L.sem("Pe"), pe)
                q.tensor_copy(tkD[:, :], psr[:, :])
                L.vsync(q)
                q.reduce_max(tkC[:, :], tkD[:, :], axis=AX.X)
                L.vsync(q)
                q.tensor_scalar(tkCn[:, :], tkC[:, :], -1.0, None,
                                ALU.mult).then_inc(L.sem("Ve"), 1)
            ve = L.bump("Ve")
            ph.v(fv)

            def fs(q, tt=tt, ve=ve):
                q.wait_ge(L.sem("Ve"), ve)
                q.activation(tkA[:, :], tkD[:, :], AF.Exp,
                             bias=tkCn[:, :]).then_inc(L.sem("Ac"), 1)
            ac = L.bump("Ac")
            ph.s(fs)

            def fv2(q, tt=tt, ac=ac):
                q.wait_ge(L.sem("Ac"), ac)
                q.reduce_sum(tkC[:, :], tkA[:, :], axis=AX.X)
                L.vsync(q)
                q.reciprocal(tkR[:, :], tkC[:, :])
                L.vsync(q)
                q.tensor_scalar(tkE[:, :], tkA[:, :], tkR[:, :], None, ALU.mult)
                L.vsync(q)
                q.reduce_max(m1c[:, :], tkE[:, :], axis=AX.X)
                L.vsync(q)
                tkB = tkB128[:, 0:8]
                q.tensor_scalar(tkA[:, :], tkE[:, :], m1c[:, :], None, ALU.is_ge)
                L.vsync(q)
                q.tensor_tensor(tkB, tkE[:, :], tkA[:, :], ALU.subtract)
                L.vsync(q)
                q.reduce_max(m2c[:, :], tkB, axis=AX.X)
                L.vsync(q)
                q.tensor_tensor(vs[:, :], m1c[:, :], m2c[:, :], ALU.add)
                q.tensor_scalar(tkA[:, :], tkE[:, :], m2c[:, :], None, ALU.is_ge)
                L.vsync(q)
                q.tensor_scalar(tkB, tkA[:, :], vs[:, :], None,
                                ALU.mult).then_inc(L.sem("Ve"), 1)
            ve2 = L.bump("Ve")
            ph.v(fv2)

            def ft2(q, tt=tt, ve2=ve2):
                q.wait_ge(L.sem("Ve"), ve2)
                if tt > 0:
                    q.wait_ge(L.sem("Ve"), L._gt_ve[tt - 1])
                q.transpose(pst[:, :], tkB128[:, :], L.ident_f[:, :]).then_inc(
                    L.sem("Pe"), 1)
            pe2 = L.bump("Pe")
            ph.t(ft2)

            def fv3(q, tt=tt, pe2=pe2):
                q.wait_ge(L.sem("Pe"), pe2)
                q.tensor_copy(g8T[:, tt * 128:(tt + 1) * 128],
                              pst[0:8, :]).then_inc(L.sem("Ve"), 1)
            L._gt_ve = getattr(L, "_gt_ve", {})
            L._gt_ve[tt] = L.bump("Ve")
            ph.v(fv3)
            ph.run()
        loc.close()

        loc = contextlib.ExitStack()
        psb = [L.ps("psb", (128, 512), stack=loc) for _ in range(2)]
        ph = Ph(L)
        for e in range(8):
            def ft(q, e=e):
                if e >= 2:
                    q.wait_ge(L.sem("Ac"), L._gbc_ac[e - 2])
                mm = None
                for tt in range(CH // 128):
                    mm = q.matmul(psb[e % 2][:, tt * 128:(tt + 1) * 128],
                                  oneh_sb[:, e * 128:(e + 1) * 128],
                                  g8T[:, tt * 128:(tt + 1) * 128],
                                  start=True, stop=True)
                mm.then_inc(L.sem("Pe"), 1)
            pe = L.bump("Pe")
            ph.t(ft)

            def fs(q, e=e, pe=pe):
                q.wait_ge(L.sem("Pe"), pe)
                q.activation(gbc[e][:, :], psb[e % 2][:, :], AF.Copy).then_inc(
                    L.sem("Ac"), 1)
            L._gbc_ac = getattr(L, "_gbc_ac", {})
            L._gbc_ac[e] = L.bump("Ac")
            ph.s(fs)
        ph.run()
        loc.close()

        loc = contextlib.ExitStack()
        psa = [L.ps("psa", (16, 512), stack=loc) for _ in range(2)]
        ph = Ph(L)
        for e in range(8):
            def ft(q, e=e):
                if e >= 2:
                    q.wait_ge(L.sem("Ac"), L._ae_ac[e - 2])
                mm = None
                for k in range(8):
                    mm = q.matmul(psa[e % 2][:, :],
                                  A_sb[:, k * 128 + e * 16:k * 128 + e * 16 + 16],
                                  xh[k][:, 0:512], start=(k == 0), stop=(k == 7))
                mm.then_inc(L.sem("Pe"), 1)
            pe = L.bump("Pe")
            ph.t(ft)

            def fs(q, e=e, pe=pe):
                q.wait_ge(L.sem("Pe"), pe)
                q.activation(a_e[e][0:16, :], psa[e % 2][:, :], AF.Copy).then_inc(
                    L.sem("Ac"), 1)
            L._ae_ac = getattr(L, "_ae_ac", {})
            L._ae_ac[e] = L.bump("Ac")
            ph.s(fs)
        ph.run()
        loc.close()

        def evb(mi, c, ps_ap, ph, pe):
            return L.ev_act(ph, pe, ps_ap, base[mi][:, c * 512:(c + 1) * 512],
                            AF.Copy)
        L.proj(rhs_tiles=[t[:, :] for t in xh], Tn=CH, K=1024, M=4096,
               w_dram=wi, bias_dram=bi, evac=evb, mblock=512)

        loc = contextlib.ExitStack()
        pse = [L.ps("pse", (128, 512), stack=loc) for _ in range(4)]
        for e in range(8):
            ph = Ph(L)
            bt = L.load(ph, bsl[e % 2][:, :],
                        B_d[e * 16:(e + 1) * 16, :])
            for m in range(32):
                bank = m % 4

                def ft(q, e=e, m=m, bank=bank, bt=bt):
                    if m == 0:
                        q.wait_ge(L.sem("Ld"), bt)
                    if m >= 4:
                        q.wait_ge(L.sem("Ac"), L._ex_ac[(e, m - 4)])
                    q.matmul(pse[bank][:, :], L.ident_b[:, :], base[m][:, 0:512],
                             start=True, stop=False)
                    q.matmul(pse[bank][:, :], bsl[e % 2][0:16, m * 128:(m + 1) * 128],
                             a_e[e][0:16, 0:512],
                             start=False, stop=True).then_inc(L.sem("Pe"), 1)
                pe = L.bump("Pe")
                ph.t(ft)

                def fs(q, e=e, m=m, bank=bank, pe=pe):
                    q.wait_ge(L.sem("Pe"), pe)
                    if m >= 2:
                        q.wait_ge(L.sem("Ve"), L._ex_ve[(e, m - 2)])
                    q.activation(r_sb[m % 2][:, :], pse[bank][:, :],
                                 AF.Relu).then_inc(L.sem("Ac"), 1)
                L._ex_ac = getattr(L, "_ex_ac", {})
                L._ex_ac[(e, m)] = L.bump("Ac")
                ph.s(fs)

                def fv(q, e=e, m=m, pe=pe):
                    q.wait_ge(L.sem("Ac"), L._ex_ac[(e, m)])
                    if e == 0:
                        q.tensor_tensor(hsum[m][:, 0:512], r_sb[m % 2][:, :],
                                        gbc[e][:, :], ALU.mult).then_inc(
                                            L.sem("Ve"), 1)
                    else:
                        q.tensor_tensor(r_sb[m % 2][:, :], r_sb[m % 2][:, :],
                                        gbc[e][:, :], ALU.mult)
                        q.tensor_tensor(hsum[m][:, 0:512], hsum[m][:, 0:512],
                                        r_sb[m % 2][:, :], ALU.add).then_inc(
                                            L.sem("Ve"), 1)
                L._ex_ve = getattr(L, "_ex_ve", {})
                L._ex_ve[(e, m)] = L.bump("Ve")
                ph.v(fv)
            ph.run()
        loc.close()

        if DEBUG_L0 and ch == 0:
            ph = Ph(L)
            L.store(ph, dbg["d_xh"], xh[0][:, :], queue="gpsimd")
            L.store(ph, dbg["d_base"], base[0][:, :], queue="gpsimd")
            L.store(ph, dbg["d_hsum"], hsum[0][:, :], queue="gpsimd")
            L.store(ph, dbg["d_gT"], g8T[:, 0:512], queue="gpsimd")
            L.store(ph, dbg["d_gbc"], gbc[0][:, :], queue="gpsimd")
            L.store(ph, dbg["d_a0"], a_e[0][:, :], queue="gpsimd")
            L.store(ph, dbg["d_tkA"], tkA[:, :], queue="gpsimd")
            L.store(ph, dbg["d_tkB"], tkB128[:, 0:8], queue="gpsimd")
            L.store(ph, dbg["d_psr"], tkD[:, :], queue="gpsimd")
            L.store(ph, dbg["d_tkC"], tkC[:, :], queue="gpsimd")
            L.store(ph, dbg["d_m1"], m1c[:, :], queue="gpsimd")
            L.store(ph, dbg["d_m2"], m2c[:, :], queue="gpsimd")
            L.wait(ph, "y", "St", L.cnt("St"))
            ph.run()

        def evo(mi, c, ps_ap, ph, pe):
            return L.ev_addx(ph, pe, ps_ap, x_sb[mi][:, c * 512:(c + 1) * 512], 0.5)
        L.proj(rhs_tiles=[t[:, :] for t in hsum], Tn=CH, K=4096, M=1024,
               w_dram=wo, bias_dram=bo, evac=evo, mblock=128)

        ph = Ph(L)
        for f in range(8):
            L.store(ph, x_out[f * 128:(f + 1) * 128, c0:c0 + CH], x_sb[f][:, :])
        ph.run()
    L.pop_scope()
    if _own:
        L.finish()
    return L


def build_l1ffn(L=None):
    _own = L is None
    if _own:
        L = Launch("l1ffn")
    L.push_scope()
    x_in = L.dram_in("x_in", (DD, TC))
    x_out = L.dram_out("x_out", (DD, TC))
    bd1 = L.dram_in("bd1", (4 * 8 * 128, 128))
    bd2 = L.dram_in("bd2", (4 * 32 * 128, 128))
    cw = L.dram_in("cw", (16384, 1024))

    L.emit_consts()
    CH = 512
    x_sb = [L.sb("x", (128, CH), F32) for _ in range(8)]
    xh = [L.sb("xh", (128, CH), BF16) for _ in range(8)]
    yh = [L.sb("yh", (128, CH), BF16) for _ in range(8)]
    hz = [L.sb("hz", (128, CH), BF16) for _ in range(32)]
    acc = [L.sb("acc", (128, CH), F32) for _ in range(8)]
    sq = L.sb("sq", (128, 8 * 512), F32)
    bd1_sb = L.sb("bd1", (128, 32 * 128), BF16)
    bd2_sb = [L.sb("bd2", (128, 32 * 128), BF16) for _ in range(2)]
    csl = [L.sb("csl", (128, 4 * 1024), BF16) for _ in range(2)]

    ph = Ph(L)
    t = L.load(ph, bd1_sb[:, :].rearrange("p (k m) -> p k m", k=32),
               bd1.rearrange("(k p) m -> p k m", p=128))
    ph.run()

    for ch in range(TC // CH):
        c0 = ch * CH
        ph = Ph(L)
        L.wait(ph, "y", "St", L.cnt("St"))
        tgts = [L.load(ph, x_sb[f][:, :], x_in[f * 128:(f + 1) * 128, c0:c0 + CH],
                       queue="sync") for f in range(8)]
        L.wait(ph, "v", "Ld", tgts[-1])
        if ch == 0:
            L.wait(ph, "t", "Ld", t)
        ph.run()
        L.ln_window([x[:, :] for x in x_sb], CH, [x[:, :] for x in xh], sq[:, :])

        ph = Ph(L)

        def fz(q):
            for f in range(8):
                q.memset(acc[f][:, :], 0.0)
        ph.g(fz)
        ph.run()

        for h in range(4):
            loc = contextlib.ExitStack()
            ph = Ph(L)
            bt = L.load(ph, bd2_sb[h % 2][:, :].rearrange("p (k m) -> p k m", k=32),
                        bd2[h * 32 * 128:(h + 1) * 32 * 128, :].rearrange(
                            "(k p) m -> p k m", p=128))
            ph.run()

            psy = [L.ps("psy", (128, 512), stack=loc) for _ in range(4)]
            ph = Ph(L)
            for f in range(8):
                for cc in range(CH // 512):
                    bank = (f * 2 + cc) % 4

                    def ft(q, f=f, cc=cc, bank=bank, h=h):
                        if f >= 2:
                            q.wait_ge(L.sem("Ac"), L._y_ac[(h, f - 2, cc)])
                        q.matmul(psy[bank][:, :],
                                 bd1_sb[:, (h * 8 + f) * 128:(h * 8 + f + 1) * 128],
                                 xh[f][:, cc * 512:(cc + 1) * 512],
                                 start=True, stop=True).then_inc(L.sem("Pe"), 1)
                    pe = L.bump("Pe")
                    ph.t(ft)

                    def fs(q, f=f, cc=cc, bank=bank, h=h, pe=pe):
                        q.wait_ge(L.sem("Pe"), pe)
                        q.activation(yh[f][:, cc * 512:(cc + 1) * 512],
                                     psy[bank][:, :], AF.Copy).then_inc(
                                         L.sem("Ac"), 1)
                    L._y_ac = getattr(L, "_y_ac", {})
                    L._y_ac[(h, f, cc)] = L.bump("Ac")
                    ph.s(fs)
            for m in range(32):
                for cc in range(CH // 512):
                    bank = (m * 2 + cc) % 4

                    def ft2(q, m=m, cc=cc, bank=bank, h=h, bt=bt):
                        if m == 0 and cc == 0:
                            q.wait_ge(L.sem("Ld"), bt)
                            q.wait_ge(L.sem("Ac"), L._y_ac[(h, 7, CH // 512 - 1)])
                        if True:
                            q.wait_ge(L.sem("Ac"), L._z_ac.get((h, m - 2, cc), 0))                                 if m >= 2 else None
                        q.matmul(psy[bank][:, :],
                                 bd2_sb[h % 2][:, m * 128:(m + 1) * 128],
                                 yh[m // 4][:, cc * 512:(cc + 1) * 512],
                                 start=True, stop=True).then_inc(L.sem("Pe"), 1)
                    pe = L.bump("Pe")
                    ph.t(ft2)

                    def fs2(q, m=m, cc=cc, bank=bank, h=h, pe=pe):
                        q.wait_ge(L.sem("Pe"), pe)
                        q.activation(hz[m][:, cc * 512:(cc + 1) * 512],
                                     psy[bank][:, :], AF.Gelu).then_inc(
                                         L.sem("Ac"), 1)
                    L._z_ac = getattr(L, "_z_ac", {})
                    L._z_ac[(h, m, cc)] = L.bump("Ac")
                    ph.s(fs2)
            ph.run()

            pso = [L.ps("pso", (128, 512), stack=loc) for _ in range(4)]
            slab_t = {}
            bank_ve = {}
            ph = Ph(L)
            for sg in range(8):
                def fg(q, sg=sg, h=h):
                    if sg >= 2:
                        q.wait_ge(L.sem("Pe"), L._cp_pe[(h, sg - 2)])
                ph.g(fg)
                slab_t[sg] = L.load(
                    ph, csl[sg % 2][:, :].rearrange("p (k m) -> p k m", k=4),
                    cw[h * 4096 + sg * 512:h * 4096 + (sg + 1) * 512, :].rearrange(
                        "(k p) m -> p k m", p=128))
            for sg in range(8):
                for mo in range(8):
                    for cc in range(CH // 512):
                        bank = (mo * (CH // 512) + cc) % 4

                        def ft3(q, sg=sg, mo=mo, cc=cc, bank=bank, h=h,
                                bt2=slab_t[sg], evprev=bank_ve.get(bank)):
                            if mo == 0 and cc == 0:
                                q.wait_ge(L.sem("Ld"), bt2)
                            if evprev:
                                q.wait_ge(L.sem("Ve"), evprev)
                            mm = None
                            for a in range(4):
                                k = sg * 4 + a
                                mm = q.matmul(
                                    pso[bank][:, :],
                                    csl[sg % 2][:, a * 1024 + mo * 128:a * 1024 + (mo + 1) * 128],
                                    hz[k][:, cc * 512:(cc + 1) * 512],
                                    start=(a == 0), stop=(a == 3))
                            mm.then_inc(L.sem("Pe"), 1)
                        pe = L.bump("Pe")
                        ph.t(ft3)

                        def fv(q, mo=mo, cc=cc, bank=bank, pe=pe):
                            q.wait_ge(L.sem("Pe"), pe)
                            q.tensor_tensor(acc[mo][:, cc * 512:(cc + 1) * 512],
                                            acc[mo][:, cc * 512:(cc + 1) * 512],
                                            pso[bank][:, :], ALU.add).then_inc(
                                                L.sem("Ve"), 1)
                        bank_ve[bank] = L.bump("Ve")
                        ph.v(fv)
                L._cp_pe = getattr(L, "_cp_pe", {})
                L._cp_pe[(h, sg)] = L.cnt("Pe")
            ph.run()
            loc.close()

        ph = Ph(L)

        def fv4(q):
            for f in range(8):
                q.scalar_tensor_tensor(x_sb[f][:, :], acc[f][:, :], 0.5,
                                       x_sb[f][:, :], ALU.mult, ALU.add)
        ph.v(fv4)
        ph.run()
        ph = Ph(L)
        for f in range(8):
            L.store(ph, x_out[f * 128:(f + 1) * 128, c0:c0 + CH], x_sb[f][:, :])
        ph.run()
    L.pop_scope()
    if _own:
        L.finish()
    return L



def build_l2attnA(L=None):
    _own = L is None
    if _own:
        L = Launch("l2attnA")
    L.push_scope()
    x_in = L.dram_in("x_in", (DD, TC))
    if DEBUG_L2A:
        d_n2a = L.dram_out("d_n2a", (128, 512))
        d_n2b = L.dram_out("d_n2b", (128, 512))
        d_n2c = L.dram_out("d_n2c", (128, 512))
        d_keys = L.dram_out("d_keys", (128, 512))
    wk = L.dram_in("wk", (1024, 1024))
    bk = L.dram_in("bk", (1024,))
    wv = L.dram_in("wv", (1024, 1024))
    bv = L.dram_in("bv", (1024,))
    bound_o = L.dram_out("bound", (DD, TC))
    kn_o = L.dram_out("kn", (DD, TC), BF16)

    L.emit_consts()
    CH = 512
    x_sb = [L.sb("x", (128, CH), F32) for _ in range(8)]
    xh = [L.sb("xh", (128, CH), BF16) for _ in range(8)]
    kf = [L.sb("kf", (128, CH), F32) for _ in range(8)]
    vf = [L.sb("vf", (128, CH), F32) for _ in range(8)]
    kn = [L.sb("kn", (128, CH), F32) for _ in range(8)]
    bnd = [L.sb("bnd", (128, CH), F32) for _ in range(8)]
    n2 = [L.sb("n2", (128, CH), F32) for _ in range(2)]
    tq = [L.sb("tq", (128, CH), F32) for _ in range(4)]
    sq = L.sb("sq", (128, 8 * 512), F32)

    for ch in range(TC // CH):
        c0 = ch * CH
        ph = Ph(L)
        L.wait(ph, "y", "St", L.cnt("St"))
        tgts = [L.load(ph, x_sb[f][:, :], x_in[f * 128:(f + 1) * 128, c0:c0 + CH],
                       queue="sync") for f in range(8)]
        L.wait(ph, "v", "Ld", tgts[-1])
        ph.run()
        L.ln_window([x[:, :] for x in x_sb], CH, [x[:, :] for x in xh], sq[:, :])

        def evk(mi, c, ps_ap, ph, pe):
            return L.ev_act(ph, pe, ps_ap, kf[mi][:, c * 512:(c + 1) * 512], AF.Copy)
        L.proj(rhs_tiles=[t[:, :] for t in xh], Tn=CH, K=1024, M=1024,
               w_dram=wk, bias_dram=bk, evac=evk, mblock=512)

        def evv(mi, c, ps_ap, ph, pe):
            return L.ev_act(ph, pe, ps_ap, vf[mi][:, c * 512:(c + 1) * 512], AF.Copy)
        L.proj(rhs_tiles=[t[:, :] for t in xh], Tn=CH, K=1024, M=1024,
               w_dram=wv, bias_dram=bv, evac=evv, mblock=512)

        ph = Ph(L)

        def fv(q):
            last = None
            for h in range(2):
                kr, ki, kj, kk = kf[h], kf[2 + h], kf[4 + h], kf[6 + h]
                vr, vi, vj, vk = vf[h], vf[2 + h], vf[4 + h], vf[6 + h]
                q.tensor_tensor(n2[h][:, :], kr[:, :], kr[:, :], ALU.mult)
                q.tensor_tensor(tq[0][:, :], ki[:, :], ki[:, :], ALU.mult)
                q.tensor_tensor(tq[1][:, :], kj[:, :], kj[:, :], ALU.mult)
                q.tensor_tensor(tq[2][:, :], kk[:, :], kk[:, :], ALU.mult)
                q.tensor_tensor(n2[h][:, :], n2[h][:, :], tq[0][:, :], ALU.add)
                q.tensor_tensor(tq[1][:, :], tq[1][:, :], tq[2][:, :], ALU.add)
                last = q.tensor_tensor(n2[h][:, :], n2[h][:, :], tq[1][:, :],
                                       ALU.add)
            last.then_inc(L.sem("Ve"), 1)
        ve = L.bump("Ve")
        ph.v(fv)

        def fs(q, ve=ve):
            q.wait_ge(L.sem("Ve"), ve)
            q.activation(n2[0][:, :], n2[0][:, :], AF.Sqrt)
            q.activation(n2[1][:, :], n2[1][:, :], AF.Sqrt).then_inc(
                L.sem("Ac"), 1)
        ac = L.bump("Ac")
        ph.s(fs)

        def fv2(q, ac=ac):
            q.wait_ge(L.sem("Ac"), ac)
            for h in range(2):
                q.tensor_scalar(tq[h][:, :], n2[h][:, :], 1e-12, None, ALU.max)
            L.vsync(q)
            for h in range(2):
                q.reciprocal(n2[h][:, :], tq[h][:, :])
            L.vsync(q)
            for h in range(2):
                for j in range(4):
                    q.tensor_tensor(kn[2 * j + h][:, :], kf[2 * j + h][:, :],
                                    n2[h][:, :], ALU.mult)
            for h in range(2):
                kr, ki, kj, kk = kn[h], kn[2 + h], kn[4 + h], kn[6 + h]
                vr, vi, vj, vk = vf[h], vf[2 + h], vf[4 + h], vf[6 + h]

                def qm(dst, s1, a1, b1, s2, a2, b2, s3, a3, b3, s4, a4, b4):
                    q.tensor_tensor(tq[0][:, :], a1[:, :], b1[:, :], ALU.mult)
                    q.tensor_tensor(tq[1][:, :], a2[:, :], b2[:, :], ALU.mult)
                    q.tensor_tensor(tq[2][:, :], a3[:, :], b3[:, :], ALU.mult)
                    q.tensor_tensor(tq[3][:, :], a4[:, :], b4[:, :], ALU.mult)
                    q.tensor_tensor(tq[0][:, :], tq[0][:, :], tq[1][:, :],
                                    ALU.add if s2 > 0 else ALU.subtract)
                    q.tensor_tensor(tq[2][:, :], tq[2][:, :], tq[3][:, :],
                                    ALU.add if (s4 * s3 > 0) else ALU.subtract)
                    if s3 > 0:
                        q.tensor_tensor(dst[:, :], tq[0][:, :], tq[2][:, :], ALU.add)
                    else:
                        q.tensor_tensor(dst[:, :], tq[0][:, :], tq[2][:, :],
                                        ALU.subtract)
                qm(bnd[h], 1, vr, kr, -1, vi, ki, -1, vj, kj, -1, vk, kk)
                qm(bnd[2 + h], 1, vr, ki, 1, vi, kr, 1, vj, kk, -1, vk, kj)
                qm(bnd[4 + h], 1, vr, kj, -1, vi, kk, 1, vj, kr, 1, vk, ki)
                qm(bnd[6 + h], 1, vr, kk, 1, vi, kj, -1, vj, ki, 1, vk, kr)
            q.engine_nop().then_inc(L.sem("Ve"), 1)
        ve2 = L.bump("Ve")
        ph.v(fv2)
        ph.run()

        if DEBUG_L2A and ch == 0:
            ph = Ph(L)
            L.store(ph, d_n2c, n2[0][:, :], queue="gpsimd")
            L.store(ph, d_keys, kf[0][:, :], queue="gpsimd")
            L.store(ph, d_n2a, kn[0][:, :], queue="gpsimd")
            L.store(ph, d_n2b, bnd[0][:, :], queue="gpsimd")
            L.wait(ph, "y", "St", L.cnt("St"))
            ph.run()

        ph = Ph(L)
        L.wait(ph, "g", "Ve", ve2)
        for f in range(8):
            L.store(ph, bound_o[f * 128:(f + 1) * 128, c0:c0 + CH], bnd[f][:, :])
            L.store(ph, kn_o[f * 128:(f + 1) * 128, c0:c0 + CH], kn[f][:, :],
                    queue="gpsimd")
        ph.run()
    L.pop_scope()
    if _own:
        L.finish()
    return L


def build_l2attnB(L=None):
    _own = L is None
    if _own:
        L = Launch("l2attnB")
    L.push_scope()
    bound_i = L.dram_in("bound", (256, SS))
    kn_i = L.dram_in("kn", (256, SS), BF16)
    rsq_i = L.dram_in("rsq", (1, SS))
    retr_o = L.dram_out("retr", (256, SS))
    L.emit_consts()
    CH = 2048
    bnd = [L.sb("bnd", (64, CH), F32) for _ in range(4)]
    knt = [L.sb("knt", (64, CH), BF16) for _ in range(4)]
    mem = [L.sb("mem", (64, CH), F32) for _ in range(4)]
    ret = [L.sb("ret", (64, CH), F32) for _ in range(4)]
    carry = L.sb("carry", (64, 4), F32)
    onesb = L.sb("onesb", (64, CH), F32)
    rsq = L.sb("rsq", (1, CH), F32)
    rsqb = L.sb("rsqb", (64, CH), F32)
    tq = [L.sb("tq", (64, CH), F32) for _ in range(4)]
    psb = [L.ps("psb", (64, 512)) for _ in range(2)]

    ph = Ph(L)

    def fz(q):
        q.memset(onesb[:, :], 1.0)
        q.memset(carry[:, :], 0.0)
    ph.g(fz)
    ph.run()

    for ch in range(SS // CH):
        c0 = ch * CH
        ph = Ph(L)
        L.wait(ph, "y", "St", L.cnt("St"))
        for j in range(4):
            L.load(ph, bnd[j][:, :], bound_i[j * 64:(j + 1) * 64, c0:c0 + CH],
                   queue="sync")
            L.load(ph, knt[j][:, :], kn_i[j * 64:(j + 1) * 64, c0:c0 + CH],
                   queue="gpsimd")
        L.load(ph, rsq[0:1, :], rsq_i[0:1, c0:c0 + CH], queue="gpsimd")
        L.wait(ph, "v", "Ld", L.cnt("Ld"))
        ph.run()

        ph = Ph(L)
        for cc in range(CH // 512):
            def ft(q, cc=cc):
                if cc >= 2:
                    q.wait_ge(L.sem("Ve"), L._rsq_ve[cc - 2])
                q.matmul(psb[cc % 2][:, :], L.ones_r128_f[0:1, 0:64],
                         rsq[0:1, cc * 512:(cc + 1) * 512],
                         start=True, stop=True).then_inc(L.sem("Pe"), 1)
            pe = L.bump("Pe")
            ph.t(ft)

            def fv0(q, cc=cc, pe=pe):
                q.wait_ge(L.sem("Pe"), pe)
                q.tensor_copy(rsqb[:, cc * 512:(cc + 1) * 512],
                              psb[cc % 2][:, :]).then_inc(L.sem("Ve"), 1)
            L._rsq_ve = getattr(L, "_rsq_ve", {})
            L._rsq_ve[cc] = L.bump("Ve")
            ph.v(fv0)
        ph.run()

        ph = Ph(L)

        def fv(q, ch=ch):
            for j in range(4):
                q.tensor_tensor_scan(mem[j][:, :], onesb[:, :], bnd[j][:, :],
                                     carry[:, j:j + 1], ALU.mult, ALU.add)
            L.vsync(q)
            for j in range(4):
                q.tensor_copy(carry[:, j:j + 1], mem[j][:, CH - 1:CH])
            L.vsync(q)
            mr, mi, mj, mk = (mem[j][:, :] for j in range(4))
            kr, ki, kj, kk = (knt[j][:, :] for j in range(4))
            rr, ri, rj, rk = (ret[j][:, :] for j in range(4))

            def qm(dst, terms):
                for i, (s, aa, bb) in enumerate(terms):
                    q.tensor_tensor(tq[i][:, :], aa, bb, ALU.mult)
                (s1, _, _), (s2, _, _), (s3, _, _), (s4, _, _) = terms
                q.tensor_tensor(tq[0][:, :], tq[0][:, :], tq[1][:, :],
                                ALU.add if s1 * s2 > 0 else ALU.subtract)
                q.tensor_tensor(tq[2][:, :], tq[2][:, :], tq[3][:, :],
                                ALU.add if s3 * s4 > 0 else ALU.subtract)
                q.tensor_tensor(dst, tq[0][:, :], tq[2][:, :],
                                ALU.add if s1 * s3 > 0 else ALU.subtract)
                if s1 < 0:
                    q.tensor_scalar(dst, dst, -1.0, None, ALU.mult)
            qm(rr, [(1, mr, kr), (1, mi, ki), (1, mj, kj), (1, mk, kk)])
            qm(ri, [(-1, mr, ki), (1, mi, kr), (-1, mj, kk), (1, mk, kj)])
            qm(rj, [(-1, mr, kj), (1, mi, kk), (1, mj, kr), (-1, mk, ki)])
            qm(rk, [(-1, mr, kk), (-1, mi, kj), (1, mj, ki), (1, mk, kr)])
            for j in range(4):
                q.tensor_tensor(ret[j][:, :], ret[j][:, :], rsqb[:, :],
                                ALU.mult)
            q.engine_nop().then_inc(L.sem("Ve"), 1)
        ve = L.bump("Ve")
        ph.v(fv)
        ph.run()

        ph = Ph(L)
        L.wait(ph, "y", "Ve", ve)
        for j in range(4):
            L.store(ph, retr_o[j * 64:(j + 1) * 64, c0:c0 + CH], ret[j][:, :])
        ph.run()
    L.pop_scope()
    if _own:
        L.finish()
    return L


def build_l2attnC(L=None):
    _own = L is None
    if _own:
        L = Launch("l2attnC")
    L.push_scope()
    x_in = L.dram_in("x_in", (DD, TC))
    retr_i = L.dram_in("retr", (DD, TC))
    wo = L.dram_in("wo", (1024, 1024))
    bo = L.dram_in("bo", (1024,))
    gb = L.dram_in("gb", (128, 16))
    x_out = L.dram_out("x_out", (DD, TC))
    L.emit_consts()
    gb_sb = L.sb("gb", (128, 16), F32)
    ph = Ph(L)
    t = L.load(ph, gb_sb[:, :], gb)
    L.wait(ph, "s", "Ld", t)
    ph.run()

    CH = 512
    x_sb = [L.sb("x", (128, CH), F32) for _ in range(8)]
    x0f = [L.sb("x0f", (128, CH), F32) for _ in range(8)]
    rt = [L.sb("rt", (128, CH), F32) for _ in range(8)]
    rn = [L.sb("rn", (128, CH), BF16) for _ in range(8)]
    sq = L.sb("sq", (128, 8 * 512), F32)
    for ch in range(TC // CH):
        c0 = ch * CH
        ph = Ph(L)
        L.wait(ph, "y", "St", L.cnt("St"))
        for f in range(8):
            L.load(ph, x_sb[f][:, :], x_in[f * 128:(f + 1) * 128, c0:c0 + CH],
                   queue="sync")
            L.load(ph, rt[f][:, :], retr_i[f * 128:(f + 1) * 128, c0:c0 + CH],
                   queue="sync")
        L.wait(ph, "v", "Ld", L.cnt("Ld"))
        ph.run()
        L.ln_window([x[:, :] for x in x_sb], CH, [x[:, :] for x in rn], sq[:, :],
                    out_f32=[x[:, :] for x in x0f])
        L.ln_window([x[:, :] for x in rt], CH, [x[:, :] for x in rn], sq[:, :],
                    gamma_cols=[gb_sb[:, f:f + 1] for f in range(8)],
                    beta_cols=[gb_sb[:, 8 + f:8 + f + 1] for f in range(8)])
        ph = Ph(L)

        def fv(q):
            for f in range(8):
                q.tensor_tensor(x_sb[f][:, :], x_sb[f][:, :], x0f[f][:, :],
                                ALU.add)
        ph.v(fv)
        ph.run()

        def evo(mi, c, ps_ap, ph, pe):
            return L.ev_addx(ph, pe, ps_ap, x_sb[mi][:, c * 512:(c + 1) * 512], 1.0)
        L.proj(rhs_tiles=[t2[:, :] for t2 in rn], Tn=CH, K=1024, M=1024,
               w_dram=wo, bias_dram=bo, evac=evo, mblock=512)

        ph = Ph(L)
        for f in range(8):
            L.store(ph, x_out[f * 128:(f + 1) * 128, c0:c0 + CH], x_sb[f][:, :])
        ph.run()
    L.pop_scope()
    if _own:
        L.finish()
    return L



def build_l0attn(L=None):
    _own = L is None
    if _own:
        L = Launch("l0attn")
    L.push_scope()
    HLO = 128
    TH = TC + 2 * HLO
    x_in = L.dram_in("x_in", (DD, TH))
    x_out = L.dram_out("x_out", (DD, TC))
    wq = L.dram_in("wq", (1024, 1024))
    bq = L.dram_in("bq", (1024,))
    wk = L.dram_in("wk", (1024, 256))
    bk = L.dram_in("bk", (256,))
    wv = L.dram_in("wv", (1024, 256))
    bv = L.dram_in("bv", (256,))
    wo = L.dram_in("wo", (1024, 1024))
    bo = L.dram_in("bo", (1024,))
    sinks_d = L.dram_in("sinks", (1, 16))
    nsc = L.dram_in("nsc", (128, 16))
    ctab = L.dram_in("ctab", (128, TH))
    stab = L.dram_in("stab", (128, TH))
    perm_d = L.dram_in("PERM", (128, 128))
    mask_d = L.dram_in("mask", (TC, 384))
    rw = L.dram_in("rw", (1024, 8))
    rb = L.dram_in("rb", (1, 8))
    gup = L.dram_in("gup", (8 * 1024, 2048))
    gupb = L.dram_in("gupb", (8, 2048))
    dwn = L.dram_in("dwn", (8 * 1024, 1024))
    dwnb = L.dram_in("dwnb", (8, 1024))
    oneh_d = L.dram_in("ONEHOT8", (8, 8 * 128))
    h0_d = L.dram_tmp("h0_d", (DD, TH), BF16)
    att_d = L.dram_tmp("att_d", (DD, TC), BF16)
    h1_d = L.dram_tmp("h1_d", (DD, TC), BF16)

    L.emit_consts()
    nsc_sb = L.sb("nsc", (128, 16), F32)
    sink_bc = L.sb("sinkbc", (128, 16), F32)
    nsink_bc = L.sb("nsinkbc", (128, 16), F32)
    oneh_sb = L.sb("oneh", (8, 8 * 128), BF16)
    sk_row = L.sb("skrow", (1, 16), F32)

    ph = Ph(L)
    L.load(ph, nsc_sb[:, :], nsc)
    L.load(ph, oneh_sb[:, :], oneh_d, queue="gpsimd")
    t = L.load(ph, sk_row[0:1, :], sinks_d)
    ph.run()
    ph = Ph(L)
    pss = L.ps("pss", (128, 16))

    def ft0(q, t=t):
        q.wait_ge(L.sem("Ld"), L.cnt("Ld"))
        q.matmul(pss[:, :], L.ones_r128_f[:, :], sk_row[0:1, :],
                 start=True, stop=True).then_inc(L.sem("Pe"), 1)
    pe0 = L.bump("Pe")
    ph.t(ft0)

    def fv0(q, pe0=pe0):
        q.wait_ge(L.sem("Pe"), pe0)
        q.tensor_copy(sink_bc[:, :], pss[:, :])
        q.tensor_scalar(nsink_bc[:, :], pss[:, :], -1.0, None, ALU.mult)
    ph.v(fv0)
    ph.run()

    loc_qkv = contextlib.ExitStack()
    qf = [L.sb("qf", (128, TC), BF16, loc_qkv) for _ in range(8)]
    kf = [L.sb("kf", (128, TH), BF16, loc_qkv) for _ in range(2)]
    vt = [L.sb("vt", (128, 256), BF16, loc_qkv) for _ in range(18)]
    mask_sb = L.sb("mask", (128, 16 * 384), BF16, loc_qkv)
    att_st = L.sb("att_st", (128, TC), BF16, loc_qkv)
    ph = Ph(L)
    L.load(ph, mask_sb[:, :].rearrange("p (tt m) -> p tt m", tt=16),
           mask_d.rearrange("(tt p) m -> p tt m", p=128), queue="gpsimd")
    ph.run()

    loc_a1 = contextlib.ExitStack()
    a1 = [L.sb("a1", (128, TH), BF16, loc_a1) for _ in range(8)]
    perm_sb = L.sb("perm", (128, 128), BF16, loc_a1)
    ct = L.sb("ct", (128, TH), BF16, loc_a1)
    st = L.sb("st", (128, TH), BF16, loc_a1)
    ph = Ph(L)
    L.load(ph, perm_sb[:, :], perm_d, queue="gpsimd")
    L.load(ph, ct[:, :], ctab, queue="gpsimd")
    L.load(ph, st[:, :], stab, queue="gpsimd")
    ph.run()

    LNCH = 256
    loc_sq = contextlib.ExitStack()
    sq = L.sb("sq", (128, 8 * 512), F32, loc_sq)
    x_sc = [L.sb("xsc", (128, LNCH), F32, loc_sq) for _ in range(8)]
    h0st = [L.sb("h0st", (128, LNCH), BF16, loc_sq) for _ in range(8)]

    for c0 in range(0, TH, LNCH):
        cw = min(LNCH, TH - c0)
        ph = Ph(L)
        L.wait(ph, "y", "St", L.cnt("St"))
        tg = [L.load(ph, x_sc[f][:, 0:cw], x_in[f * 128:(f + 1) * 128, c0:c0 + cw],
                     queue="sync") for f in range(8)]
        L.wait(ph, "v", "Ld", tg[-1])
        ph.run()
        L.ln_window([x[:, 0:cw] for x in x_sc], cw,
                    [h[:, 0:cw] for h in h0st], sq[:, :])
        ph = Ph(L)
        for f in range(8):
            L.store(ph, h0_d[f * 128:(f + 1) * 128, c0:c0 + cw], h0st[f][:, 0:cw])
        ph.run()

    if L0_STOP <= 1:
        loc_sq.close(); loc_a1.close(); loc_qkv.close()
        L.pop_scope()
        if _own:
            L.finish()
        return L
    for c0 in range(0, TH, LNCH):
        cw = min(LNCH, TH - c0)
        ph = Ph(L)
        L.wait(ph, "y", "St", L.cnt("St"))
        tg = [L.load(ph, h0st[f][:, 0:cw], h0_d[f * 128:(f + 1) * 128, c0:c0 + cw],
                     queue="sync") for f in range(8)]
        L.wait(ph, "v", "Ld", tg[-1])
        L.wait(ph, "s", "Ld", tg[-1])
        ph.run()
        L.ln_window([h[:, 0:cw] for h in h0st], cw,
                    [a[:, c0:c0 + cw] for a in a1], sq[:, :],
                    rms_mode=True, in_bf16=True,
                    gamma_cols=[nsc_sb[:, f:f + 1] for f in range(8)])
    loc_sq.close()
    if L0_STOP <= 2:
        loc_a1.close(); loc_qkv.close()
        L.pop_scope()
        if _own:
            L.finish()
        return L

    def evq(mi, c, ps_ap, ph, pe):
        return L.ev_act(ph, pe, ps_ap, qf[mi][:, c * 512:(c + 1) * 512], AF.Copy)
    L.proj(rhs_tiles=[a[:, HLO:HLO + TC] for a in a1], Tn=TC, K=1024, M=1024,
           w_dram=wq, bias_dram=bq, evac=evq, mblock=512)

    def evk(mi, c, ps_ap, ph, pe):
        w = 512 if (c + 1) * 512 <= TH else TH - c * 512
        return L.ev_act(ph, pe, ps_ap, kf[mi][:, c * 512:c * 512 + w], AF.Copy,
                        w=w)
    L.proj(rhs_tiles=[a[:, 0:TH] for a in a1], Tn=TH, K=1024, M=256,
           w_dram=wk, bias_dram=bk, evac=evk, mblock=256)

    locv0 = contextlib.ExitStack()
    wv_sb = L.sb("wv", (128, 8 * 256), BF16, locv0)
    bvrow = L.sb("bvrow", (1, 256), BF16, locv0)
    ph = Ph(L)
    tw = L.load(ph, wv_sb[:, :].rearrange("p (k m) -> p k m", k=8),
                wv.rearrange("(k p) m -> p k m", p=128))
    tb = L.load(ph, bvrow[0:1, :], bv.rearrange("(o m) -> o m", o=1))
    ph.run()
    psv = [L.ps("psv", (128, 256), stack=locv0) for _ in range(4)]
    ph = Ph(L)
    for tt in range(18):
        bank = tt % 4
        cw = min(128, TH - tt * 128)

        def ftv(q, tt=tt, bank=bank, cw=cw, tw=tw, tb=tb):
            if tt == 0:
                q.wait_ge(L.sem("Ld"), tw)
                q.wait_ge(L.sem("Ld"), tb)
            if tt >= 4:
                q.wait_ge(L.sem("Ac"), L._vt_ac[tt - 4])
            for k in range(8):
                q.matmul(psv[bank][0:cw, :], a1[k][:, tt * 128:tt * 128 + cw],
                         wv_sb[:, k * 256:(k + 1) * 256],
                         start=(k == 0), stop=False)
            q.matmul(psv[bank][0:cw, :], L.ones_row_b[0:1, 0:cw], bvrow[0:1, :],
                     start=False, stop=True).then_inc(L.sem("Pe"), 1)
        pe = L.bump("Pe")
        ph.t(ftv)

        def fsv(q, tt=tt, bank=bank, cw=cw, pe=pe):
            q.wait_ge(L.sem("Pe"), pe)
            q.activation(vt[tt][0:cw, :], psv[bank][0:cw, :],
                         AF.Copy).then_inc(L.sem("Ac"), 1)
        L._vt_ac = getattr(L, "_vt_ac", {})
        L._vt_ac[tt] = L.bump("Ac")
        ph.s(fsv)
    ph.run()
    locv0.close()

    locr = contextlib.ExitStack()
    psr = [L.ps("psr", (128, 512), stack=locr) for _ in range(2)]
    rt1 = L.sb("rt1", (128, 512), F32, locr)
    rt2 = L.sb("rt2", (128, 512), F32, locr)
    ph = Ph(L)
    units = [("q", i, c) for i in range(8) for c in range(TC // 512)] + \
            [("k", i, c) for i in range(2) for c in range((TH + 511) // 512)]
    for ui, (kind, i, c) in enumerate(units):
        arr = qf if kind == "q" else kf
        off = HLO if kind == "q" else 0
        cw = 512 if kind == "q" or (c + 1) * 512 <= TH else TH - c * 512
        bank = ui % 2

        def ftr(q, i=i, c=c, cw=cw, bank=bank, ui=ui, arr=arr):
            if ui >= 2:
                q.wait_ge(L.sem("Ve"), L._rp_ve[ui - 2])
            q.matmul(psr[bank][:, 0:cw], perm_sb[:, :],
                     arr[i][:, c * 512:c * 512 + cw],
                     start=True, stop=True).then_inc(L.sem("Pe"), 1)
        pe = L.bump("Pe")
        ph.t(ftr)

        def fvr(q, i=i, c=c, cw=cw, bank=bank, arr=arr, off=off, pe=pe):
            q.wait_ge(L.sem("Pe"), pe)
            sl = slice(c * 512, c * 512 + cw)
            tsl = slice(off + c * 512, off + c * 512 + cw)
            q.tensor_tensor(rt1[:, 0:cw], arr[i][:, sl], ct[:, tsl], ALU.mult)
            q.tensor_tensor(rt2[:, 0:cw], psr[bank][:, 0:cw], st[:, tsl],
                            ALU.mult)
            q.tensor_tensor(arr[i][:, sl], rt1[:, 0:cw], rt2[:, 0:cw],
                            ALU.add).then_inc(L.sem("Ve"), 1)
        L._rp_ve = getattr(L, "_rp_ve", {})
        L._rp_ve[ui] = L.bump("Ve")
        ph.v(fvr)
    ph.run()
    locr.close()
    loc_a1.close()
    if L0_STOP <= 3:
        loc_qkv.close()
        L.pop_scope()
        if _own:
            L.finish()
        return L

    loca = contextlib.ExitStack()
    ps_s = [L.ps("ps_s", (128, 384), stack=loca) for _ in range(2)]
    ps_t = [L.ps("ps_t", (128, 128), BF16, stack=loca) for _ in range(2)]
    ps_o = [L.ps("ps_o", (64, 128), stack=loca) for _ in range(2)]
    p_sb = L.sb("p_sb", (128, 384), BF16, loca)
    pT_sb = [L.sb("pT", (128, 128), BF16, loca) for _ in range(3)]
    mrow = L.sb("mrow", (128, 1), F32, loca)
    mneg = L.sb("mneg", (128, 1), F32, loca)
    dsum = L.sb("dsum", (128, 1), F32, loca)
    esink = L.sb("esink", (128, 1), F32, loca)
    rcp = L.sb("rcp", (128, 1), F32, loca)

    Q0 = [0, 1, 2, 3, 8, 9, 10, 11]
    Q1 = [4, 5, 6, 7, 12, 13, 14, 15]
    for pair in range(8):
        ph = Ph(L)
        L.wait(ph, "s", "St", L.cnt("St"))
        for qh in (Q0[pair], Q1[pair]):
            kvh = qh // 4
            qtile = pair
            qbase = 0 if qh in Q0 else 64
            ktile, kbase = kvh // 2, (kvh % 2) * 64
            for tt in range(16):
                sbank = tt % 2

                def fta(q, qh=qh, tt=tt, sbank=sbank, qtile=qtile, qbase=qbase,
                        ktile=ktile, kbase=kbase,
                        prev=L.cnt("Ac")):
                    if prev:
                        q.wait_ge(L.sem("Ac"), prev)
                    q.matmul(ps_s[sbank][:, :],
                             qf[qtile][qbase:qbase + 64, tt * 128:(tt + 1) * 128],
                             kf[ktile][kbase:kbase + 64, tt * 128:tt * 128 + 384],
                             start=True, stop=True).then_inc(L.sem("Pe"), 1)
                pe1 = L.bump("Pe")
                ph.t(fta)

                def fva(q, qh=qh, tt=tt, sbank=sbank, pe1=pe1):
                    q.wait_ge(L.sem("Pe"), pe1)
                    q.reduce_max(mrow[:, :], ps_s[sbank][:, :], axis=AX.X)
                    L.vsync(q)
                    q.tensor_scalar(mneg[:, :], mrow[:, :], -1.0, None, ALU.mult)
                    L.vsync(q)
                    q.tensor_tensor(mneg[:, :], mneg[:, :],
                                    nsink_bc[:, qh:qh + 1], ALU.min).then_inc(
                                        L.sem("Ve"), 1)
                ve1 = L.bump("Ve")
                ph.v(fva)

                def fsa(q, qh=qh, sbank=sbank, ve1=ve1):
                    q.wait_ge(L.sem("Ve"), ve1)
                    q.activation(p_sb[:, :], ps_s[sbank][:, :], AF.Exp,
                                 bias=mneg[:, :])
                    q.activation(esink[:, :], sink_bc[:, qh:qh + 1], AF.Exp,
                                 bias=mneg[:, :]).then_inc(L.sem("Ac"), 1)
                L._at_ac = getattr(L, "_at_ac", {})
                ac1 = L.bump("Ac")
                L._at_ac[(qh, tt)] = ac1
                ph.s(fsa)

                def fvb(q, tt=tt, ac1=ac1):
                    q.wait_ge(L.sem("Ac"), ac1)
                    q.tensor_tensor(p_sb[:, :], p_sb[:, :],
                                    mask_sb[:, tt * 384:(tt + 1) * 384], ALU.mult)
                    L.vsync(q)
                    q.reduce_sum(dsum[:, :], p_sb[:, :], axis=AX.X)
                    L.vsync(q)
                    q.tensor_tensor(dsum[:, :], dsum[:, :], esink[:, :], ALU.add)
                    L.vsync(q)
                    q.reciprocal(rcp[:, :], dsum[:, :])
                    L.vsync(q)
                    q.engine_nop().then_inc(L.sem("Ve"), 1)
                ve2 = L.bump("Ve")
                ph.v(fvb)

                def fsb(q, ve2=ve2):
                    q.wait_ge(L.sem("Ve"), ve2)
                    q.activation(p_sb[:, :], p_sb[:, :], AF.Identity,
                                 scale=rcp[:, :]).then_inc(L.sem("Ac"), 1)
                ac2 = L.bump("Ac")
                ph.s(fsb)

                for j in range(3):
                    tbank = j % 2

                    def ftt(q, qh=qh, tt=tt, j=j, tbank=tbank, ac2=ac2):
                        if j == 0:
                            q.wait_ge(L.sem("Ac"), ac2)
                        if j >= 2:
                            q.wait_ge(L.sem("Ve"), L._tp_ve[(qh, tt, j - 2)])
                        q.transpose(ps_t[tbank][:, :],
                                    p_sb[:, j * 128:(j + 1) * 128],
                                    L.ident_b[:, :]).then_inc(L.sem("Pe"), 1)
                    pe2 = L.bump("Pe")
                    ph.t(ftt)

                    def fvt(q, qh=qh, tt=tt, j=j, tbank=tbank, pe2=pe2):
                        q.wait_ge(L.sem("Pe"), pe2)
                        q.tensor_copy(pT_sb[j][:, :],
                                      ps_t[tbank][:, :]).then_inc(L.sem("Ve"), 1)
                    L._tp_ve = getattr(L, "_tp_ve", {})
                    L._tp_ve[(qh, tt, j)] = L.bump("Ve")
                    ph.v(fvt)
                velast = L._tp_ve[(qh, tt, 2)]

                def ftp(q, qh=qh, tt=tt, sbank=sbank, velast=velast, kvh=kvh):
                    q.wait_ge(L.sem("Ve"), velast)
                    if tt >= 2:
                        q.wait_ge(L.sem("Ac"), L._pv_ac[(qh, tt - 2)])
                    mm = None
                    for j in range(3):
                        kt = tt + j
                        mm = q.matmul(ps_o[sbank][:, :],
                                      vt[kt][:, kvh * 64:(kvh + 1) * 64],
                                      pT_sb[j][:, :],
                                      start=(j == 0), stop=(j == 2))
                    mm.then_inc(L.sem("Pe"), 1)
                pe3 = L.bump("Pe")
                ph.t(ftp)

                def fse(q, qh=qh, tt=tt, sbank=sbank, pe3=pe3, qbase=qbase):
                    q.wait_ge(L.sem("Pe"), pe3)
                    obase = qbase
                    q.activation(att_st[obase:obase + 64,
                                        tt * 128:(tt + 1) * 128],
                                 ps_o[sbank][:, :], AF.Copy).then_inc(
                                     L.sem("Ac"), 1)
                L._pv_ac = getattr(L, "_pv_ac", {})
                L._pv_ac[(qh, tt)] = L.bump("Ac")
                ph.s(fse)
            L._at_last = L._at_ac[(qh, 15)]
        ph.run()
        ph = Ph(L)
        L.store(ph, att_d[pair * 128:(pair + 1) * 128, :], att_st[:, :])
        ph.run()
    loca.close()
    loc_qkv.close()
    if L0_STOP <= 4:
        L.pop_scope()
        if _own:
            L.finish()
        return L

    loc5 = contextlib.ExitStack()
    att_full = [L.sb("attf", (128, TC), BF16, loc5) for _ in range(8)]
    h0loc = [L.sb("h0loc", (128, TC), BF16, loc5) for _ in range(8)]
    h1 = [L.sb("h1", (128, TC), BF16, loc5) for _ in range(8)]
    ph = Ph(L)
    L.wait(ph, "y", "St", L.cnt("St"))
    for f in range(8):
        L.load(ph, att_full[f][:, :], att_d[f * 128:(f + 1) * 128, :],
               queue="sync")
        L.load(ph, h0loc[f][:, :], h0_d[f * 128:(f + 1) * 128, HLO:HLO + TC],
               queue="sync")
    L.wait(ph, "v", "Ld", L.cnt("Ld"))
    L.wait(ph, "t", "Ld", L.cnt("Ld"))
    ph.run()

    def evo(mi, c, ps_ap, ph, pe):
        ev = L.bump("Ev")

        def f(q, mi=mi, c=c, ps_ap=ps_ap, pe=pe):
            q.wait_ge(L.sem("Pe"), pe)
            q.tensor_tensor(h1[mi][:, c * 512:(c + 1) * 512], ps_ap[:, 0:512],
                            h0loc[mi][:, c * 512:(c + 1) * 512],
                            ALU.add).then_inc(L.sem("Ev"), 1)
        ph.v(f)
        return ev
    L.proj(rhs_tiles=[a[:, 0:TC] for a in att_full], Tn=TC, K=1024, M=1024,
           w_dram=wo, bias_dram=bo, evac=evo, mblock=512)
    ph = Ph(L)
    for f in range(8):
        L.store(ph, h1_d[f * 128:(f + 1) * 128, :], h1[f][:, :])
    ph.run()
    loc5.close()
    if L0_STOP <= 5:
        L.pop_scope()
        if _own:
            L.finish()
        return L
    return _l0attn_moe(L, x_in, x_out, h1_d, nsc_sb, oneh_sb,
                       rw, rb, gup, gupb, dwn, dwnb, HLO, _own)


def _l0attn_moe(L, x_in, x_out, h1_d, nsc_sb, oneh_sb,
                rw, rb, gup, gupb, dwn, dwnb, HLO, _own=True):
    loc = contextlib.ExitStack()
    h1 = [L.sb("h1m", (128, 1024), BF16, loc) for _ in range(8)]
    a2 = [L.sb("a2", (128, 1024), BF16, loc) for _ in range(8)]
    moe_acc = [L.sb("macc", (128, 1024), F32, loc) for _ in range(8)]
    sq = L.sb("sqm", (128, 8 * 512), F32, loc)
    gcl = [L.sb("gcl", (128, 512), BF16, loc) for _ in range(8)]
    sg = [L.sb("sg", (128, 512), BF16, loc) for _ in range(8)]
    gtd = [L.sb("gtd", (128, 512), BF16, loc) for _ in range(8)]
    gbc = L.sb("gbc", (128, 512), BF16, loc)
    g8T = L.sb("g8T", (8, 1024), BF16, loc)
    dwnb_sb = L.sb("dwnb", (8, 1024), BF16, loc)
    rw_sb = L.sb("rw", (128, 64), BF16, loc)
    rb_sb = L.sb("rb", (1, 8), F32, loc)
    x_sc2 = [L.sb("xsc2", (128, 512), F32, loc) for _ in range(8)]
    ph = Ph(L)
    L.load(ph, dwnb_sb[:, :], dwnb)
    L.load(ph, rw_sb[:, :].rearrange("p (k m) -> p k m", k=8),
           rw.rearrange("(k p) m -> p k m", p=128))
    t = L.load(ph, rb_sb[0:1, :], rb)
    L.wait(ph, "t", "Ld", t)
    ph.run()

    locr = contextlib.ExitStack()
    tkD = L.sb("tkD", (128, 8), F32, locr)
    tkA = L.sb("tkA", (128, 8), F32, locr)
    tkB128 = L.sb("tkB128", (128, 128), F32, locr)
    tkC = L.sb("tkC", (128, 1), F32, locr)
    tkCn = L.sb("tkCn", (128, 1), F32, locr)
    tkE = L.sb("tkE", (128, 8), F32, locr)
    tkR = L.sb("tkR", (128, 1), F32, locr)
    mcur = L.sb("mcur", (128, 1), F32, locr)
    m1c = L.sb("m1c", (128, 1), F32, locr)
    ph = Ph(L)

    def fz(q):
        q.memset(tkB128[:, :], 0.0)
    ph.g(fz)
    ph.run()

    for half in range(2):
        H0 = half * 1024
        ph = Ph(L)
        L.wait(ph, "y", "St", L.cnt("St"))
        tg = [L.load(ph, h1[f][:, :], h1_d[f * 128:(f + 1) * 128, H0:H0 + 1024],
                     queue="sync") for f in range(8)]
        L.wait(ph, "v", "Ld", tg[-1])
        L.wait(ph, "s", "Ld", tg[-1])

        def fz2(q):
            for f in range(8):
                q.memset(moe_acc[f][:, :], 0.0)
        ph.g(fz2)
        ph.run()
        for c0 in range(0, 1024, 512):
            L.ln_window([h[:, c0:c0 + 512] for h in h1], 512,
                        [a[:, c0:c0 + 512] for a in a2], sq[:, :],
                        rms_mode=True, in_bf16=True,
                        gamma_cols=[nsc_sb[:, 8 + f:8 + f + 1] for f in range(8)])

        locps = contextlib.ExitStack()
        psr = L.ps("psr", (128, 8), stack=locps)
        pst = L.ps("pst", (128, 128), stack=locps)
        for tt in range(1024 // 128):
            ph = Ph(L)

            def ft(q, tt=tt):
                for k in range(8):
                    q.matmul(psr[:, :], a2[k][:, tt * 128:(tt + 1) * 128],
                             rw_sb[:, k * 8:(k + 1) * 8],
                             start=(k == 0), stop=False)
                q.matmul(psr[:, :], L.ones_r128_f[:, :], rb_sb[:, :],
                         start=False, stop=True).then_inc(L.sem("Pe"), 1)
            pe = L.bump("Pe")
            ph.t(ft)

            def fv(q, tt=tt, pe=pe):
                q.wait_ge(L.sem("Pe"), pe)
                q.tensor_copy(tkD[:, :], psr[:, :])
                L.vsync(q)
                q.tensor_copy(tkE[:, :], tkD[:, :])
                L.vsync(q)
                for r in range(4):
                    q.reduce_max(mcur[:, :], tkE[:, :], axis=AX.X)
                    L.vsync(q)
                    if r == 0:
                        q.tensor_copy(m1c[:, :], mcur[:, :])
                    if r < 3:
                        q.tensor_scalar(tkA[:, :], tkE[:, :], mcur[:, :], None,
                                        ALU.is_ge)
                        L.vsync(q)
                        q.tensor_scalar(tkA[:, :], tkA[:, :], 1e9, None,
                                        ALU.mult)
                        L.vsync(q)
                        q.tensor_tensor(tkE[:, :], tkE[:, :], tkA[:, :],
                                        ALU.subtract)
                        L.vsync(q)
                q.tensor_scalar(tkA[:, :], tkD[:, :], mcur[:, :], None,
                                ALU.is_ge)
                q.tensor_scalar(tkCn[:, :], m1c[:, :], -1.0, None,
                                ALU.mult).then_inc(L.sem("Ve"), 1)
            ve = L.bump("Ve")
            ph.v(fv)

            def fs(q, tt=tt, ve=ve):
                q.wait_ge(L.sem("Ve"), ve)
                q.activation(tkE[:, :], tkD[:, :], AF.Exp,
                             bias=tkCn[:, :]).then_inc(L.sem("Ac"), 1)
            ac = L.bump("Ac")
            ph.s(fs)

            def fv2(q, tt=tt, ac=ac):
                q.wait_ge(L.sem("Ac"), ac)
                q.tensor_tensor(tkE[:, :], tkE[:, :], tkA[:, :], ALU.mult)
                L.vsync(q)
                q.reduce_sum(tkC[:, :], tkE[:, :], axis=AX.X)
                L.vsync(q)
                q.tensor_scalar(tkC[:, :], tkC[:, :], 4.0, None, ALU.mult)
                L.vsync(q)
                q.reciprocal(tkR[:, :], tkC[:, :])
                L.vsync(q)
                q.tensor_scalar(tkB128[:, 0:8], tkE[:, :], tkR[:, :], None,
                                ALU.mult).then_inc(L.sem("Ve"), 1)
            ve2 = L.bump("Ve")
            ph.v(fv2)

            def ft2(q, tt=tt, ve2=ve2):
                q.wait_ge(L.sem("Ve"), ve2)
                q.transpose(pst[:, :], tkB128[:, :], L.ident_f[:, :]).then_inc(
                    L.sem("Pe"), 1)
            pe2 = L.bump("Pe")
            ph.t(ft2)

            def fv3(q, tt=tt, pe2=pe2):
                q.wait_ge(L.sem("Pe"), pe2)
                q.tensor_copy(g8T[:, tt * 128:(tt + 1) * 128], pst[0:8, :])
            ph.v(fv3)
            ph.run()

        locps.close()
        if L0_STOP <= 6:
            continue
        locps2 = contextlib.ExitStack()
        psg = L.ps("psg", (128, 512), stack=locps2)
        psd = [L.ps("psd", (128, 512), stack=locps2) for _ in range(2)]
        for c in range(2):
            for e in range(8):
                ph = Ph(L)

                def ftg(q, e=e, c=c):
                    q.matmul(psg[:, :], oneh_sb[:, e * 128:(e + 1) * 128],
                             g8T[:, c * 512:(c + 1) * 512],
                             start=True, stop=True).then_inc(L.sem("Pe"), 1)
                pe = L.bump("Pe")
                ph.t(ftg)

                def fvg(q, pe=pe):
                    q.wait_ge(L.sem("Pe"), pe)
                    q.tensor_copy(gbc[:, :], psg[:, :])
                ph.v(fvg)
                ph.run()

                def evgu(mi, cc, ps_ap, ph2, pe2):
                    ev1 = L.bump("Ev")

                    def f(q, mi=mi, ps_ap=ps_ap, pe2=pe2):
                        q.wait_ge(L.sem("Pe"), pe2)
                        if mi < 8:
                            q.activation(sg[mi][:, :], ps_ap[:, 0:512],
                                         AF.Sigmoid, scale=IM(1.702)).then_inc(
                                             L.sem("Ev"), 1)
                        else:
                            q.nop().then_inc(L.sem("Ev"), 1)
                    ph2.s(f)
                    ev = L.bump("Ev")

                    def f2(q, mi=mi, ps_ap=ps_ap, pe2=pe2, ev1=ev1):
                        q.wait_ge(L.sem("Ev"), ev1)
                        if mi < 8:
                            q.tensor_scalar(gcl[mi][:, :], ps_ap[:, 0:512], 7.0,
                                            None, ALU.min).then_inc(
                                                L.sem("Ev"), 1)
                        else:
                            q.tensor_scalar(gtd[mi - 8][:, :], ps_ap[:, 0:512],
                                            7.0, -7.0, ALU.min,
                                            ALU.max).then_inc(L.sem("Ev"), 1)
                    ph2.v(f2)
                    return ev
                L.proj(rhs_tiles=[a[:, c * 512:(c + 1) * 512] for a in a2],
                       Tn=512, K=1024, M=2048,
                       w_dram=gup[e * 1024:(e + 1) * 1024, :],
                       bias_dram=gupb[e:e + 1, :],
                       evac=evgu, mblock=512)
                ph2 = Ph(L)

                def fvm(q):
                    last = None
                    for mi in range(8):
                        q.tensor_tensor(gcl[mi][:, :], gcl[mi][:, :],
                                        sg[mi][:, :], ALU.mult)
                    for mi in range(8):
                        q.scalar_tensor_tensor(gtd[mi][:, :], gtd[mi][:, :],
                                               1.0, gcl[mi][:, :], ALU.add,
                                               ALU.mult)
                    for mi in range(8):
                        last = q.tensor_tensor(gtd[mi][:, :], gtd[mi][:, :],
                                               gbc[:, :], ALU.mult)
                    last.then_inc(L.sem("Ve"), 1)
                L.bump("Ve")
                ph2.v(fvm)
                ph2.run()

                def evd(mi, cc, ps_ap, ph3, pe3):
                    ev = L.bump("Ev")

                    def f(q, mi=mi, ps_ap=ps_ap, pe3=pe3, c=c):
                        q.wait_ge(L.sem("Pe"), pe3)
                        q.tensor_tensor(moe_acc[mi][:, c * 512:(c + 1) * 512],
                                        moe_acc[mi][:, c * 512:(c + 1) * 512],
                                        ps_ap[:, 0:512], ALU.add).then_inc(
                                            L.sem("Ev"), 1)
                    ph3.v(f)
                    return ev
                L.proj(rhs_tiles=[t2[:, :] for t2 in gtd], Tn=512, K=1024,
                       M=1024, w_dram=dwn[e * 1024:(e + 1) * 1024, :],
                       evac=evd, mblock=512)
            ph = Ph(L)
            for mi in range(8):
                bank = mi % 2

                def ftb(q, mi=mi, c=c, bank=bank):
                    if mi >= 2:
                        q.wait_ge(L.sem("Ve"), L._db_ve[(half, c, mi - 2)])
                    q.matmul(psd[bank][:, :],
                             dwnb_sb[:, mi * 128:(mi + 1) * 128],
                             g8T[:, c * 512:(c + 1) * 512],
                             start=True, stop=True).then_inc(L.sem("Pe"), 1)
                pe = L.bump("Pe")
                ph.t(ftb)

                def fvb(q, mi=mi, c=c, bank=bank, pe=pe):
                    q.wait_ge(L.sem("Pe"), pe)
                    q.tensor_tensor(moe_acc[mi][:, c * 512:(c + 1) * 512],
                                    moe_acc[mi][:, c * 512:(c + 1) * 512],
                                    psd[bank][:, 0:512], ALU.add).then_inc(
                                        L.sem("Ve"), 1)
                L._db_ve = getattr(L, "_db_ve", {})
                L._db_ve[(half, c, mi)] = L.bump("Ve")
                ph.v(fvb)
            ph.run()
        locps2.close()
        if L0_STOP <= 7:
            continue

        HLO_ = HLO
        for c in range(2):
            ph = Ph(L)
            L.wait(ph, "y", "St", L.cnt("St"))
            tg = [L.load(ph, x_sc2[f][:, :],
                         x_in[f * 128:(f + 1) * 128,
                              HLO_ + H0 + c * 512:HLO_ + H0 + (c + 1) * 512],
                         queue="sync") for f in range(8)]
            L.wait(ph, "v", "Ld", tg[-1])
            ph.run()
            ph = Ph(L)

            def fvx(q, c=c):
                last = None
                for f in range(8):
                    q.tensor_tensor(x_sc2[f][:, :], x_sc2[f][:, :],
                                    h1[f][:, c * 512:(c + 1) * 512], ALU.add)
                    last = q.tensor_tensor(x_sc2[f][:, :], x_sc2[f][:, :],
                                           moe_acc[f][:, c * 512:(c + 1) * 512],
                                           ALU.add)
                last.then_inc(L.sem("Ve"), 1)
            L.bump("Ve")
            ph.v(fvx)
            ph.run()
            ph = Ph(L)
            for f in range(8):
                L.store(ph, x_out[f * 128:(f + 1) * 128,
                               H0 + c * 512:H0 + (c + 1) * 512], x_sc2[f][:, :])
            ph.run()
    locr.close()
    loc.close()
    L.pop_scope()
    if _own:
        L.finish()
    return L



def build_l3attnA(L=None):
    _own = L is None
    if _own:
        L = Launch("l3attnA")
    L.push_scope()
    x_i = L.dram_in("x_cb", (256, SS))
    gb = L.dram_in("gb", (128, 4))
    g4_d = L.dram_in("G4", (128, 4))
    g4t_d = L.dram_in("G4T", (4, 128))
    hn_o = L.dram_out("hn", (256, SS), BF16)
    L.emit_consts()
    EPSG = 1e-6
    g4 = L.sb("g4", (128, 4), F32)
    g4t = L.sb("g4t", (4, 128), F32)
    gb_sb = L.sb("gb", (128, 4), F32)
    xt = [L.sb("xt", (128, 512), F32) for _ in range(4)]
    hnt = L.sb("hnt", (128, 512), BF16)
    sqs = L.sb("sqs", (128, 512), F32)
    t4a = L.sb("t4a", (4, 8), F32)
    scol = [L.sb("scol", (128, 1), F32) for _ in range(2)]
    bcol = [L.sb("bcol", (128, 1), F32) for _ in range(2)]
    ph = Ph(L)
    L.load(ph, g4[:, :], g4_d)
    L.load(ph, g4t[:, :], g4t_d, queue="gpsimd")
    t = L.load(ph, gb_sb[:, :], gb)
    L.wait(ph, "t", "Ld", t)
    L.wait(ph, "v", "Ld", t)
    ph.run()

    loc = contextlib.ExitStack()
    ps1 = L.ps("ps1", (4, 512), stack=loc)
    ps2 = L.ps("ps2", (4, 512), stack=loc)
    pse = [L.ps("pse", (128, 1), stack=loc) for _ in range(2)]
    for t2 in range(2):
        ph = Ph(L)
        for cc in range(SS // 512):
            if cc >= 4:
                L.wait(ph, "y", "Pe", L._a_pe[(t2, cc - 4)])
            tg = L.load(ph, xt[cc % 4][:, :],
                        x_i[t2 * 128:(t2 + 1) * 128, cc * 512:(cc + 1) * 512],
                        queue="sync")

            def fsq(q, cc=cc, tg=tg):
                q.wait_ge(L.sem("Ld"), tg)
                if cc >= 1:
                    q.wait_ge(L.sem("Pe"), L._a_pe[(t2, cc - 1)])
                q.activation(sqs[:, :], xt[cc % 4][:, :],
                             AF.Square).then_inc(L.sem("Ac"), 1)
            ac = L.bump("Ac")
            ph.s(fsq)

            def ftm(q, t2=t2, cc=cc, ac=ac, tg=tg):
                q.wait_ge(L.sem("Ld"), tg)
                q.wait_ge(L.sem("Ac"), ac)
                q.matmul(ps1[:, :], g4[:, :], xt[cc % 4][:, :],
                         start=(cc == 0), stop=(cc == SS // 512 - 1))
                q.matmul(ps2[:, :], g4[:, :], sqs[:, :],
                         start=(cc == 0),
                         stop=(cc == SS // 512 - 1)).then_inc(L.sem("Pe"), 1)
            L._a_pe = getattr(L, "_a_pe", {})
            L._a_pe[(t2, cc)] = L.bump("Pe")
            ph.t(ftm)
        ph.run()

        ph = Ph(L)
        pe = L.cnt("Pe")

        def fvr(q, pe=pe):
            q.wait_ge(L.sem("Pe"), pe)
            q.reduce_sum(t4a[:, 0:1], ps1[:, :], axis=AX.X)
            L.vsync(q)
            q.reduce_sum(t4a[:, 1:2], ps2[:, :], axis=AX.X)
            L.vsync(q)
            N = float(SS * 32)
            q.tensor_scalar(t4a[:, 2:3], t4a[:, 0:1], 1.0 / N, None, ALU.mult)
            L.vsync(q)
            q.tensor_scalar(t4a[:, 3:4], t4a[:, 1:2], 1.0 / N, None, ALU.mult)
            L.vsync(q)
            q.tensor_tensor(t4a[:, 4:5], t4a[:, 2:3], t4a[:, 2:3], ALU.mult)
            L.vsync(q)
            q.tensor_tensor(t4a[:, 5:6], t4a[:, 3:4], t4a[:, 4:5], ALU.subtract)
            L.vsync(q)
            q.engine_nop().then_inc(L.sem("Ve"), 1)
        ve = L.bump("Ve")
        ph.v(fvr)

        def fsr(q, ve=ve):
            q.wait_ge(L.sem("Ve"), ve)
            q.activation(t4a[:, 6:7], t4a[:, 5:6], AF.Ln, bias=IM(EPSG))
            q.activation(t4a[:, 7:8], t4a[:, 6:7], AF.Exp,
                         scale=IM(-0.5)).then_inc(L.sem("Ac"), 1)
        ac2 = L.bump("Ac")
        ph.s(fsr)

        def fte(q, ac2=ac2):
            q.wait_ge(L.sem("Ac"), ac2)
            mm = q.matmul(pse[0][:, :], g4t[:, :], t4a[:, 7:8],
                          start=True, stop=True)
            mm = q.matmul(pse[1][:, :], g4t[:, :], t4a[:, 2:3],
                          start=True, stop=True)
            mm.then_inc(L.sem("Pe"), 1)
        pe2 = L.bump("Pe")
        ph.t(fte)

        def fvc(q, t2=t2, pe2=pe2):
            q.wait_ge(L.sem("Pe"), pe2)
            q.tensor_tensor(scol[t2][:, :], pse[0][:, :], gb_sb[:, t2:t2 + 1],
                            ALU.mult)
            L.vsync(q)
            q.tensor_tensor(bcol[t2][:, :], pse[1][:, :], scol[t2][:, :],
                            ALU.mult)
            L.vsync(q)
            q.tensor_tensor(bcol[t2][:, :], gb_sb[:, 2 + t2:3 + t2],
                            bcol[t2][:, :], ALU.subtract).then_inc(
                                L.sem("Ve"), 1)
        L.bump("Ve")
        ph.v(fvc)
        ph.run()

        for cc in range(SS // 512):
            ph = Ph(L)
            L.wait(ph, "y", "St", L.cnt("St"))
            tg = L.load(ph, xt[0][:, :],
                        x_i[t2 * 128:(t2 + 1) * 128, cc * 512:(cc + 1) * 512],
                        queue="sync")

            def fsn(q, t2=t2, tg=tg):
                q.wait_ge(L.sem("Ld"), tg)
                q.activation(hnt[:, :], xt[0][:, :], AF.Identity,
                             scale=scol[t2][:, :], bias=bcol[t2][:, :])
            ph.s(fsn)
            ph.run()
            ph = Ph(L)
            L.store(ph, hn_o[t2 * 128:(t2 + 1) * 128, cc * 512:(cc + 1) * 512],
                    hnt[:, :])
            ph.run()
    loc.close()
    L.pop_scope()
    if _own:
        L.finish()
    return L


def build_l3attnB(L=None):
    _own = L is None
    if _own:
        L = Launch("l3attnB")
    L.push_scope()
    hn_i = L.dram_in("hn", (DD, 2 * SS), BF16)
    wq = L.dram_in("wq", (1024, 128))
    wk = L.dram_in("wk", (1024, 128))
    wv = L.dram_in("wv", (1024, 128))
    bqkv = L.dram_in("bqkv", (3, 128))
    att_o = L.dram_out("att", (128, 2 * SS), BF16)
    L.emit_consts()
    T = 2 * SS
    NT = T // 128
    qfm = L.sb("qfm", (128, T), BF16)
    kfm = L.sb("kfm", (128, T), BF16)
    vta = L.sb("vta", (128, NT * 130), BF16)
    wq_sb = L.sb("wq", (128, 8 * 128), BF16)
    wk_sb = L.sb("wk", (128, 8 * 128), BF16)
    wv_sb = L.sb("wv", (128, 8 * 128), BF16)
    bq_sb = L.sb("bq", (1, 128), BF16)
    bk_sb = L.sb("bk", (1, 128), BF16)
    bv_sb = L.sb("bv", (1, 128), BF16)
    sqs2 = L.sb("sqs2", (128, 1024), F32)
    kmt = L.sb("kmt", (1, 16), F32)
    kmt2 = L.sb("kmt2", (1, 4), F32)
    qmt = L.sb("qmt", (1, 64), F32)
    kmx64 = L.sb("kmx64", (1, 64), F32)
    nbrow = L.sb("nbrow", (1, 64), F32)
    nbq = L.sb("nbq", (1, 64), F32)
    bcolall = L.sb("bcolall", (128, 64), F32)
    ph = Ph(L)
    L.load(ph, wq_sb[:, :].rearrange("p (k m) -> p k m", k=8),
           wq.rearrange("(k p) m -> p k m", p=128))
    L.load(ph, wk_sb[:, :].rearrange("p (k m) -> p k m", k=8),
           wk.rearrange("(k p) m -> p k m", p=128))
    L.load(ph, wv_sb[:, :].rearrange("p (k m) -> p k m", k=8),
           wv.rearrange("(k p) m -> p k m", p=128))
    L.load(ph, bq_sb[0:1, :], bqkv[0:1, :])
    L.load(ph, bk_sb[0:1, :], bqkv[1:2, :])
    t = L.load(ph, bv_sb[0:1, :], bqkv[2:3, :])

    def fones(q):
        v3 = vta[:, :].rearrange("p (t c) -> p t c", c=130)
        q.memset(v3[:, :, 64:65], 1.0)
        q.memset(v3[:, :, 129:130], 1.0)
    ph.v(fones)
    ph.run()

    loc = contextlib.ExitStack()
    hnc = [L.sb("hnc", (128, 512), BF16, loc) for _ in range(16)]
    psq = [L.ps("psq", (128, 512), stack=loc) for _ in range(2)]
    psk = [L.ps("psk", (128, 512), stack=loc) for _ in range(2)]
    psv = [L.ps("psv", (128, 128), stack=loc) for _ in range(4)]
    for cc in range(T // 512):
        par = cc % 2
        ph = Ph(L)

        def fgl(q, cc=cc, par=par):
            if cc >= 2:
                q.wait_ge(L.sem("Pe"), L._b_pe[cc - 2])
        ph.g(fgl)
        tgs = [L.load(ph, hnc[par * 8 + f][:, :],
                      hn_i[f * 128:(f + 1) * 128, cc * 512:(cc + 1) * 512])
               for f in range(8)]

        def ftp(q, cc=cc, par=par, tg=tgs[-1], t=t):
            q.wait_ge(L.sem("Ld"), tg)
            if cc == 0:
                q.wait_ge(L.sem("Ld"), t)
            if cc >= 1:
                q.wait_ge(L.sem("Ac"), L._b_ac[cc - 1])
            for k in range(8):
                q.matmul(psq[par][:, :], wq_sb[:, k * 128:(k + 1) * 128],
                         hnc[par * 8 + k][:, :], start=(k == 0), stop=False)
            q.matmul(psq[par][:, :], bq_sb[0:1, :], L.ones_row_b[0:1, :],
                     start=False, stop=True)
            for k in range(8):
                q.matmul(psk[par][:, :], wk_sb[:, k * 128:(k + 1) * 128],
                         hnc[par * 8 + k][:, :], start=(k == 0), stop=False)
            mm = q.matmul(psk[par][:, :], bk_sb[0:1, :], L.ones_row_b[0:1, :],
                          start=False, stop=True)
            for j in range(4):
                for k in range(8):
                    q.matmul(psv[j][:, :],
                             hnc[par * 8 + k][:, j * 128:(j + 1) * 128],
                             wv_sb[:, k * 128:(k + 1) * 128],
                             start=(k == 0), stop=False)
                mm = q.matmul(psv[j][:, :], L.ones_row_b[0:1, 0:128],
                              bv_sb[0:1, :], start=False, stop=True)
                mm.then_inc(L.sem("Pe"), 1)
            q.nop().then_inc(L.sem("Pe"), 1)
        for _ in range(5):
            pe = L.bump("Pe")
        L._b_pe = getattr(L, "_b_pe", {})
        L._b_pe[cc] = pe
        ph.t(ftp)

        def fsp(q, cc=cc, par=par, pe=pe):
            q.wait_ge(L.sem("Pe"), pe)
            q.activation(qfm[:, cc * 512:(cc + 1) * 512], psq[par][:, :],
                         AF.Identity, scale=IM(0.125))
            q.activation(kfm[:, cc * 512:(cc + 1) * 512], psk[par][:, :],
                         AF.Copy)
            for j in range(4):
                c0 = (cc * 4 + j) * 130
                q.activation(vta[:, c0:c0 + 64], psv[j][:, 0:64], AF.Copy)
                q.activation(vta[:, c0 + 65:c0 + 129], psv[j][:, 64:128],
                             AF.Copy).then_inc(L.sem("Ac"), 1)
        L._b_ac = getattr(L, "_b_ac", {})
        for _ in range(4):
            ac = L.bump("Ac")
        L._b_ac[cc] = ac
        ph.s(fsp)
        ph.run()
    loc.close()

    loc2 = contextlib.ExitStack()
    ps_r = [L.ps("ps_r", (1, 512), stack=loc2) for _ in range(2)]
    ps_bb = L.ps("ps_bb", (128, 64), stack=loc2)
    ph = Ph(L)
    km_sq_ve = {}
    km_red_ve = {}
    km_mm_pe = {}

    def prep_chunk(ki, base, c0, src_fm, red_dst):
        par = ki % 2

        def fsq(q, base=base, c0=c0, par=par, ki=ki, src_fm=src_fm):
            if ki - 2 in km_mm_pe:
                q.wait_ge(L.sem("Pe"), km_mm_pe[ki - 2])
            q.tensor_tensor(
                sqs2[base:base + 64, par * 512:par * 512 + 512],
                src_fm[base:base + 64, c0:c0 + 512],
                src_fm[base:base + 64, c0:c0 + 512],
                ALU.mult).then_inc(L.sem("Ve"), 1)
        km_sq_ve[ki] = L.bump("Ve")
        ph.v(fsq)

        def fmm(q, base=base, par=par, ki=ki):
            q.wait_ge(L.sem("Ve"), km_sq_ve[ki])
            if ki - 2 in km_red_ve:
                q.wait_ge(L.sem("Ve"), km_red_ve[ki - 2])
            q.matmul(ps_r[par][:, :], L.ones_col_f[base:base + 64, :],
                     sqs2[base:base + 64, par * 512:par * 512 + 512],
                     start=True, stop=True).then_inc(L.sem("Pe"), 1)
        km_mm_pe[ki] = L.bump("Pe")
        ph.t(fmm)

        def fred(q, par=par, ki=ki, red_dst=red_dst):
            q.wait_ge(L.sem("Pe"), km_mm_pe[ki])
            q.reduce_max(red_dst, ps_r[par][:, :],
                         axis=AX.X).then_inc(L.sem("Ve"), 1)
        km_red_ve[ki] = L.bump("Ve")
        ph.v(fred)

    for p in range(4):
        b, h = p // 2, p % 2
        for cc in range(16):
            prep_chunk(p * 16 + cc, h * 64, b * SS + cc * 512, kfm,
                       kmt[0:1, cc:cc + 1])

        def ffin(q, p=p):
            L.vsync(q)
            q.reduce_max(kmt2[0:1, p:p + 1], kmt[0:1, 0:16], axis=AX.X)
            L.vsync(q)
        ph.v(ffin)
    for p in range(4):
        b, h = p // 2, p % 2
        for cc in range(16):
            prep_chunk(64 + p * 16 + cc, h * 64, b * SS + cc * 512, qfm,
                       qmt[0:1, p * 16 + cc:p * 16 + cc + 1])

    def fbnd(q):
        L.vsync(q)
        for p in range(4):
            q.tensor_scalar(kmx64[0:1, p * 16:(p + 1) * 16],
                            L.ones_row_f[0:1, 0:16],
                            kmt2[0:1, p:p + 1], None, ALU.mult)
        q.tensor_tensor(nbrow[0:1, :], qmt[0:1, :], kmx64[0:1, :], ALU.mult)
        L.vsync(q)
        q.engine_nop().then_inc(L.sem("Ve"), 1)
    ve_bnd = L.bump("Ve")
    ph.v(fbnd)

    def fsqrtb(q):
        q.wait_ge(L.sem("Ve"), ve_bnd)
        q.activation(nbq[0:1, :], nbrow[0:1, :],
                     AF.Sqrt).then_inc(L.sem("Ac"), 1)
    ac_bnd = L.bump("Ac")
    ph.s(fsqrtb)

    def fbcast(q):
        q.wait_ge(L.sem("Ac"), ac_bnd)
        q.matmul(ps_bb[:, :], L.ones_r128_f[0:1, :], nbq[0:1, :],
                 start=True, stop=True).then_inc(L.sem("Pe"), 1)
    pe_bnd = L.bump("Pe")
    ph.t(fbcast)

    def fccall(q):
        q.wait_ge(L.sem("Pe"), pe_bnd)
        q.activation(bcolall[:, :], ps_bb[:, :], AF.Identity,
                     scale=IM(-1.0)).then_inc(L.sem("Ac"), 1)
    L.bump("Ac")
    ph.s(fccall)
    ph.run()
    loc2.close()

    loc3 = contextlib.ExitStack()
    ps_sT = [L.ps("ps_sT", (128, 1536), stack=loc3) for _ in range(2)]
    ps_pv = L.ps("ps_pv", (65, 512), stack=loc3)
    ps_bc = L.ps("ps_bc", (64, 512), stack=loc3)
    pT = [L.sb("pT3", (128, 1536), BF16, loc3) for _ in range(2)]
    o32 = [L.sb("o32", (64, 512), F32, loc3) for _ in range(2)]
    obf = [L.sb("obf", (64, 512), BF16, loc3) for _ in range(2)]
    denr = [L.sb("denr", (1, 512), F32, loc3) for _ in range(2)]
    denr2 = [L.sb("denr2", (1, 512), F32, loc3) for _ in range(2)]
    rcpb = [L.sb("rcpb", (1, 512), BF16, loc3) for _ in range(2)]
    units = [(b, h, g) for b in range(2) for h in range(2)
             for g in range(SS // 512)]
    NU = len(units)
    NKT = SS // 128
    RS = [3] * 21 + [1]
    NR = len(RS)
    pe_S, pe_P, pe_bc = {}, {}, {}
    ac_exp = {}
    ve_pvfree, ve_den, ve_fvf2 = {}, {}, {}
    st_tgt = {}
    bank_exp = {}
    bank_pvb = {}
    ph = Ph(L)

    def em_Sb(u, r, with_wait):
        ub, uh, ug = units[u]
        base, q0 = uh * 64, ub * SS + ug * 512
        bs = (r % 2) * 3
        w = bank_exp.get(bs) if with_wait else None

        def f(q, base=base, q0=q0, ub=ub, r=r, bs=bs, w=w):
            if w is not None:
                q.wait_ge(L.sem("Ac"), w)
            for jj in range(RS[r]):
                kt = 3 * r + jj
                kc = ub * SS + kt * 128
                q.matmul(ps_sT[r % 2][:, jj * 512:(jj + 1) * 512],
                         kfm[base:base + 64, kc:kc + 128],
                         qfm[base:base + 64, q0:q0 + 512],
                         start=True, stop=True).then_inc(L.sem("Pe"), 1)
        for jj in range(RS[r]):
            pe_S[(u, 3 * r + jj)] = L.bump("Pe")
        ph.t(f)

    def em_expb(u, r):
        bs = (r % 2) * 3
        prevP = bank_pvb.get(bs)

        def f(q, u=u, r=r, bs=bs, prevP=prevP):
            q.wait_ge(L.sem("Pe"), pe_S[(u, 3 * r + RS[r] - 1)])
            if prevP is not None:
                q.wait_ge(L.sem("Pe"), prevP)
            q.activation(pT[r % 2][:, 0:RS[r] * 512],
                         ps_sT[r % 2][:, 0:RS[r] * 512],
                         AF.Exp, scale=IM(1.0),
                         bias=bcolall[0:128, u:u + 1]).then_inc(
                             L.sem("Ac"), 1)
        ac_exp[(u, r)] = L.bump("Ac")
        bank_exp[bs] = ac_exp[(u, r)]
        ph.s(f)

    def em_Pb(u, r):
        ub, uh, _ = units[u]
        bs = (r % 2) * 3

        def f(q, u=u, r=r, ub=ub, uh=uh, bs=bs):
            q.wait_ge(L.sem("Ac"), ac_exp[(u, r)])
            if r == 0 and u - 1 in ve_pvfree:
                q.wait_ge(L.sem("Ve"), ve_pvfree[u - 1])
            for jj in range(RS[r]):
                kt = 3 * r + jj
                c = (ub * 64 + kt) * 130 + uh * 65
                q.matmul(ps_pv[:, :], vta[:, c:c + 65],
                         pT[r % 2][:, jj * 512:(jj + 1) * 512],
                         start=(kt == 0),
                         stop=(kt == NKT - 1)).then_inc(L.sem("Pe"), 1)
        for jj in range(RS[r]):
            pe_P[(u, 3 * r + jj)] = L.bump("Pe")
        bank_pvb[bs] = pe_P[(u, 3 * r + RS[r] - 1)]
        ph.t(f)

    def em_fvf(u):
        du = u % 2

        def f(q, u=u, du=du):
            q.wait_ge(L.sem("Pe"), pe_P[(u, NKT - 1)])
            if u - 2 in ve_fvf2:
                q.wait_ge(L.sem("Ve"), ve_fvf2[u - 2])
            if u - 2 in pe_bc:
                q.wait_ge(L.sem("Pe"), pe_bc[u - 2])
            q.tensor_copy(denr2[du][0:1, :], ps_pv[64:65, :])
            q.tensor_copy(o32[du][0:64, :],
                          ps_pv[0:64, :]).then_inc(L.sem("Ve"), 1)
            q.reciprocal(denr[du][0:1, :], denr2[du][0:1, :])
            L.vsync(q)
            q.tensor_copy(rcpb[du][0:1, :],
                          denr[du][0:1, :]).then_inc(L.sem("Ve"), 1)
        ve_pvfree[u] = L.bump("Ve")
        ve_den[u] = L.bump("Ve")
        ph.v(f)

    def em_ftbc(u):
        du = u % 2

        def f(q, u=u, du=du):
            q.wait_ge(L.sem("Ve"), ve_den[u])
            if u - 1 in ve_fvf2:
                q.wait_ge(L.sem("Ve"), ve_fvf2[u - 1])
            q.matmul(ps_bc[:, :], L.ones_r128_b[0:1, 0:64],
                     rcpb[du][0:1, :],
                     start=True, stop=True).then_inc(L.sem("Pe"), 1)
        pe_bc[u] = L.bump("Pe")
        ph.t(f)

    def em_fvf2(u):
        du = u % 2

        def f(q, u=u, du=du):
            q.wait_ge(L.sem("Pe"), pe_bc[u])
            if u - 2 in st_tgt:
                q.wait_ge(L.sem("St"), st_tgt[u - 2])
            q.tensor_tensor(obf[du][0:64, :], o32[du][0:64, :],
                            ps_bc[0:64, :],
                            ALU.mult).then_inc(L.sem("Ve"), 1)
        ve_fvf2[u] = L.bump("Ve")
        ph.v(f)

    def em_store(u):
        ub, uh, ug = units[u]
        q0 = ub * SS + ug * 512
        du = u % 2

        def f(q, u=u, uh=uh, q0=q0, du=du):
            q.wait_ge(L.sem("Ve"), ve_fvf2[u])
            q.dma_start(out=att_o[uh * 64:(uh + 1) * 64, q0:q0 + 512],
                        in_=obf[du][0:64, :]).then_inc(L.sem("St"), 16)
        st_tgt[u] = L.bump("St", 16)
        ph.y(f)

    for u in range(NU):
        em_Sb(u, 0, True)
        em_Sb(u, 1, True)
        for r in range(NR):
            em_expb(u, r)
            em_Pb(u, r)
            if r + 2 < NR:
                em_Sb(u, r + 2, False)
            if r == 1 and u >= 1:
                em_ftbc(u - 1)
                em_fvf2(u - 1)
                em_store(u - 1)
        em_fvf(u)
    em_ftbc(NU - 1)
    em_fvf2(NU - 1)
    em_store(NU - 1)
    ph.run()
    loc3.close()
    L.pop_scope()
    if _own:
        L.finish()
    return L


def build_l3attnC(L=None):
    _own = L is None
    if _own:
        L = Launch("l3attnC")
    L.push_scope()
    x_in = L.dram_in("x_in", (DD, TC))
    att_i = L.dram_in("att", (DD, TC), BF16)
    wo = L.dram_in("wo", (1024, 1024))
    bo = L.dram_in("bo", (1024,))
    x_out = L.dram_out("x_out", (DD, TC))
    L.emit_consts()
    CH = 512
    x_sb = [L.sb("x", (128, CH), F32) for _ in range(8)]
    x0f = [L.sb("x0f", (128, CH), F32) for _ in range(8)]
    dmy = [L.sb("dmy", (128, CH), BF16) for _ in range(8)]
    at = [L.sb("at", (128, CH), BF16) for _ in range(8)]
    sq = L.sb("sq", (128, 8 * 512), F32)
    for ch in range(TC // CH):
        c0 = ch * CH
        ph = Ph(L)
        L.wait(ph, "y", "St", L.cnt("St"))
        for f in range(8):
            L.load(ph, x_sb[f][:, :], x_in[f * 128:(f + 1) * 128, c0:c0 + CH],
                   queue="sync")
            L.load(ph, at[f][:, :], att_i[f * 128:(f + 1) * 128, c0:c0 + CH],
                   queue="gpsimd")
        L.wait(ph, "v", "Ld", L.cnt("Ld"))
        L.wait(ph, "t", "Ld", L.cnt("Ld"))
        ph.run()
        L.ln_window([x[:, :] for x in x_sb], CH, [d[:, :] for d in dmy],
                    sq[:, :], out_f32=[x[:, :] for x in x0f])
        ph = Ph(L)

        def fv(q):
            for f in range(8):
                q.tensor_tensor(x_sb[f][:, :], x_sb[f][:, :], x0f[f][:, :],
                                ALU.add)
        ph.v(fv)
        ph.run()

        def evo(mi, c, ps_ap, ph, pe):
            return L.ev_addx(ph, pe, ps_ap, x_sb[mi][:, c * 512:(c + 1) * 512],
                             1.0)
        L.proj(rhs_tiles=[a[:, :] for a in at], Tn=CH, K=1024, M=1024,
               w_dram=wo, bias_dram=bo, evac=evo, mblock=512)
        ph = Ph(L)
        for f in range(8):
            L.store(ph, x_out[f * 128:(f + 1) * 128, c0:c0 + CH], x_sb[f][:, :])
        ph.run()
    L.pop_scope()
    if _own:
        L.finish()
    return L



def build_l1attnA(L=None):
    _own = L is None
    if _own:
        L = Launch("l1attnA")
    L.push_scope()
    HLO = 512
    TH = TC + HLO
    x_in = L.dram_in("x_in", (DD, TH))
    wq = L.dram_in("wq", (1024, 1024))
    wk = L.dram_in("wk", (1024, 1024))
    wv = L.dram_in("wv", (1024, 1024))
    wb = L.dram_in("wb", (1024, 8))
    cw_d = L.dram_in("cw", (128, 8 * 4 * 3))
    fw_d = L.dram_in("flw", (1024, 8))
    fb_d = L.dram_in("flb", (1, 8))
    w1 = L.dram_in("w1", (1024, 2048))
    b1 = L.dram_in("b1", (2048,))
    w2 = L.dram_in("w2", (2048, 32))
    b2 = L.dram_in("b2", (1, 32))
    q_o = L.dram_out("q_o", (1024, TC), BF16)
    k_o = L.dram_out("k_o", (1024, TC), BF16)
    v_o = L.dram_out("v_o", (1024, TC), BF16)
    beta_o = L.dram_out("beta_o", (8, TC))
    flr_o = L.dram_out("flr_o", (8, TC))
    fus_o = L.dram_out("fus_o", (32, TC))

    L.emit_consts()
    cw_sb = L.sb("cw", (128, 96), F32)
    ph = Ph(L)
    t = L.load(ph, cw_sb[:, :], cw_d)
    L.wait(ph, "v", "Ld", t)
    ph.run()

    xh = [L.sb("xh", (128, TH), BF16) for _ in range(8)]
    loc0 = contextlib.ExitStack()
    sq = L.sb("sq", (128, 8 * 512), F32, loc0)
    x_sc = [L.sb("xsc", (128, 512), F32, loc0) for _ in range(8)]
    for c0 in range(0, TH, 512):
        ph = Ph(L)
        L.wait(ph, "y", "St", L.cnt("St"))
        tg = [L.load(ph, x_sc[f][:, :], x_in[f * 128:(f + 1) * 128, c0:c0 + 512],
                     queue="sync") for f in range(8)]
        L.wait(ph, "v", "Ld", tg[-1])
        ph.run()
        L.ln_window([x[:, :] for x in x_sc], 512,
                    [h[:, c0:c0 + 512] for h in xh], sq[:, :])
    loc0.close()

    loc1 = contextlib.ExitStack()
    lin = [L.sb("lin", (128, TH), BF16, loc1) for _ in range(8)]
    cso = [L.sb("cso", (128, TC), BF16, loc1) for _ in range(8)]
    cvt = [L.sb("cvt", (128, 512), F32, loc1) for _ in range(2)]
    for wi_, out_d, cwoff in ((wq, q_o, 0), (wk, k_o, 32), (wv, v_o, 64)):
        def evl(mi, c, ps_ap, ph, pe):
            return L.ev_act(ph, pe, ps_ap,
                            lin[mi][:, c * 512:(c + 1) * 512], AF.Copy)
        L.proj(rhs_tiles=[h[:, 0:TH] for h in xh], Tn=TH, K=1024, M=1024,
               w_dram=wi_, evac=evl, mblock=512)
        ph = Ph(L)
        for f in range(8):
            for c in range(TC // 512):
                lo = HLO + c * 512
                ui = f * (TC // 512) + c
                par = ui % 2

                def fvc(q, f=f, c=c, lo=lo, par=par, cwoff=cwoff, ui=ui):
                    if ui >= 2:
                        q.wait_ge(L.sem("Ac"), L._cv_ac[ui - 2])
                    q.tensor_scalar(cvt[par][:, :], lin[f][:, lo - 3:lo + 509],
                                    cw_sb[:, cwoff + f * 4 + 0:cwoff + f * 4 + 1],
                                    None, ALU.mult)
                    for i in (1, 2, 3):
                        q.scalar_tensor_tensor(
                            cvt[par][:, :], lin[f][:, lo - 3 + i:lo - 3 + i + 512],
                            cw_sb[:, cwoff + f * 4 + i:cwoff + f * 4 + i + 1],
                            cvt[par][:, :], ALU.mult, ALU.add)
                    q.engine_nop().then_inc(L.sem("Ve"), 1)
                ve = L.bump("Ve")
                ph.v(fvc)

                def fsc(q, f=f, c=c, par=par, ve=ve):
                    q.wait_ge(L.sem("Ve"), ve)
                    q.activation(cso[f][:, c * 512:(c + 1) * 512],
                                 cvt[par][:, :], AF.Silu).then_inc(
                                     L.sem("Ac"), 1)
                L._cv_ac = getattr(L, "_cv_ac", {})
                L._cv_ac[ui] = L.bump("Ac")
                ph.s(fsc)
        ph.run()
        ph = Ph(L)
        L.wait(ph, "g", "St", L.cnt("St"))
        for f in range(8):
            L.store(ph, out_d[f * 128:(f + 1) * 128, :], cso[f][:, :],
                    queue="gpsimd")
        ph.run()

    loc1.close()
    loc2 = contextlib.ExitStack()
    ps8 = [L.ps("ps8", (8, 512), stack=loc2) for _ in range(2)]
    wb_sb = L.sb("wb", (128, 64), BF16, loc2)
    fw_sb = L.sb("fw", (128, 64), BF16, loc2)
    fb_sb = L.sb("fb", (1, 8), BF16, loc2)
    brow = L.sb("brow", (8, TC), F32, loc2)
    frow = L.sb("frow", (8, TC), F32, loc2)
    ph = Ph(L)
    L.load(ph, wb_sb[:, :].rearrange("p (k m) -> p k m", k=8),
           wb.rearrange("(k p) m -> p k m", p=128))
    L.load(ph, fw_sb[:, :].rearrange("p (k m) -> p k m", k=8),
           fw_d.rearrange("(k p) m -> p k m", p=128))
    t = L.load(ph, fb_sb[0:1, :], fb_d)
    ph.run()
    ph = Ph(L)
    for which, wsb, dst, useb in (("b", wb_sb, brow, False),
                                  ("f", fw_sb, frow, True)):
        for c in range(TC // 512):
            par = c % 2

            def ftb(q, which=which, wsb=wsb, c=c, par=par, useb=useb, t=t):
                if (which, c) == ("b", 0):
                    q.wait_ge(L.sem("Ld"), t)
                prev = L._bf_ac.get((which, c - 2)) if c >= 2 else \
                    (L._bf_ac.get(("b", TC // 512 - 2)) if which == "f" else None)
                if prev:
                    q.wait_ge(L.sem("Ac"), prev)
                mm = None
                for k in range(8):
                    mm = q.matmul(ps8[par][:, :], wsb[:, k * 8:(k + 1) * 8],
                                  xh[k][:, HLO + c * 512:HLO + (c + 1) * 512],
                                  start=(k == 0), stop=(k == 7) and not useb)
                if useb:
                    mm = q.matmul(ps8[par][:, :], fb_sb[0:1, :],
                                  L.ones_row_b[0:1, :], start=False, stop=True)
                mm.then_inc(L.sem("Pe"), 1)
            pe = L.bump("Pe")
            ph.t(ftb)

            def fsb(q, c=c, par=par, dst=dst, pe=pe):
                q.wait_ge(L.sem("Pe"), pe)
                q.activation(dst[:, c * 512:(c + 1) * 512], ps8[par][:, :],
                             AF.Sigmoid).then_inc(L.sem("Ac"), 1)
            L._bf_ac = getattr(L, "_bf_ac", {})
            L._bf_ac[(which, c)] = L.bump("Ac")
            ph.s(fsb)
    ph.run()
    ph = Ph(L)
    L.store(ph, beta_o[:, :], brow[:, :])
    L.store(ph, flr_o[:, :], frow[:, :])
    ph.run()
    loc2.close()

    loc3 = contextlib.ExitStack()
    g1 = [L.sb("g1", (128, 512), BF16, loc3) for _ in range(16)]
    w2_sb = L.sb("w2", (128, 16 * 32), BF16, loc3)
    b2_sb = L.sb("b2", (1, 32), BF16, loc3)
    fusr = L.sb("fusr", (32, TC), F32, loc3)
    ps32 = [L.ps("ps32", (32, 512), stack=loc3) for _ in range(2)]
    ph = Ph(L)
    L.load(ph, w2_sb[:, :].rearrange("p (k m) -> p k m", k=16),
           w2.rearrange("(k p) m -> p k m", p=128))
    t2 = L.load(ph, b2_sb[0:1, :], b2)
    ph.run()
    for c in range(TC // 512):
        def evg(mi, cc, ps_ap, ph, pe):
            return L.ev_act(ph, pe, ps_ap, g1[mi][:, :], AF.Gelu)
        L.proj(rhs_tiles=[h[:, HLO + c * 512:HLO + (c + 1) * 512] for h in xh],
               Tn=512, K=1024, M=2048, w_dram=w1, bias_dram=b1, evac=evg,
               mblock=512)
        ph = Ph(L)
        par = c % 2

        def ftf(q, c=c, par=par, t2=t2):
            if c == 0:
                q.wait_ge(L.sem("Ld"), t2)
            if c >= 2:
                q.wait_ge(L.sem("Ac"), L._fu_ac[c - 2])
            mm = None
            for k in range(16):
                mm = q.matmul(ps32[par][:, :], w2_sb[:, k * 32:(k + 1) * 32],
                              g1[k][:, :], start=(k == 0), stop=False)
            mm = q.matmul(ps32[par][:, :], b2_sb[0:1, :], L.ones_row_b[0:1, :],
                          start=False, stop=True)
            mm.then_inc(L.sem("Pe"), 1)
        pe = L.bump("Pe")
        ph.t(ftf)

        def fsf(q, c=c, par=par, pe=pe):
            q.wait_ge(L.sem("Pe"), pe)
            q.activation(fusr[:, c * 512:(c + 1) * 512], ps32[par][:, :],
                         AF.Copy).then_inc(L.sem("Ac"), 1)
        L._fu_ac = getattr(L, "_fu_ac", {})
        L._fu_ac[c] = L.bump("Ac")
        ph.s(fsf)
        ph.run()
    ph = Ph(L)
    L.store(ph, fus_o[:, :], fusr[:, :])
    ph.run()
    loc3.close()
    L.pop_scope()
    if _own:
        L.finish()
    return L


def build_l1attnC(L=None):
    _own = L is None
    if _own:
        L = Launch("l1attnC")
    L.push_scope()
    x_in = L.dram_in("x_in", (DD, TC))
    o_i = L.dram_in("o_i", (DD, TC), BF16)
    wo = L.dram_in("wo", (1024, 1024))
    x_out = L.dram_out("x_out", (DD, TC))
    L.emit_consts()
    CH = 512
    x_sb = [L.sb("x", (128, CH), F32) for _ in range(8)]
    ot = [L.sb("ot", (128, CH), BF16) for _ in range(8)]
    for ch in range(TC // CH):
        c0 = ch * CH
        ph = Ph(L)
        L.wait(ph, "y", "St", L.cnt("St"))
        for f in range(8):
            L.load(ph, x_sb[f][:, :], x_in[f * 128:(f + 1) * 128, c0:c0 + CH],
                   queue="sync")
            L.load(ph, ot[f][:, :], o_i[f * 128:(f + 1) * 128, c0:c0 + CH],
                   queue="gpsimd")
        L.wait(ph, "v", "Ld", L.cnt("Ld"))
        L.wait(ph, "t", "Ld", L.cnt("Ld"))
        ph.run()

        def evo(mi, c, ps_ap, ph, pe):
            return L.ev_addx(ph, pe, ps_ap, x_sb[mi][:, c * 512:(c + 1) * 512],
                             1.0)
        L.proj(rhs_tiles=[a[:, :] for a in ot], Tn=CH, K=1024, M=1024,
               w_dram=wo, evac=evo, mblock=512)
        ph = Ph(L)
        for f in range(8):
            L.store(ph, x_out[f * 128:(f + 1) * 128, c0:c0 + CH], x_sb[f][:, :])
        ph.run()
    L.pop_scope()
    if _own:
        L.finish()
    return L



def build_l1attnB(L=None):
    _own = L is None
    if _own:
        L = Launch("l1attnB")
    L.push_scope()
    T = 2 * SS
    PW = 2048
    NCHK = PW // 32
    q_i = L.dram_in("q_i", (128, T), BF16)
    k_i = L.dram_in("k_i", (128, T), BF16)
    v_i = L.dram_in("v_i", (128, T), BF16)
    b_i = L.dram_in("b_i", (1, T))
    f0_i = L.dram_in("f0", (1, T))
    f1_i = L.dram_in("f1", (1, T))
    f2_i = L.dram_in("f2", (1, T))
    f3_i = L.dram_in("f3", (1, T))
    fl_i = L.dram_in("fl", (1, T))
    fw_i = L.dram_in("fw", (128, 34))
    nw_i = L.dram_in("nw", (128, 2))
    mp_i = L.dram_in("MPAIR", (64, 64))
    mpl_i = L.dram_in("MPAIRL", (64, 64))
    ml_i = L.dram_in("MLOC", (32, 32))
    ey_i = L.dram_in("EYE64", (64, 64))
    o_o = L.dram_out("o_o", (128, T), BF16)

    L.emit_consts()
    fw_sb = L.sb("fw", (128, 34), F32)
    nw_sb = L.sb("nw", (128, 2), F32)
    mp_sb = L.sb("mp", (64, 64), F32)
    mpl_sb = L.sb("mpl", (64, 64), F32)
    ml_sb = L.sb("ml", (32, 32), F32)
    ey_sb = L.sb("ey", (64, 64), F32)
    S32 = L.sb("S32", (128, 128), F32)
    Sbf = L.sb("Sbf", (128, 128), BF16)
    vhist = L.sb("vhist", (128, 30), BF16)
    ph = Ph(L)
    L.load(ph, fw_sb[:, :], fw_i)
    L.load(ph, nw_sb[:, :], nw_i, queue="gpsimd")
    L.load(ph, mp_sb[:, :], mp_i, queue="gpsimd")
    L.load(ph, mpl_sb[:, :], mpl_i, queue="gpsimd")
    L.load(ph, ml_sb[:, :], ml_i, queue="gpsimd")
    t0 = L.load(ph, ey_sb[:, :], ey_i, queue="gpsimd")
    L.wait(ph, "v", "Ld", t0)
    L.wait(ph, "s", "Ld", t0)
    ph.run()

    qn = L.sb("qn", (128, PW), BF16)
    kn = L.sb("kn", (128, PW), BF16)
    kb = L.sb("kb", (128, PW), BF16)
    vf = L.sb("vf", (128, PW), BF16)
    vext = L.sb("vext", (128, PW + 30), BF16)
    kn_tm = [L.sb("kntm", (64, 128), BF16) for _ in range(PW // 64)]
    kb_tm = [L.sb("kbtm", (64, 128), BF16) for _ in range(PW // 64)]
    vb_tm = [L.sb("vbtm", (64, 128), BF16) for _ in range(PW // 64)]
    attnT = [L.sb("attnT", (64, 64), BF16) for _ in range(PW // 64)]
    o_pair = [L.sb("opair", (64, 128), BF16) for _ in range(PW // 64)]
    o_fm = L.sb("ofm", (128, PW), BF16)
    ls_fm = L.sb("lsfm", (128, PW), BF16)
    ll_fm = L.sb("llfm", (128, PW), BF16)
    rowA = L.sb("rowA", (1, PW), F32)
    rowB = L.sb("rowB", (1, PW), F32)
    rowC = L.sb("rowC", (1, PW), F32)
    rowD = L.sb("rowD", (1, PW), F32)
    bbf = L.sb("bbf", (1, PW), BF16)
    ctxb = [L.sb("ctxb", (1, PW), BF16) for _ in range(4)]
    srow = L.sb("srow", (1, 512), F32)
    sqB = L.sb("sqB", (128, 512), F32)
    un64 = L.sb("un64", (64, 128), BF16)
    u64 = L.sb("u64", (64, 128), BF16)
    w_sb = [L.sb("w", (128, 32), BF16) for _ in range(2)]
    al64 = L.sb("al64", (64, 64), BF16)
    ppair = [L.sb("ppair", (64, 64), BF16) for _ in range(2)]
    qpair = [L.sb("qpair", (64, 64), BF16) for _ in range(2)]
    rpair = [L.sb("rpair", (64, 64), BF16) for _ in range(2)]
    rtpair = [L.sb("rtpair", (64, 64), BF16) for _ in range(2)]

    for p in range(8):
        b, seg = p // 4, p % 4
        c0g = p * PW

        ph = Ph(L)
        L.wait(ph, "y", "St", L.cnt("St"))
        L.load(ph, qn[:, :], q_i[:, c0g:c0g + PW], queue="sync")
        L.load(ph, kn[:, :], k_i[:, c0g:c0g + PW], queue="sync")
        L.load(ph, vf[:, :], v_i[:, c0g:c0g + PW], queue="sync")
        L.load(ph, bbf[0:1, :], b_i[0:1, c0g:c0g + PW], queue="gpsimd")
        for j, fi in enumerate((f0_i, f1_i, f2_i)):
            L.load(ph, rowA[0:1, :] if j == 0 else
                   (rowB[0:1, :] if j == 1 else rowC[0:1, :]),
                   fi[0:1, c0g:c0g + PW], queue="gpsimd")
        L.load(ph, rowD[0:1, :], f3_i[0:1, c0g:c0g + PW], queue="gpsimd")
        tfl = L.load(ph, srow[0:1, 0:512], fl_i[0:1, c0g:c0g + 512],
                     queue="gpsimd") if False else None
        L.wait(ph, "v", "Ld", L.cnt("Ld"))
        L.wait(ph, "s", "Ld", L.cnt("Ld"))
        L.wait(ph, "t", "Ld", L.cnt("Ld"))
        ph.run()

        locg = contextlib.ExitStack()
        flr = L.sb("flr", (1, PW), F32, locg)
        e0 = L.sb("e0", (1, PW), F32, locg)
        e1 = L.sb("e1", (1, PW), F32, locg)
        e2 = L.sb("e2", (1, PW), F32, locg)
        den = L.sb("den", (1, PW), F32, locg)
        pvr = L.sb("pvr", (1, PW), F32, locg)
        ph = Ph(L)
        tfl = L.load(ph, flr[0:1, :], fl_i[0:1, c0g:c0g + PW], queue="gpsimd")

        def fsg(q, tfl=tfl):
            q.wait_ge(L.sem("Ld"), tfl)
            q.activation(e0[0:1, :], rowA[0:1, :], AF.Exp)
            q.activation(e1[0:1, :], rowB[0:1, :], AF.Exp)
            q.activation(e2[0:1, :], rowC[0:1, :], AF.Exp)
            q.activation(pvr[0:1, :], rowD[0:1, :],
                         AF.Sigmoid).then_inc(L.sem("Ac"), 1)
        ac = L.bump("Ac")
        ph.s(fsg)

        def fvg(q, ac=ac):
            q.wait_ge(L.sem("Ac"), ac)
            q.tensor_tensor(den[0:1, :], e0[0:1, :], e1[0:1, :], ALU.add)
            L.vsync(q)
            q.tensor_tensor(den[0:1, :], den[0:1, :], e2[0:1, :], ALU.add)
            L.vsync(q)
            q.reciprocal(rowA[0:1, :], den[0:1, :])
            L.vsync(q)
            q.tensor_scalar(rowB[0:1, :], flr[0:1, :], -0.2, 1.0,
                            ALU.mult, ALU.add)
            L.vsync(q)
            q.tensor_tensor(pvr[0:1, :], pvr[0:1, :], rowB[0:1, :], ALU.mult)
            L.vsync(q)
            q.tensor_scalar(rowB[0:1, :], pvr[0:1, :], -1.0, 1.0,
                            ALU.mult, ALU.add)
            L.vsync(q)
            q.tensor_tensor(rowC[0:1, :], rowB[0:1, :], rowA[0:1, :], ALU.mult)
            L.vsync(q)
            q.tensor_tensor(ctxb[0][0:1, :], e0[0:1, :], rowC[0:1, :], ALU.mult)
            q.tensor_tensor(ctxb[1][0:1, :], e1[0:1, :], rowC[0:1, :], ALU.mult)
            q.tensor_tensor(ctxb[2][0:1, :], e2[0:1, :], rowC[0:1, :], ALU.mult)
            q.tensor_copy(ctxb[3][0:1, :], pvr[0:1, :]).then_inc(L.sem("Ve"), 1)
        L.bump("Ve")
        ph.v(fvg)
        ph.run()
        locg.close()

        locn = contextlib.ExitStack()
        ps_r = L.ps("ps_r", (1, 512), stack=locn)
        ps_b = [L.ps("ps_b", (128, 512), stack=locn) for _ in range(2)]
        ps_t = [L.ps("ps_tB", (64, 128), BF16, stack=locn) for _ in range(2)]
        vbt = L.sb("vbt", (128, 512), BF16, locn)
        for cc in range(PW // 512):
            sl = slice(cc * 512, (cc + 1) * 512)
            for which, arr in (("q", qn), ("k", kn)):
                ph = Ph(L)

                def fsn(q, arr=arr, sl=sl):
                    q.activation(sqB[:, :], arr[:, sl],
                                 AF.Square).then_inc(L.sem("Ac"), 1)
                ac = L.bump("Ac")
                ph.s(fsn)

                def ftn(q, ac=ac):
                    q.wait_ge(L.sem("Ac"), ac)
                    q.matmul(ps_r[:, :], L.ones_col_f[:, :], sqB[:, :],
                             start=True, stop=True).then_inc(L.sem("Pe"), 1)
                pe = L.bump("Pe")
                ph.t(ftn)

                def fvn(q, pe=pe):
                    q.wait_ge(L.sem("Pe"), pe)
                    q.tensor_copy(srow[0:1, :], ps_r[:, :])
                    L.vsync(q)
                    q.engine_nop().then_inc(L.sem("Ve"), 1)
                ve = L.bump("Ve")
                ph.v(fvn)

                def fsn2(q, ve=ve):
                    q.wait_ge(L.sem("Ve"), ve)
                    q.activation(srow[0:1, :], srow[0:1, :], AF.Ln,
                                 bias=IM(1e-6))
                    q.activation(srow[0:1, :], srow[0:1, :], AF.Exp,
                                 scale=IM(-0.5)).then_inc(L.sem("Ac"), 1)
                ac2 = L.bump("Ac")
                ph.s(fsn2)

                def ftn2(q, ac2=ac2):
                    q.wait_ge(L.sem("Ac"), ac2)
                    q.matmul(ps_b[0][:, :], L.ones_r128_f[:, :], srow[0:1, :],
                             start=True, stop=True).then_inc(L.sem("Pe"), 1)
                pe2 = L.bump("Pe")
                ph.t(ftn2)

                def fvn2(q, arr=arr, sl=sl, pe2=pe2):
                    q.wait_ge(L.sem("Pe"), pe2)
                    q.tensor_tensor(arr[:, sl], arr[:, sl], ps_b[0][:, :],
                                    ALU.mult).then_inc(L.sem("Ve"), 1)
                L.bump("Ve")
                ph.v(fvn2)
                ph.run()
            ph = Ph(L)

            def ftb(q, sl=sl, cc=cc):
                q.matmul(ps_b[1][:, :], L.ones_r128_b[0:1, :],
                         bbf[0:1, sl], start=True, stop=True).then_inc(
                             L.sem("Pe"), 1)
            pe3 = L.bump("Pe")
            ph.t(ftb)

            def fvb(q, sl=sl, pe3=pe3):
                q.wait_ge(L.sem("Pe"), pe3)
                q.tensor_tensor(kb[:, sl], kn[:, sl], ps_b[1][:, :], ALU.mult)
                q.tensor_tensor(vbt[:, :], vf[:, sl], ps_b[1][:, :],
                                ALU.mult).then_inc(L.sem("Ve"), 1)
            veb = L.bump("Ve")
            ph.v(fvb)

            for pp in range(8):
                gp = cc * 8 + pp
                csl = slice(cc * 512 + pp * 64, cc * 512 + (pp + 1) * 64)

                def ftt(q, gp=gp, csl=csl, pp=pp, veb=veb):
                    if pp == 0:
                        q.wait_ge(L.sem("Ve"), veb)
                    if gp % 8 >= 2 or True:
                        prev = L._t_ve.get(gp - 1)
                        if prev:
                            q.wait_ge(L.sem("Ve"), prev)
                    q.transpose(ps_t[0][:, :], kn[:, csl], L.ident_b[:, :])
                    q.transpose(ps_t[1][:, :], kb[:, csl],
                                L.ident_b[:, :]).then_inc(L.sem("Pe"), 1)
                pe4 = L.bump("Pe")
                ph.t(ftt)

                def fvt(q, gp=gp, pe4=pe4):
                    q.wait_ge(L.sem("Pe"), pe4)
                    q.tensor_copy(kn_tm[gp][:, :], ps_t[0][:, :])
                    q.tensor_copy(kb_tm[gp][:, :], ps_t[1][:, :]).then_inc(
                        L.sem("Ve"), 1)
                L._t_ve = getattr(L, "_t_ve", {})
                L._t_ve[gp] = L.bump("Ve")
                ph.v(fvt)

                def ftt2(q, gp=gp, pp=pp, veT=L._t_ve[gp]):
                    q.wait_ge(L.sem("Ve"), veT)
                    q.transpose(ps_t[0][:, :], vbt[:, pp * 64:(pp + 1) * 64],
                                L.ident_b[:, :]).then_inc(L.sem("Pe"), 1)
                pe5 = L.bump("Pe")
                ph.t(ftt2)

                def fvt2(q, gp=gp, pe5=pe5):
                    q.wait_ge(L.sem("Pe"), pe5)
                    q.tensor_copy(vb_tm[gp][:, :], ps_t[0][:, :]).then_inc(
                        L.sem("Ve"), 1)
                L._t_ve[gp] = L.bump("Ve")
                ph.v(fvt2)
            ph.run()
        locn.close()

        loci = contextlib.ExitStack()
        psi = [L.ps("psi", (64, 64), stack=loci) for _ in range(4)]
        for gp in range(PW // 64):
            psl = slice(gp * 64, (gp + 1) * 64)
            ph = Ph(L)

            def fti(q, psl=psl, gp=gp):
                q.matmul(psi[0][:, :], kn[:, psl], kb[:, psl],
                         start=True, stop=True)
                q.matmul(psi[1][:, :], kb[:, psl], kn[:, psl],
                         start=True, stop=True).then_inc(L.sem("Pe"), 1)
            pe = L.bump("Pe")
            ph.t(fti)

            def fvi(q, pe=pe):
                q.wait_ge(L.sem("Pe"), pe)
                q.tensor_tensor(ppair[0][:, :], psi[0][:, :], mp_sb[:, :],
                                ALU.mult)
                q.tensor_tensor(qpair[0][:, :], psi[1][:, :], mpl_sb[:, :],
                                ALU.mult)
                L.vsync(q)
                q.tensor_tensor(rpair[0][:, :], ppair[0][:, :], ey_sb[:, :],
                                ALU.add)
                q.tensor_tensor(rtpair[0][:, :], qpair[0][:, :], ey_sb[:, :],
                                ALU.add).then_inc(L.sem("Ve"), 1)
            ve = L.bump("Ve")
            ph.v(fvi)
            last_ve = ve
            for lev in range(4):
                def ftl(q, lev=lev, last_ve=last_ve):
                    q.wait_ge(L.sem("Ve"), last_ve)
                    q.matmul(psi[0][:, :], qpair[lev % 2][:, :],
                             ppair[lev % 2][:, :], start=True, stop=True)
                    q.matmul(psi[1][:, :], ppair[lev % 2][:, :],
                             qpair[lev % 2][:, :], start=True,
                             stop=True).then_inc(L.sem("Pe"), 1)
                pe = L.bump("Pe")
                ph.t(ftl)

                def fvl(q, lev=lev, pe=pe):
                    q.wait_ge(L.sem("Pe"), pe)
                    q.tensor_copy(ppair[(lev + 1) % 2][:, :], psi[0][:, :])
                    q.tensor_copy(qpair[(lev + 1) % 2][:, :],
                                  psi[1][:, :]).then_inc(L.sem("Ve"), 1)
                ve2 = L.bump("Ve")
                ph.v(fvl)

                def ftl2(q, lev=lev, ve2=ve2):
                    q.wait_ge(L.sem("Ve"), ve2)
                    q.matmul(psi[2][:, :], rtpair[lev % 2][:, :],
                             ppair[(lev + 1) % 2][:, :], start=True, stop=True)
                    q.matmul(psi[3][:, :], ppair[(lev + 1) % 2][:, :],
                             rtpair[lev % 2][:, :], start=True,
                             stop=True).then_inc(L.sem("Pe"), 1)
                pe2 = L.bump("Pe")
                ph.t(ftl2)

                def fvl2(q, lev=lev, pe2=pe2):
                    q.wait_ge(L.sem("Pe"), pe2)
                    q.tensor_tensor(rpair[(lev + 1) % 2][:, :],
                                    rpair[lev % 2][:, :], psi[2][:, :], ALU.add)
                    q.tensor_tensor(rtpair[(lev + 1) % 2][:, :],
                                    rtpair[lev % 2][:, :], psi[3][:, :],
                                    ALU.add).then_inc(L.sem("Ve"), 1)
                last_ve = L.bump("Ve")
                ph.v(fvl2)

            def fvfin(q, gp=gp, last_ve=last_ve):
                q.wait_ge(L.sem("Ve"), last_ve)
                q.tensor_copy(attnT[gp][:, :], rpair[0][:, :]).then_inc(
                    L.sem("Ve"), 1)
            L.bump("Ve")
            ph.v(fvfin)
            ph.run()
        loci.close()

        locs = contextlib.ExitStack()
        psU = L.ps("psU", (32, 128), stack=locs)
        psW = L.ps("psW", (128, 32), stack=locs)
        psWS = L.ps("psWS", (32, 128), stack=locs)
        psAL = L.ps("psAL", (32, 32), stack=locs)
        psO = L.ps("psO", (32, 128), stack=locs)
        psSU = L.ps("psSU", (128, 128), stack=locs)
        if seg == 0:
            ph = Ph(L)

            def fz(q):
                q.memset(S32[:, :], 0.0)
                q.memset(Sbf[:, :], 0.0)
            ph.g(fz)
            ph.run()
        ph = Ph(L)
        for c in range(NCHK):
            gp, s = c // 2, c % 2
            bs = 32 * s
            fsl = slice(c * 32, (c + 1) * 32)

            def ft1(q, c=c, gp=gp, bs=bs, fsl=fsl):
                if c >= 1:
                    q.wait_ge(L.sem("Ve"), L._s_ve[c - 1])
                q.matmul(psU[:, :], attnT[gp][bs:bs + 32, bs:bs + 32],
                         vb_tm[gp][bs:bs + 32, :], start=True, stop=True)
                q.matmul(psW[:, :], kb_tm[gp][bs:bs + 32, :],
                         attnT[gp][bs:bs + 32, bs:bs + 32],
                         start=True, stop=True)
                q.matmul(psAL[:, :], kn[:, fsl], qn[:, fsl],
                         start=True, stop=True).then_inc(L.sem("Pe"), 1)
            pe1 = L.bump("Pe")
            ph.t(ft1)

            def fv1(q, c=c, bs=bs, pe1=pe1):
                q.wait_ge(L.sem("Pe"), pe1)
                q.tensor_copy(u64[bs:bs + 32, :], psU[:, :])
                q.tensor_copy(w_sb[c % 2][:, :], psW[:, :])
                q.tensor_tensor(al64[bs:bs + 32, bs:bs + 32], psAL[:, :],
                                ml_sb[:, :], ALU.mult).then_inc(L.sem("Ve"), 1)
            ve1 = L.bump("Ve")
            ph.v(fv1)

            def ft2(q, c=c, ve1=ve1):
                q.wait_ge(L.sem("Ve"), ve1)
                q.matmul(psWS[:, :], w_sb[c % 2][:, :], Sbf[:, :],
                         start=True, stop=True).then_inc(L.sem("Pe"), 1)
            pe2 = L.bump("Pe")
            ph.t(ft2)

            def fv2(q, c=c, bs=bs, pe2=pe2):
                q.wait_ge(L.sem("Pe"), pe2)
                q.tensor_tensor(un64[bs:bs + 32, :], u64[bs:bs + 32, :],
                                psWS[:, :], ALU.subtract).then_inc(
                                    L.sem("Ve"), 1)
            ve2 = L.bump("Ve")
            ph.v(fv2)

            def ft3(q, c=c, gp=gp, bs=bs, fsl=fsl, ve2=ve2):
                q.wait_ge(L.sem("Ve"), ve2)
                q.matmul(psO[:, :], qn[:, fsl], Sbf[:, :],
                         start=True, stop=False)
                q.matmul(psO[:, :], al64[bs:bs + 32, bs:bs + 32],
                         un64[bs:bs + 32, :], start=False, stop=True)
                q.matmul(psSU[:, :], kn_tm[gp][bs:bs + 32, :],
                         un64[bs:bs + 32, :], start=True,
                         stop=True).then_inc(L.sem("Pe"), 1)
            pe3 = L.bump("Pe")
            ph.t(ft3)

            def fv3(q, c=c, gp=gp, bs=bs, pe3=pe3):
                q.wait_ge(L.sem("Pe"), pe3)
                q.tensor_tensor(S32[:, :], S32[:, :], psSU[:, :], ALU.add)
                L.vsync(q)
                q.tensor_copy(Sbf[:, :], S32[:, :]).then_inc(L.sem("Ve"), 1)
            ve3 = L.bump("Ve")
            ph.v(fv3)

            def fs3(q, c=c, gp=gp, bs=bs, pe3=pe3):
                q.wait_ge(L.sem("Pe"), pe3)
                q.activation(o_pair[gp][bs:bs + 32, :], psO[:, :],
                             AF.Copy).then_inc(L.sem("Ac"), 1)
            acS = L.bump("Ac")
            ph.s(fs3)
            L._s_ve = getattr(L, "_s_ve", {})
            L._s_ve[c] = ve3
        ph.run()
        locs.close()

        locf = contextlib.ExitStack()
        ph = Ph(L)

        def fvx(q, seg=seg):
            if seg == 0:
                q.memset(vext[:, 0:30], 0.0)
            else:
                q.tensor_copy(vext[:, 0:30], vhist[:, :])
            q.tensor_copy(vext[:, 30:30 + PW], vf[:, :])
            L.vsync(q)
            q.tensor_copy(vhist[:, :], vf[:, PW - 30:PW])
            L.vsync(q)
            for cc in range(PW // 512):
                ob = cc * 512
                q.tensor_scalar(ls_fm[:, ob:ob + 512],
                                vext[:, ob + 30:ob + 542], fw_sb[:, 2:3],
                                None, ALU.mult)
                for i in (1, 2):
                    q.scalar_tensor_tensor(
                        ls_fm[:, ob:ob + 512], vext[:, ob + 30 - i:ob + 542 - i],
                        fw_sb[:, 2 - i:3 - i], ls_fm[:, ob:ob + 512],
                        ALU.mult, ALU.add)
                q.tensor_scalar(ll_fm[:, ob:ob + 512],
                                vext[:, ob + 30:ob + 542], fw_sb[:, 33:34],
                                None, ALU.mult)
                for i in range(1, 31):
                    q.scalar_tensor_tensor(
                        ll_fm[:, ob:ob + 512], vext[:, ob + 30 - i:ob + 542 - i],
                        fw_sb[:, 33 - i:34 - i], ll_fm[:, ob:ob + 512],
                        ALU.mult, ALU.add)
            q.engine_nop().then_inc(L.sem("Ve"), 1)
        vef = L.bump("Ve")
        ph.v(fvx)
        ph.run()

        ps_t2 = [L.ps("ps_t2", (128, 64), BF16, stack=locf) for _ in range(2)]
        ph = Ph(L)
        for gp in range(PW // 64):
            def fto(q, gp=gp):
                if gp >= 2:
                    q.wait_ge(L.sem("Ve"), L._o_ve[gp - 2])
                q.transpose(ps_t2[gp % 2][:, :], o_pair[gp][:, :],
                            L.ident_b[0:64, 0:64]).then_inc(L.sem("Pe"), 1)
            pe = L.bump("Pe")
            ph.t(fto)

            def fvo(q, gp=gp, pe=pe):
                q.wait_ge(L.sem("Pe"), pe)
                q.tensor_copy(o_fm[:, gp * 64:(gp + 1) * 64],
                              ps_t2[gp % 2][:, :]).then_inc(L.sem("Ve"), 1)
            L._o_ve = getattr(L, "_o_ve", {})
            L._o_ve[gp] = L.bump("Ve")
            ph.v(fvo)
        ph.run()

        ps_g = [L.ps("ps_g", (128, 512), stack=locf) for _ in range(4)]
        ps_rr = L.ps("ps_rr", (1, 512), stack=locf)
        ps_bb = L.ps("ps_bb", (128, 512), stack=locf)
        cmb = L.sb("cmb", (128, 512), F32, locf)
        for cc in range(PW // 512):
            sl = slice(cc * 512, (cc + 1) * 512)
            ph = Ph(L)

            def ftc(q, sl=sl):
                mm = None
                for j in range(4):
                    mm = q.matmul(ps_g[j][:, :], L.ones_r128_b[0:1, :],
                                  ctxb[j][0:1, sl], start=True, stop=True)
                mm.then_inc(L.sem("Pe"), 1)
            pe = L.bump("Pe")
            ph.t(ftc)

            def fvc(q, sl=sl, cc=cc, pe=pe):
                q.wait_ge(L.sem("Pe"), pe)
                q.tensor_tensor(cmb[:, :], ls_fm[:, sl], ps_g[0][:, :],
                                ALU.mult)
                q.tensor_tensor(sqB[:, :], ll_fm[:, sl], ps_g[1][:, :],
                                ALU.mult)
                q.tensor_tensor(cmb[:, :], cmb[:, :], sqB[:, :], ALU.add)
                q.tensor_tensor(sqB[:, :], o_fm[:, sl], ps_g[2][:, :], ALU.mult)
                q.tensor_tensor(cmb[:, :], cmb[:, :], sqB[:, :], ALU.add)
                q.tensor_tensor(sqB[:, :], vf[:, sl], ps_g[3][:, :], ALU.mult)
                q.tensor_tensor(cmb[:, :], cmb[:, :], sqB[:, :],
                                ALU.add).then_inc(L.sem("Ve"), 1)
            ve = L.bump("Ve")
            ph.v(fvc)
            for r in range(2):
                def fsr(q, r=r, ve=ve):
                    if r == 0:
                        q.wait_ge(L.sem("Ve"), ve)
                    else:
                        q.wait_ge(L.sem("Ve"), L._r_ve[0])
                    q.activation(sqB[:, :], cmb[:, :],
                                 AF.Square).then_inc(L.sem("Ac"), 1)
                acr = L.bump("Ac")
                ph.s(fsr)

                def ftr(q, acr=acr):
                    q.wait_ge(L.sem("Ac"), acr)
                    q.matmul(ps_rr[:, :], L.ones_col_f[:, :], sqB[:, :],
                             start=True, stop=True).then_inc(L.sem("Pe"), 1)
                per = L.bump("Pe")
                ph.t(ftr)

                def fvr(q, per=per):
                    q.wait_ge(L.sem("Pe"), per)
                    q.tensor_copy(srow[0:1, :], ps_rr[:, :])
                    L.vsync(q)
                    q.engine_nop().then_inc(L.sem("Ve"), 1)
                ver = L.bump("Ve")
                ph.v(fvr)

                def fsr2(q, ver=ver):
                    q.wait_ge(L.sem("Ve"), ver)
                    q.activation(srow[0:1, :], srow[0:1, :], AF.Ln,
                                 bias=IM(EPS_LN), scale=IM(1.0 / 128.0))
                    q.activation(srow[0:1, :], srow[0:1, :], AF.Exp,
                                 scale=IM(-0.5)).then_inc(L.sem("Ac"), 1)
                acr2 = L.bump("Ac")
                ph.s(fsr2)

                def ftr2(q, acr2=acr2):
                    q.wait_ge(L.sem("Ac"), acr2)
                    q.matmul(ps_bb[:, :], L.ones_r128_f[:, :], srow[0:1, :],
                             start=True, stop=True).then_inc(L.sem("Pe"), 1)
                per2 = L.bump("Pe")
                ph.t(ftr2)

                def fvr2(q, r=r, per2=per2):
                    q.wait_ge(L.sem("Pe"), per2)
                    q.tensor_tensor(cmb[:, :], cmb[:, :], ps_bb[:, :],
                                    ALU.mult).then_inc(L.sem("Ve"), 1)
                vrr = L.bump("Ve")
                ph.v(fvr2)

                def fsr3(q, r=r, vrr=vrr):
                    q.wait_ge(L.sem("Ve"), vrr)
                    q.activation(cmb[:, :], cmb[:, :], AF.Identity,
                                 scale=nw_sb[:, r:r + 1]).then_inc(
                                     L.sem("Ac"), 1)
                L._r_ve = getattr(L, "_r_ve", {})
                L._r_ve[0] = L.bump("Ac")
                ph.s(fsr3)
            lastac = L._r_ve[0]

            def fvfin(q, sl=sl, lastac=lastac):
                q.wait_ge(L.sem("Ac"), lastac)
                q.tensor_copy(o_fm[:, sl], cmb[:, :]).then_inc(L.sem("Ve"), 1)
            L.bump("Ve")
            ph.v(fvfin)
            ph.run()
        locf.close()

        ph = Ph(L)
        L.store(ph, o_o[:, c0g:c0g + PW], o_fm[:, :])
        ph.run()
    L.pop_scope()
    if _own:
        L.finish()
    return L



def _ident_np():
    return np.eye(128, dtype=np.float32)


def shard_x(x_full):
    out = []
    for c in range(NCORES):
        b, s = c // 4, (c % 4) * TC
        out.append(np.ascontiguousarray(x_full[b, s:s + TC, :].T))
    return out


def unshard_x(xs):
    out = np.empty((BB, SS, DD), np.float32)
    for c in range(NCORES):
        b, s = c // 4, (c % 4) * TC
        out[b, s:s + TC, :] = xs[c].T
    return out


def gbcols(vec_g, vec_b):
    g = np.ascontiguousarray(vec_g.reshape(8, 128).T)
    b = np.ascontiguousarray(vec_b.reshape(8, 128).T)
    return np.concatenate([g, b], axis=1).astype(np.float32)


_PROGS = {}


def get_prog(name, builder):
    if name not in _PROGS:
        _PROGS[name] = builder()
    return _PROGS[name]


def run_l3ffn(xs, inputs):
    L = get_prog("l3ffn", build_l3ffn)
    common = {
        "IDENT": _ident_np(),
        "fc1_q": inputs["L3_ffn_fc1_q"],
        "fc1_s": inputs["L3_ffn_fc1_scales"],
        "fc1_b": inputs["L3_ffn_fc1_bias"],
        "fc2_q": inputs["L3_ffn_fc2_q"],
        "fc2_s": inputs["L3_ffn_fc2_scales"],
        "fc2_b": inputs["L3_ffn_fc2_bias"],
        "gb": gbcols(inputs["L3_ffn_ln_gamma"], inputs["L3_ffn_ln_beta"]),
    }
    in_maps = [dict(common, x_in=xs[c]) for c in range(NCORES)]
    res = run_bass_kernel_spmd(L.nc, in_maps, core_ids=list(range(NCORES)))
    return [np.asarray(r["x_out"]) for r in res.results]


def run_l2ffn(xs, inputs):
    L = get_prog("l2ffn", build_l2ffn)
    scols = np.concatenate(
        [np.ascontiguousarray(inputs["L2_ffn_" + k].reshape(-1, 128).T)
         for k in ["w1_s", "w2_s", "h1_s", "h2_s", "w3_s"]], axis=1)
    common = {
        "IDENT": _ident_np(),
        "w1": np.ascontiguousarray(inputs["L2_ffn_w1_w"].T),
        "w2": np.ascontiguousarray(inputs["L2_ffn_w2_w"].T),
        "h1": np.ascontiguousarray(inputs["L2_ffn_h1_w"].T),
        "h2": np.ascontiguousarray(inputs["L2_ffn_h2_w"].T),
        "w3": np.ascontiguousarray(inputs["L2_ffn_w3_w"].T),
        "scols": np.ascontiguousarray(scols, dtype=np.float32),
    }
    in_maps = [dict(common, x_in=xs[c]) for c in range(NCORES)]
    res = run_bass_kernel_spmd(L.nc, in_maps, core_ids=list(range(NCORES)))
    return [np.asarray(r["x_out"]) for r in res.results]


def run_l0ffn(xs, inputs):
    L = get_prog("l0ffn", build_l0ffn)
    lA = inputs["L0_ffn_lora_A"]
    A_sb = np.ascontiguousarray(
        lA.transpose(2, 0, 1).reshape(8, 128, 128).transpose(1, 0, 2).reshape(128, 1024))
    lB = inputs["L0_ffn_lora_B"]
    B_all = np.ascontiguousarray(
        lB.transpose(0, 2, 1).reshape(128, 4096))
    oneh = np.zeros((8, 8 * 128), np.float32)
    for e in range(8):
        oneh[e, e * 128:(e + 1) * 128] = 1.0
    common = {
        "IDENT": _ident_np(),
        "ONEHOT8": oneh,
        "wi": np.ascontiguousarray(inputs["L0_ffn_wi"].T),
        "bi": inputs["L0_ffn_bi"],
        "wo": np.ascontiguousarray(inputs["L0_ffn_wo"].T),
        "bo": inputs["L0_ffn_bo"],
        "gw": np.ascontiguousarray(inputs["L0_ffn_gate_w"].T),
        "gbias": np.ascontiguousarray(inputs["L0_ffn_gate_b"].reshape(1, 8)),
        "A_sb": A_sb.astype(np.float32),
        "B_all": B_all.astype(np.float32),
    }
    in_maps = [dict(common, x_in=xs[c]) for c in range(NCORES)]
    res = run_bass_kernel_spmd(L.nc, in_maps, core_ids=list(range(NCORES)))
    return [np.asarray(r["x_out"]) for r in res.results]


def run_l1ffn(xs, inputs):
    L = get_prog("l1ffn", build_l1ffn)
    rm = inputs["L1_ffn_rail_maker"]
    rup = inputs["L1_ffn_rail_up_proj"]
    bd1 = np.zeros((4, 8, 128, 128), np.float32)
    for h in range(4):
        for f in range(8):
            for rr in range(8):
                r = f * 8 + rr
                bd1[h, f, rr * 16:(rr + 1) * 16, rr * 16:(rr + 1) * 16] = rm[h, r]
    bd1 = bd1.reshape(4 * 8 * 128, 128)
    bd2 = np.zeros((4, 32, 128, 128), np.float32)
    for h in range(4):
        for m in range(32):
            for rr in range(2):
                r = m * 2 + rr
                f = m // 4
                krow = (r - f * 8) * 16
                bd2[h, m, krow:krow + 16, rr * 64:(rr + 1) * 64] = \
                    rup[h, r].T
    bd2 = bd2.reshape(4 * 32 * 128, 128)
    common = {
        "IDENT": _ident_np(),
        "bd1": np.ascontiguousarray(bd1),
        "bd2": np.ascontiguousarray(bd2),
        "cw": np.ascontiguousarray(inputs["L1_ffn_w_cproj"].T),
    }
    in_maps = [dict(common, x_in=xs[c]) for c in range(NCORES)]
    res = run_bass_kernel_spmd(L.nc, in_maps, core_ids=list(range(NCORES)))
    return [np.asarray(r["x_out"]) for r in res.results]


def _l2perm():
    n = np.arange(1024)
    j, qt = n // 256, n % 256
    return qt * 4 + j


def run_l2attn(xs, inputs):
    perm = _l2perm()
    LA = get_prog("l2attnA", build_l2attnA)
    wkT = np.ascontiguousarray(inputs["L2_attn_Wk"].T[:, perm])
    wvT = np.ascontiguousarray(inputs["L2_attn_Wv"].T[:, perm])
    commonA = {
        "IDENT": _ident_np(),
        "wk": wkT, "bk": np.ascontiguousarray(inputs["L2_attn_bk"][perm]),
        "wv": wvT, "bv": np.ascontiguousarray(inputs["L2_attn_bv"][perm]),
    }
    in_maps = [dict(commonA, x_in=xs[c]) for c in range(NCORES)]
    resA = run_bass_kernel_spmd(LA.nc, in_maps, core_ids=list(range(NCORES)))
    bound = [np.asarray(r["bound"]) for r in resA.results]
    kn = [np.asarray(r["kn"]) for r in resA.results]

    LB = get_prog("l2attnB", build_l2attnB)
    rsq = (1.0 / np.sqrt(np.arange(1, SS + 1, dtype=np.float64))).astype(
        np.float32).reshape(1, SS)
    in_mapsB = []
    for c in range(NCORES):
        b, qr = c // 4, c % 4
        bf = np.concatenate([bound[4 * b + i] for i in range(4)], axis=1)
        kf = np.concatenate([kn[4 * b + i] for i in range(4)], axis=1)
        rows = np.concatenate(
            [np.arange(j * 256 + qr * 64, j * 256 + qr * 64 + 64)
             for j in range(4)])
        in_mapsB.append({"IDENT": _ident_np(),
                         "bound": np.ascontiguousarray(bf[rows]),
                         "kn": np.ascontiguousarray(kf[rows]),
                         "rsq": rsq})
    resB = run_bass_kernel_spmd(LB.nc, in_mapsB, core_ids=list(range(NCORES)))
    retr = [np.asarray(r["retr"]) for r in resB.results]

    LC = get_prog("l2attnC", build_l2attnC)
    woT = np.ascontiguousarray(inputs["L2_attn_Wo"].T[perm, :])
    gbq = gbcols(inputs["L2_attn_ln_g"][perm], inputs["L2_attn_ln_b"][perm])
    commonC = {"IDENT": _ident_np(), "wo": woT,
               "bo": inputs["L2_attn_bo"], "gb": gbq}
    in_mapsC = []
    for c in range(NCORES):
        b, sc = c // 4, c % 4
        rfull = np.empty((1024, SS), np.float32)
        for qr in range(4):
            rb = retr[4 * b + qr]
            for j in range(4):
                rfull[j * 256 + qr * 64:j * 256 + qr * 64 + 64] = \
                    rb[j * 64:(j + 1) * 64]
        in_mapsC.append(dict(commonC, x_in=xs[c],
                             retr=np.ascontiguousarray(
                                 rfull[:, sc * TC:(sc + 1) * TC])))
    resC = run_bass_kernel_spmd(LC.nc, in_mapsC, core_ids=list(range(NCORES)))
    return [np.asarray(r["x_out"]) for r in resC.results]


def _rope_perm(nheads):
    perm = np.zeros(nheads * 64, dtype=np.int64)
    for h in range(nheads):
        for d in range(64):
            new = (h // 2) * 128 + (h % 2) * 64 + (d % 2) * 32 + d // 2
            perm[new] = h * 64 + d
    return perm


def _halo_x(xs, c, hlo):
    b, sc = c // 4, c % 4
    left = (np.zeros((DD, hlo), np.float32) if sc == 0
            else xs[c - 1][:, TC - hlo:])
    right = (np.zeros((DD, hlo), np.float32) if sc == 3
             else xs[c + 1][:, :hlo])
    return np.ascontiguousarray(np.concatenate([left, xs[c], right], axis=1))


def run_l0attn(xs, inputs):
    L = get_prog("l0attn", build_l0attn)
    HLO = 128
    Q0 = [0, 1, 2, 3, 8, 9, 10, 11]
    Q1 = [4, 5, 6, 7, 12, 13, 14, 15]
    pq = np.zeros(1024, dtype=np.int64)
    wo_perm = np.zeros(1024, dtype=np.int64)
    for j in range(8):
        for b, qh in ((0, Q0[j]), (64, Q1[j])):
            for d in range(64):
                new_col = j * 128 + b + (d % 2) * 32 + d // 2
                pq[new_col] = qh * 64 + d
                wo_perm[j * 128 + b + d] = qh * 64 + d
    pk = _rope_perm(4)
    wq = inputs["L0_attn_Wq"].reshape(1024, 1024)[:, pq]
    bq = inputs["L0_attn_bq"].reshape(1024)[pq]
    wk = inputs["L0_attn_Wk"].reshape(1024, 256)[:, pk]
    bk = inputs["L0_attn_bk"].reshape(256)[pk]
    wv = inputs["L0_attn_Wv"].reshape(1024, 256)
    bv = inputs["L0_attn_bv"].reshape(256)
    wo = inputs["L0_attn_Wo"].reshape(1024, 1024)[wo_perm, :]
    SCALE = np.float64(64.0) ** -0.25
    oneh = np.zeros((8, 8 * 128), np.float32)
    for e in range(8):
        oneh[e, e * 128:(e + 1) * 128] = 1.0
    P = np.zeros((128, 128), np.float32)
    for m in range(128):
        blk, local = m // 64, m % 64
        P[blk * 64 + (local + 32) % 64, m] = 1.0
    common = {
        "IDENT": _ident_np(), "PERM": P, "ONEHOT8": oneh,
        "wq": np.ascontiguousarray(wq), "bq": np.ascontiguousarray(bq),
        "wk": np.ascontiguousarray(wk), "bk": np.ascontiguousarray(bk),
        "wv": np.ascontiguousarray(wv), "bv": np.ascontiguousarray(bv),
        "wo": np.ascontiguousarray(wo), "bo": inputs["L0_attn_bo"],
        "sinks": np.ascontiguousarray(inputs["L0_attn_sinks"].reshape(1, 16)),
        "nsc": gbcols(inputs["L0_attn_attn_norm_scale"],
                      inputs["L0_attn_moe_norm_scale"]),
        "rw": np.ascontiguousarray(inputs["L0_attn_router_W"]),
        "rb": np.ascontiguousarray(inputs["L0_attn_router_b"].reshape(1, 8)),
        "gup": np.ascontiguousarray(
            inputs["L0_attn_gate_up_proj"].reshape(8 * 1024, 2048)),
        "gupb": inputs["L0_attn_gate_up_bias"],
        "dwn": np.ascontiguousarray(
            inputs["L0_attn_down_proj"].reshape(8 * 1024, 1024)),
        "dwnb": inputs["L0_attn_down_bias"],
    }
    in_maps = []
    half = 32
    inv_freq = 1.0 / (150000.0 ** (np.arange(half, dtype=np.float64) / half))
    for c in range(NCORES):
        b, sc = c // 4, c % 4
        s0 = sc * TC
        pos = np.arange(s0 - HLO, s0 + TC + HLO, dtype=np.float64)
        posc = np.clip(pos, 0, SS - 1)
        ang = posc[None, :] * inv_freq[:, None]
        cosv = (np.cos(ang) * SCALE).astype(np.float32)
        sinv = (np.sin(ang) * SCALE).astype(np.float32)
        ctab = np.empty((128, TC + 2 * HLO), np.float32)
        stab = np.empty((128, TC + 2 * HLO), np.float32)
        for blk in range(2):
            ctab[blk * 64:blk * 64 + 32] = cosv
            ctab[blk * 64 + 32:blk * 64 + 64] = cosv
            stab[blk * 64:blk * 64 + 32] = -sinv
            stab[blk * 64 + 32:blk * 64 + 64] = sinv
        tloc = np.arange(TC)
        tabs = s0 + tloc
        mask = np.zeros((TC, 384), np.float32)
        for tt in range(16):
            rows = slice(tt * 128, (tt + 1) * 128)
            kabs = s0 + tt * 128 - HLO + np.arange(384)
            qa = tabs[rows][:, None]
            mask[rows] = ((np.abs(qa - kabs[None, :]) <= 128) &
                          (kabs[None, :] >= 0) & (kabs[None, :] < SS))
        in_maps.append(dict(common, x_in=_halo_x(xs, c, HLO),
                            ctab=ctab, stab=stab,
                            mask=np.ascontiguousarray(mask)))
    res = run_bass_kernel_spmd(L.nc, in_maps, core_ids=list(range(NCORES)))
    return [np.asarray(r["x_out"]) for r in res.results]


def run_l3attn(xs, inputs):
    LA = get_prog("l3attnA", build_l3attnA)
    G4 = np.zeros((128, 4), np.float32)
    for p in range(128):
        G4[p, p // 32] = 1.0
    gam = inputs["L3_attn_gn_gamma"]
    bet = inputs["L3_attn_gn_beta"]
    in_mapsA = []
    for c in range(NCORES):
        b, cb = c // 4, c % 4
        xf = np.concatenate([xs[4 * b + i] for i in range(4)], axis=1)
        ch0 = cb * 256
        g2 = np.stack([gam[ch0:ch0 + 128], gam[ch0 + 128:ch0 + 256],
                       bet[ch0:ch0 + 128], bet[ch0 + 128:ch0 + 256]], axis=1)
        in_mapsA.append({"IDENT": _ident_np(), "G4": G4,
                         "G4T": np.ascontiguousarray(G4.T),
                         "x_cb": np.ascontiguousarray(xf[ch0:ch0 + 256]),
                         "gb": np.ascontiguousarray(g2, dtype=np.float32)})
    resA = run_bass_kernel_spmd(LA.nc, in_mapsA, core_ids=list(range(NCORES)))
    hn = [np.asarray(r["hn"]) for r in resA.results]

    import ml_dtypes
    hnf = np.empty((1024, 2 * SS), dtype=ml_dtypes.bfloat16)
    for c in range(NCORES):
        b, cb = c // 4, c % 4
        hnf[cb * 256:(cb + 1) * 256, b * SS:(b + 1) * SS] = hn[c]
    LB = get_prog("l3attnB", build_l3attnB)
    wqkv = inputs["L3_attn_W_qkv"]
    bqkv = inputs["L3_attn_b_qkv"]
    in_mapsB = []
    for c in range(NCORES):
        h0c = 2 * c * 64
        in_mapsB.append({
            "IDENT": _ident_np(),
            "hn": hnf,
            "wq": np.ascontiguousarray(wqkv[:, h0c:h0c + 128]),
            "wk": np.ascontiguousarray(wqkv[:, 1024 + h0c:1024 + h0c + 128]),
            "wv": np.ascontiguousarray(wqkv[:, 2048 + h0c:2048 + h0c + 128]),
            "bqkv": np.ascontiguousarray(np.stack(
                [bqkv[h0c:h0c + 128], bqkv[1024 + h0c:1024 + h0c + 128],
                 bqkv[2048 + h0c:2048 + h0c + 128]])),
        })
    resB = run_bass_kernel_spmd(LB.nc, in_mapsB, core_ids=list(range(NCORES)))
    attp = [np.asarray(r["att"]) for r in resB.results]
    attf = np.concatenate(attp, axis=0)

    LC = get_prog("l3attnC", build_l3attnC)
    commonC = {"IDENT": _ident_np(),
               "wo": np.ascontiguousarray(inputs["L3_attn_W_o"]),
               "bo": inputs["L3_attn_b_o"]}
    in_mapsC = []
    for c in range(NCORES):
        b, sc = c // 4, c % 4
        cols = slice(b * SS + sc * TC, b * SS + sc * TC + TC)
        in_mapsC.append(dict(commonC, x_in=xs[c],
                             att=np.ascontiguousarray(attf[:, cols])))
    resC = run_bass_kernel_spmd(LC.nc, in_mapsC, core_ids=list(range(NCORES)))
    return [np.asarray(r["x_out"]) for r in resC.results]


def run_l1attn(xs, inputs):
    LA = get_prog("l1attnA", build_l1attnA)
    HLO = 512
    cw = np.zeros((128, 96), np.float32)
    for off, key in ((0, "conv_q_w"), (32, "conv_k_w"), (64, "conv_v_w")):
        w = inputs["L1_attn_" + key]
        cw[:, off:off + 32] = w.reshape(8, 128, 4).transpose(1, 0, 2).reshape(
            128, 32)
    commonA = {
        "IDENT": _ident_np(), "cw": cw,
        "wq": np.ascontiguousarray(inputs["L1_attn_Wq"]),
        "wk": np.ascontiguousarray(inputs["L1_attn_Wk"]),
        "wv": np.ascontiguousarray(inputs["L1_attn_Wv"]),
        "wb": np.ascontiguousarray(inputs["L1_attn_Wb"]),
        "flw": np.ascontiguousarray(inputs["L1_attn_floor_W"]),
        "flb": np.ascontiguousarray(inputs["L1_attn_floor_b"].reshape(1, 8)),
        "w1": np.ascontiguousarray(inputs["L1_attn_fus_W1"]),
        "b1": inputs["L1_attn_fus_b1"],
        "w2": np.ascontiguousarray(inputs["L1_attn_fus_W2"]),
        "b2": np.ascontiguousarray(inputs["L1_attn_fus_b2"].reshape(1, 32)),
    }
    in_mapsA = []
    for c in range(NCORES):
        b, sc = c // 4, c % 4
        left = (np.zeros((DD, HLO), np.float32) if sc == 0
                else xs[c - 1][:, TC - HLO:])
        in_mapsA.append(dict(commonA, x_in=np.ascontiguousarray(
            np.concatenate([left, xs[c]], axis=1))))
    resA = run_bass_kernel_spmd(LA.nc, in_mapsA, core_ids=list(range(NCORES)))
    qf = np.concatenate([np.asarray(r["q_o"]) for r in resA.results], axis=1)
    kf = np.concatenate([np.asarray(r["k_o"]) for r in resA.results], axis=1)
    vf = np.concatenate([np.asarray(r["v_o"]) for r in resA.results], axis=1)
    bet = np.concatenate([np.asarray(r["beta_o"]) for r in resA.results], axis=1)
    flr = np.concatenate([np.asarray(r["flr_o"]) for r in resA.results], axis=1)
    fus = np.concatenate([np.asarray(r["fus_o"]) for r in resA.results], axis=1)

    LB = get_prog("l1attnB", build_l1attnB)
    MP = np.zeros((64, 64), np.float32)
    MPL = np.zeros((64, 64), np.float32)
    for blk in range(2):
        for m in range(32):
            for n in range(32):
                if n > m:
                    MP[blk * 32 + m, blk * 32 + n] = -1.0
                if m > n:
                    MPL[blk * 32 + m, blk * 32 + n] = -1.0
    MLOC = np.zeros((32, 32), np.float32)
    for m in range(32):
        MLOC[m, m:] = 1.0
    EYE = np.eye(64, dtype=np.float32)
    in_mapsB = []
    for c in range(NCORES):
        h = c
        rows = slice(h * 128, (h + 1) * 128)
        in_mapsB.append({
            "IDENT": _ident_np(), "MPAIR": MP, "MPAIRL": MPL, "MLOC": MLOC,
            "EYE64": EYE,
            "q_i": np.ascontiguousarray(qf[rows]),
            "k_i": np.ascontiguousarray(kf[rows]),
            "v_i": np.ascontiguousarray(vf[rows]),
            "b_i": np.ascontiguousarray(bet[h:h + 1]),
            "f0": np.ascontiguousarray(fus[4 * h:4 * h + 1]),
            "f1": np.ascontiguousarray(fus[4 * h + 1:4 * h + 2]),
            "f2": np.ascontiguousarray(fus[4 * h + 2:4 * h + 3]),
            "f3": np.ascontiguousarray(fus[4 * h + 3:4 * h + 4]),
            "fl": np.ascontiguousarray(flr[h:h + 1]),
            "fw": np.ascontiguousarray(np.concatenate(
                [inputs["L1_attn_fir_short_w"][h],
                 inputs["L1_attn_fir_long_w"][h]], axis=1)),
            "nw": np.ascontiguousarray(np.stack(
                [inputs["L1_attn_norm_pf_w"],
                 inputs["L1_attn_norm_o_w"]], axis=1)),
        })
    resB = run_bass_kernel_spmd(LB.nc, in_mapsB, core_ids=list(range(NCORES)))
    of = np.concatenate([np.asarray(r["o_o"]) for r in resB.results], axis=0)

    LC = get_prog("l1attnC", build_l1attnC)
    commonC = {"IDENT": _ident_np(),
               "wo": np.ascontiguousarray(inputs["L1_attn_Wo"])}
    in_mapsC = []
    for c in range(NCORES):
        b, sc = c // 4, c % 4
        cols = slice(b * SS + sc * TC, b * SS + sc * TC + TC)
        in_mapsC.append(dict(commonC, x_in=xs[c],
                             o_i=np.ascontiguousarray(of[:, cols])))
    resC = run_bass_kernel_spmd(LC.nc, in_mapsC, core_ids=list(range(NCORES)))
    return [np.asarray(r["x_out"]) for r in resC.results]


def kernel(**inputs):
    xs = shard_x(np.ascontiguousarray(inputs["L0_attn_hidden_states"],
                                      dtype=np.float32))
    xs = run_l0ffn(xs, inputs)
    xs = run_l0attn(xs, inputs)
    xs = run_l0ffn(xs, inputs)
    xs = run_l1ffn(xs, inputs)
    xs = run_l1attn(xs, inputs)
    xs = run_l1ffn(xs, inputs)
    xs = run_l2ffn(xs, inputs)
    xs = run_l2attn(xs, inputs)
    xs = run_l2ffn(xs, inputs)
    xs = run_l3ffn(xs, inputs)
    xs = run_l3attn(xs, inputs)
    xs = run_l3ffn(xs, inputs)
    return unshard_x(xs)
